# version 04
# baseline (speedup 1.0000x reference)
import numpy as np
import ml_dtypes
from contextlib import ExitStack

import concourse.bass as bass
import concourse.mybir as mybir
from concourse.bass_utils import run_bass_kernel_spmd

F32 = mybir.dt.float32
BF16 = mybir.dt.bfloat16
I32 = mybir.dt.int32
AF = mybir.ActivationFunctionType
ALU = mybir.AluOpType
AX = mybir.AxisListType

D = 4096
S = 4096
TPC = 1024
NEG = -1.0e30
EPS = 1e-5
ASCALE = 1.0 / np.sqrt(128.0)

ENGINES = ("sync", "scalar", "vector", "tensor", "gpsimd")



class H:
    __slots__ = ("eng", "kind", "val")

    def __init__(self, eng, kind, val):
        self.eng, self.kind, self.val = eng, kind, val


class Plan:

    def __init__(self, nc):
        self.nc = nc
        self.q = {e: [] for e in ENGINES}
        self.cnt = {}
        for e in ENGINES:
            self.cnt[(e, "c")] = 0
        self.settle_fn = None

    def op(self, eng, fn, deps=(), kind="c"):
        amt = 16 if (isinstance(kind, tuple) and kind[0] == "d") else 1
        key = (eng, kind)
        self.cnt[key] = self.cnt.get(key, 0) + amt
        h = H(eng, kind, self.cnt[key])
        self.q[eng].append((fn, tuple(d for d in deps if d is not None), kind))
        return h

    def fence_all(self):
        return [H(e, k, v) for (e, k), v in self.cnt.items() if v > 0]

    def barrier(self, eng, deps):
        self.q[eng].append((lambda e: None,
                            tuple(d for d in deps if d is not None), "c"))

    def raw(self, eng, fn, deps=()):
        self.q[eng].append((lambda e, f=fn: (f(e), None)[1],
                            tuple(d for d in deps if d is not None), "c"))

    def dma(self, eng, out, in_, deps=(), slot=None):
        if slot is None:
            to = out.tensor if hasattr(out, "tensor") else out
            if type(to).__name__ == "DRamTensorHandle":
                ti = in_.tensor if hasattr(in_, "tensor") else in_
                slot = "st_" + getattr(ti, "name", "anon")
            else:
                slot = getattr(to, "name", "anon")
        return self.op(eng, lambda e, o=out, i=in_: e.dma_start(out=o, in_=i),
                       deps, kind=("d", slot))

    def emit(self, ctx, block):
        sems = {}
        for (e, k), v in self.cnt.items():
            if v > 0:
                nm = f"sem_{e}_{k if isinstance(k, str) else 'd_' + k[1]}"
                sems[(e, k)] = ctx.enter_context(self.nc.semaphore(nm))

        def mk(e):
            def section(eng):
                waited = {}
                for fn, deps, kind in self.q[e]:
                    for dd in deps:
                        key = (dd.eng, dd.kind)
                        if waited.get(key, 0) >= dd.val:
                            continue
                        eng.wait_ge(sems[key], dd.val)
                        waited[key] = dd.val
                    inst = fn(eng)
                    if inst is not None:
                        if e == "tensor" and self.settle_fn is not None:
                            inst = self.settle_fn(eng)
                        inst.then_inc(
                            sems[(e, kind)],
                            16 if (isinstance(kind, tuple) and kind[0] == "d")
                            else 1)
            return section

        for e in ENGINES:
            if self.q[e]:
                getattr(block, e)(mk(e))
        return sems



def make_masks():
    i = np.arange(128)[:, None]
    j = np.arange(128)[None, :]
    mask_diag = np.where(j <= i, 0.0, NEG).astype(np.float32)
    mask_win = np.where(j > i, 0.0, NEG).astype(np.float32)
    return mask_diag, mask_win


def make_rope_tables(pos, scale):
    half = 64
    inv_freq = 10000.0 ** (-(np.arange(half, dtype=np.float64) / half))
    ang = pos[:, None].astype(np.float64) * inv_freq[None, :]
    cos = np.cos(ang).T
    sin = np.sin(ang).T
    cosf = np.concatenate([cos, cos], 0) * scale
    sinf = np.concatenate([sin, -sin], 0) * scale
    return cosf.astype(np.float32), sinf.astype(np.float32)



def alloc_stats(nc, ctx, tag):
    names = ["ssq", "ssum", "mean", "tmp", "vpe", "s0", "r0", "t2", "s1",
             "rstd", "nmr"]
    return {n: ctx.enter_context(nc.sbuf_tensor(f"{tag}_{n}", [128, 1], F32))
            for n in names}


def emit_ln(P, bufs, x_dram, xnt_sb, ident_sb, ident_h=None, extra_dep=()):
    ntiles = TPC // 128
    xs = bufs["xs"]
    st = bufs["stats"]
    xn_tm = bufs["xn_tm"]
    pt_ps = bufs["pt_ps"]
    ident = ident_sb
    last_xs = [None, None]
    last_tm = [None, None]
    last_pt = [None, None]
    prev_stats = None
    done = []
    for t in range(ntiles):
        bi = t % 2
        dep = list(extra_dep) + [last_xs[bi]]
        hld = P.dma("sync", xs[bi][:], x_dram[t * 128:(t + 1) * 128, :], dep)
        hsq = P.op("scalar", lambda e, b=bi: e.activation(
            xn_tm[b][:], xs[b][:], AF.Square, accum_out=st["ssq"][:]),
            [hld, prev_stats, last_tm[bi]])
        hsum = P.op("vector", lambda e, b=bi: e.reduce_sum(
            st["ssum"][:], xs[b][:], axis=AX.X), [hld, prev_stats])
        hmean = P.op("vector", lambda e: e.tensor_scalar(
            st["mean"][:], st["ssum"][:], 1.0 / D, None, ALU.mult), [hsum])
        htmp = P.op("vector", lambda e: e.tensor_scalar(
            st["tmp"][:], st["mean"][:], st["mean"][:], EPS,
            ALU.mult, ALU.subtract), [hmean])
        hvpe = P.op("vector", lambda e: e.tensor_scalar(
            st["vpe"][:], st["ssq"][:], 1.0 / D, st["tmp"][:],
            ALU.mult, ALU.subtract), [htmp, hsq])
        hs0 = P.op("scalar", lambda e: e.activation(
            st["s0"][:], st["vpe"][:], AF.Sqrt), [hvpe])
        hr0 = P.op("vector", lambda e: e.reciprocal(st["r0"][:], st["s0"][:]),
                   [hs0])
        ht2 = P.op("vector", lambda e: e.tensor_scalar(
            st["t2"][:], st["vpe"][:], st["r0"][:], None, ALU.mult),
            [hr0])
        hs1 = P.op("vector", lambda e: e.tensor_scalar(
            st["s1"][:], st["s0"][:], st["t2"][:], 0.5, ALU.add, ALU.mult),
            [ht2])
        hrstd = P.op("vector", lambda e: e.reciprocal(
            st["rstd"][:], st["s1"][:]), [hs1])
        hnmr = P.op("vector", lambda e: e.tensor_scalar(
            st["nmr"][:], st["mean"][:], st["rstd"][:], -1.0,
            ALU.mult, ALU.mult), [hrstd])
        hxn = P.op("scalar", lambda e, b=bi: e.activation(
            xn_tm[b][:], xs[b][:], AF.Identity, bias=st["nmr"][:],
            scale=st["rstd"][:]), [hnmr, last_tm[bi]])
        prev_stats = hxn
        last_xs[bi] = hxn
        for g in range(4):
            pi = (t * 4 + g) % 2

            def trg(e, b=bi, g=g, pi=pi):
                r = None
                for c in range(8):
                    kc = g * 8 + c
                    r = e.transpose(pt_ps[pi][:, c * 128:(c + 1) * 128],
                                    xn_tm[b][:, kc * 128:(kc + 1) * 128],
                                    ident[:])
                return r
            htr = P.op("tensor", trg, [hxn, last_pt[pi], ident_h])

            def evg(e, t=t, g=g, pi=pi):
                r = None
                for c in range(8):
                    kc = g * 8 + c
                    dst = xnt_sb[:, kc * TPC + t * 128: kc * TPC + (t + 1) * 128]
                    src = pt_ps[pi][:, c * 128:(c + 1) * 128]
                    r = e.tensor_copy(dst, src) if hasattr(e, "tensor_copy") \
                        else e.copy(dst, src)
                return r
            hev = P.op("vector" if g % 2 == 0 else "scalar", evg, [htr])
            last_pt[pi] = hev
            last_tm[bi] = hev
            done.append(hev)
    return done


def emit_proj_fm(P, bufs, wt_dram, out_dram, xnt_done, xnt_sb, n_out,
                 bias_sb=None, bias_h=None, scale=1.0, rope=None,
                 rope_dep=None, tag=""):
    nc = P.nc
    w_sb = bufs["w_fm"]
    ps = bufs["ps_fm"]
    ob = bufs["o_fm"]
    ra = bufs.get("rope_a")
    rb = bufs.get("rope_b")
    if "rope_war" not in bufs:
        bufs["rope_war"] = [None]
    last_w = bufs.setdefault("war_w_fm", [None, None])
    last_ps = bufs.setdefault("war_ps_fm", [None, None])
    last_ob = bufs.setdefault("war_o_fm", [None, None])
    single_w = len(w_sb) > 1 and (w_sb[0] is w_sb[1])
    nchunk = n_out // 128
    out_handles = []
    for oc in range(nchunk):
        wb = 0 if single_w else oc % 2
        hw = P.dma("gpsimd", w_sb[wb][:].rearrange("p (k c) -> p k c", k=32),
                   wt_dram.rearrange("(k p) o -> p k o", p=128)
                   [:, :, oc * 128:(oc + 1) * 128],
                   [last_w[wb]])
        tile_done = []
        for tt in range(TPC // 512):
            pi = (oc * (TPC // 512) + tt) % 2

            def mmg(e, wb=wb, tt=tt, pi=pi):
                r = None
                for kc in range(32):
                    r = e.matmul(ps[pi][:],
                                 w_sb[wb][:, kc * 128:(kc + 1) * 128],
                                 xnt_sb[:, kc * TPC + tt * 512:
                                        kc * TPC + (tt + 1) * 512],
                                 start=(kc == 0), stop=(kc == 31))
                return r
            hmm = P.op("tensor", mmg,
                       fences_list(xnt_done) + [hw, last_ps[pi]])
            if rope is None:
                bias_ap = (bias_sb[:, oc:oc + 1]
                           if bias_sb is not None else 0.0)
                deps = [hmm, last_ob[wb]]
                if bias_h is not None:
                    deps.append(bias_h)

                def ev(e, wb=wb, tt=tt, pi=pi, bias_ap=bias_ap):
                    return e.activation(
                        ob[wb][:, tt * 512:(tt + 1) * 512], ps[pi][:],
                        AF.Identity, bias=bias_ap, scale=scale)
                hev = P.op("scalar", ev, deps)
                last_ps[pi] = hev
                tile_done.append(hev)
            else:
                cos_sb, sin_sb = rope
                c_ap = cos_sb[:, tt * 512:(tt + 1) * 512]
                sL = sin_sb[0:64, tt * 512:(tt + 1) * 512]
                sH = sin_sb[64:128, tt * 512:(tt + 1) * 512]
                rw = bufs.get("rope_war")

                def r1(e, pi=pi, c_ap=c_ap):
                    return e.tensor_tensor(ra[:], ps[pi][:], c_ap, ALU.mult)
                h1 = P.op("vector", r1, [hmm, rw[0]] + list(rope_dep or []))

                def r2(e, pi=pi, sH=sH):
                    return e.tensor_tensor(rb[0:64, :], ps[pi][64:128, :], sH,
                                           ALU.mult)
                h2 = P.op("vector", r2, [hmm, rw[0]] + list(rope_dep or []))

                def r3(e, pi=pi, sL=sL):
                    return e.tensor_tensor(rb[64:128, :], ps[pi][0:64, :], sL,
                                           ALU.mult)
                h3 = P.op("vector", r3, [hmm, rw[0]] + list(rope_dep or []))

                def r4(e, wb=wb, tt=tt):
                    return e.tensor_tensor(ob[wb][:, tt * 512:(tt + 1) * 512],
                                           ra[:], rb[:], ALU.add)
                h4 = P.op("vector", r4, [h1, h2, h3, last_ob[wb]])
                rw[0] = h4
                last_ps[pi] = h4
                tile_done.append(h4)
        hst = P.dma("sync", out_dram[oc * 128:(oc + 1) * 128, :], ob[wb][:],
                    tile_done)
        last_ob[wb] = hst
        last_w[wb] = hmm
        out_handles.append(hst)
    return out_handles


def fences_list(handles):
    best = {}
    for h in handles:
        if h is None:
            continue
        k = (h.eng, h.kind)
        if k not in best or best[k].val < h.val:
            best[k] = h
    return list(best.values())


def emit_proj_tm(P, bufs, w_dram, out_dram, xnt_done, xnt_sb, n_out,
                 bias_row_sb=None, bias_h=None, ones_sb=None, ones_h=None,
                 tag=""):
    w_sb = bufs["w_tm"]
    single_w = w_sb[0] is w_sb[1] if len(w_sb) > 1 else True
    ps = bufs["ps_tm"]
    vb = bufs["v_tm"]
    last_w = bufs.setdefault("war_w_tm", [None, None])
    last_ps = bufs.setdefault("war_ps_tm", [None, None])
    last_vb = bufs.setdefault("war_v_tm", [None, None])
    out_handles = []
    nblk = n_out // 512
    for obk in range(nblk):
        wb = 0 if single_w else obk % 2
        hw = P.dma("gpsimd", w_sb[wb][:].rearrange("p (k c) -> p k c", k=32),
                   w_dram.rearrange("(k p) o -> p k o", p=128)
                   [:, :, obk * 512:(obk + 1) * 512],
                   [last_w[wb]])
        for tt in range(TPC // 128):
            pi = (obk * (TPC // 128) + tt) % 2

            def mmg(e, wb=wb, tt=tt, pi=pi, obk=obk):
                r = None
                if bias_row_sb is not None:
                    r = e.matmul(ps[pi][:], ones_sb[:],
                                 bias_row_sb[:, obk * 512:(obk + 1) * 512],
                                 start=True, stop=False)
                for kc in range(32):
                    r = e.matmul(ps[pi][:],
                                 xnt_sb[:, kc * TPC + tt * 128:
                                        kc * TPC + (tt + 1) * 128],
                                 w_sb[wb][:, kc * 512:(kc + 1) * 512],
                                 start=(kc == 0 and bias_row_sb is None),
                                 stop=(kc == 31))
                return r
            deps = fences_list(xnt_done) + [hw, last_ps[pi]]
            if bias_h is not None:
                deps += [bias_h, ones_h]
            hmm = P.op("tensor", mmg, deps)
            bj = tt % 2

            def ev(e, pi=pi, bj=bj):
                return e.tensor_copy(vb[bj][:], ps[pi][:]) \
                    if hasattr(e, "tensor_copy") else e.copy(vb[bj][:], ps[pi][:])
            hev = P.op("vector" if tt % 2 == 0 else "scalar", ev,
                       [hmm, last_vb[bj]])
            last_ps[pi] = hev
            hst = P.dma("sync",
                        out_dram[tt * 128:(tt + 1) * 128,
                                 obk * 512:(obk + 1) * 512],
                        vb[bj][:], [hev])
            last_vb[bj] = hst
            out_handles.append(hst)
        last_w[wb] = hmm
    return out_handles





def alloc_attn_bufs(nc, ctx, tag=""):
    sb = lambda n, s, d: ctx.enter_context(nc.sbuf_tensor(tag + n, s, d))
    psum = lambda n, s, d: ctx.enter_context(nc.psum_tensor(tag + n, s, d))
    names = ["negm", "negm2", "dtmp", "alpha", "zz", "zp", "rz", "negmsc"]
    fl = {n: sb("fl_" + n, [128, 1], F32) for n in names}
    return {
        "fl": fl,
        "oacc": sb("oacc", [128, 128], F32),
        "p_sb": [sb("p0", [128, 2048], BF16), sb("p1", [128, 2048], BF16)],
        "pt_sb": [sb("pt0", [128, 1024], BF16), sb("pt1", [128, 1024], BF16)],
        "ohead": [sb("oh0", [128, 32 * 128], BF16),
                  sb("oh1", [128, 32 * 128], BF16)],
        "s_ps": psum("s_ps", [128, 2048], F32),
        "pt_ps": [psum("ptp0", [128, 1024], BF16),
                  psum("ptp1", [128, 1024], BF16)],
        "o_ps": psum("o_ps", [128, 512], F32),
        "war": {},
    }


def _s_matmuls(e, ab, q_sb, k_sb, j, kstart, p0, chunks):
    r = None
    for (c0, cw) in chunks:
        r = e.matmul(ab["s_ps"][:, c0:c0 + cw],
                     q_sb[:, j * 128:(j + 1) * 128],
                     k_sb[:, kstart + p0 + c0: kstart + p0 + c0 + cw],
                     start=True, stop=True)
    return r


def _tr_group(e, ab, pp, gp, g0, nch, ident_sb):
    r = None
    for c in range(nch):
        r = e.transpose(ab["pt_ps"][gp][:, c * 128:(c + 1) * 128],
                        ab["p_sb"][pp][:, (g0 + c) * 128:(g0 + c + 1) * 128],
                        ident_sb[:])
    return r


def _av_group(e, ab, v_sb, gp, g0, nch, kbase_c, first, lastg):
    r = None
    for c in range(nch):
        kc = kbase_c + g0 + c
        r = e.matmul(ab["o_ps"][:, 0:128],
                     ab["pt_sb"][gp][:, c * 128:(c + 1) * 128],
                     v_sb[:, kc * 128:(kc + 1) * 128],
                     start=(first and c == 0),
                     stop=(lastg and c == nch - 1))
    return r


def emit_attn_head(P, ab, h, n_jtiles, q_sb, k_sb, v_sb, kv_dep, q_dep,
                   o_dram_ap, mask_diag_sb, mask_win_sb, mask_h, ident_sb,
                   ident_h, windowed, sscale=ASCALE, mask_h2=None,
                   transpose_out=False, o_store_eng="sync", o_slot=None):
    fl = ab["fl"]
    oacc = ab["oacc"]
    war = ab["war"]
    hp = h % 2
    ohead = ab["ohead"][hp]
    qk_reads = []
    v_reads = []
    for j in range(n_jtiles):
        Q0 = j * 128
        kend = Q0 + 128
        kstart = max(0, kend - 1152) if windowed else 0
        kn = kend - kstart
        pbounds = list(range(0, kn, 2048)) + [kn]
        npass = len(pbounds) - 1
        hZ = None
        hO = None
        halpha = None
        for ip in range(npass):
            first = ip == 0
            lastp = ip == npass - 1
            p0, p1 = pbounds[ip], pbounds[ip + 1]
            w = p1 - p0
            chunks = []
            c0 = 0
            while c0 < w:
                cw = min(512, w - c0)
                chunks.append((c0, cw))
                c0 += cw
            hs = P.op("tensor",
                      (lambda e, j=j, ks=kstart, p0=p0, ch=tuple(chunks):
                       _s_matmuls(e, ab, q_sb, k_sb, j, ks, p0, ch)),
                      [war.get("s_ps_free"), q_dep, ident_h] + list(kv_dep))
            qk_reads.append(hs)
            deps_red = [hs]
            if lastp:
                hmd = P.op("vector", lambda e, w=w: e.tensor_tensor(
                    ab["s_ps"][:, w - 128:w], ab["s_ps"][:, w - 128:w],
                    mask_diag_sb[:], ALU.add), [hs, mask_h])
                deps_red.append(hmd)
            if first and windowed and kstart > 0:
                hmw = P.op("vector", lambda e: e.tensor_tensor(
                    ab["s_ps"][:, 0:128], ab["s_ps"][:, 0:128],
                    mask_win_sb[:], ALU.add), [hs, mask_h2 or mask_h])
                deps_red.append(hmw)
            hred = P.op("vector", lambda e, w=w: e.reduce_max(
                fl["negm2"][:], ab["s_ps"][:, 0:w], axis=AX.X, negate=True),
                deps_red + [war.get("negm2_free")])
            if first:
                halpha = None
                hngm = P.op("vector", lambda e: e.tensor_copy(
                    fl["negm"][:], fl["negm2"][:]),
                    [hred, war.get("negm_free")])
            else:
                hdm = P.op("vector", lambda e: e.tensor_tensor(
                    fl["dtmp"][:], fl["negm"][:], fl["negm2"][:], ALU.min),
                    [hred, war.get("dtmp_free")])
                hdf = P.op("vector", lambda e: e.tensor_tensor(
                    fl["negm2"][:], fl["dtmp"][:], fl["negm"][:],
                    ALU.subtract), [hdm])
                halpha = P.op("scalar", lambda e: e.activation(
                    fl["alpha"][:], fl["negm2"][:], AF.Exp, scale=sscale),
                    [hdf, war.get("alpha_free")])
                hngm = P.op("vector", lambda e: e.tensor_copy(
                    fl["negm"][:], fl["dtmp"][:]),
                    [hdf, war.get("negm_free")])
                war["dtmp_free"] = hngm
            war["negm2_free"] = halpha or hngm
            pp = war.setdefault("pp_ctr", [0])[0] % 2
            war["pp_ctr"][0] += 1
            hnsc = P.op("vector", lambda e: e.tensor_scalar(
                fl["negmsc"][:], fl["negm"][:], sscale, None, ALU.mult),
                [hngm, war.get("negmsc_free")])
            hexp = P.op("scalar", lambda e, pp=pp, w=w: e.activation(
                ab["p_sb"][pp][:, 0:w], ab["s_ps"][:, 0:w], AF.Exp,
                scale=sscale, bias=fl["negmsc"][:], accum_out=fl["zp"][:]),
                [hnsc, halpha, war.get(("p_free", pp)), war.get("zp_free")])
            war["negmsc_free"] = hexp
            war["s_ps_free"] = hexp
            war["negm_free"] = hexp
            if first:
                hZ = P.op("vector", lambda e: e.tensor_copy(
                    fl["zz"][:], fl["zp"][:]), [hexp, war.get("zz_free")])
            else:
                hZ = P.op("vector", lambda e: e.scalar_tensor_tensor(
                    fl["zz"][:], fl["zz"][:], fl["alpha"][:], fl["zp"][:],
                    ALU.mult, ALU.add), [hexp, hZ, halpha])
            war["zp_free"] = hZ
            nkc = w // 128
            kbase_c = (kstart + p0) // 128
            hav = None
            htr_last = None
            first_av = True
            for g0 in range(0, nkc, 8):
                nch = min(8, nkc - g0)
                lastg = g0 + 8 >= nkc
                gp = war.setdefault("gp_ctr", [0])[0] % 2
                war["gp_ctr"][0] += 1
                htr = P.op("tensor",
                           (lambda e, pp=pp, gp=gp, g0=g0, nch=nch:
                            _tr_group(e, ab, pp, gp, g0, nch, ident_sb)),
                           [hexp, ident_h, war.get(("ptps_free", gp))])
                htr_last = htr
                hev = P.op("vector" if (g0 // 8) % 2 == 0 else "scalar",
                           (lambda e, gp=gp, nch=nch:
                            (e.tensor_copy if hasattr(e, "tensor_copy")
                             else e.copy)(ab["pt_sb"][gp][:, 0:nch * 128],
                                          ab["pt_ps"][gp][:, 0:nch * 128])),
                           [htr, war.get(("ptsb_free", gp))])
                war[("ptps_free", gp)] = hev
                hav = P.op("tensor",
                           (lambda e, gp=gp, g0=g0, nch=nch, kb=kbase_c,
                            fa=first_av, lg=lastg:
                            _av_group(e, ab, v_sb, gp, g0, nch, kb, fa, lg)),
                           [hev, war.get("o_ps_free")] + list(kv_dep))
                war[("ptsb_free", gp)] = hav
                v_reads.append(hav)
                first_av = False
            war[("p_free", pp)] = htr_last
            if first:
                hO = P.op("vector", lambda e: e.tensor_copy(
                    oacc[:], ab["o_ps"][:, 0:128]),
                    [hav, war.get("oacc_free")])
            else:
                hO = P.op("vector", lambda e: e.scalar_tensor_tensor(
                    oacc[:], oacc[:], fl["alpha"][:], ab["o_ps"][:, 0:128],
                    ALU.mult, ALU.add), [hav, hO, halpha])
                war["alpha_free"] = hO
            war["o_ps_free"] = hO
        hrz = P.op("vector", lambda e: e.reciprocal(
            fl["rz"][:], fl["zz"][:]), [hZ, war.get("rz_free")])
        war["zz_free"] = hrz
        if not transpose_out:
            hfin = P.op("vector", lambda e, j=j: e.tensor_scalar(
                ohead[:, j * 128:(j + 1) * 128], oacc[:], fl["rz"][:], None,
                ALU.mult), [hrz, hO, war.get(("ohead_free", hp))])
        else:
            hfn0 = P.op("vector", lambda e: e.tensor_scalar(
                ab["o_tmp"][:], oacc[:], fl["rz"][:], None, ALU.mult),
                [hrz, hO, war.get("o_tmp_free")])
            hotr = P.op("tensor", lambda e: e.transpose(
                ab["otr_ps"][:], ab["o_tmp"][:], ident_sb[:]),
                [hfn0, ident_h, war.get("otr_free")])
            war["o_tmp_free"] = hotr
            hfin = P.op("scalar", lambda e, j=j: e.copy(
                ohead[:, j * 128:(j + 1) * 128], ab["otr_ps"][:]),
                [hotr, war.get(("ohead_free", hp))])
            war["otr_free"] = hfin
        war["rz_free"] = hfin
        war["oacc_free"] = hfin
        war.setdefault("fins", []).append(hfin)
    hdma = P.dma(o_store_eng, o_dram_ap,
                 ohead[:].rearrange("p (j c) -> p j c", j=n_jtiles),
                 war["fins"], slot=o_slot)
    war["fins"] = []
    war[("ohead_free", hp)] = hdma
    return qk_reads, v_reads, hdma



def build_launch1():
    nc = bass.Bass("TRN2", target_bir_lowering=False)
    x = nc.dram_tensor("x", [TPC, D], F32, kind="ExternalInput").ap()
    wqt = nc.dram_tensor("wqt", [D, D], F32, kind="ExternalInput").ap()
    wkt = nc.dram_tensor("wkt", [D, D], F32, kind="ExternalInput").ap()
    wvt = nc.dram_tensor("wvt", [D, D], F32, kind="ExternalInput").ap()
    bqt = nc.dram_tensor("bqt", [128, 32], F32, kind="ExternalInput").ap()
    bkt = nc.dram_tensor("bkt", [128, 32], F32, kind="ExternalInput").ap()
    bvr = nc.dram_tensor("bvr", [1, D], F32, kind="ExternalInput").ap()
    identity = nc.dram_tensor("identity", [128, 128], F32,
                              kind="ExternalInput").ap()
    qt = nc.dram_tensor("qt", [D, TPC], BF16, kind="ExternalOutput").ap()
    kt = nc.dram_tensor("kt", [D, TPC], BF16, kind="ExternalOutput").ap()
    v = nc.dram_tensor("v", [TPC, D], BF16, kind="ExternalOutput").ap()
    xnt = nc.dram_tensor("xnt", [D, TPC], BF16, kind="ExternalOutput").ap()

    with ExitStack() as ctx:
        sb = lambda n, s, d: ctx.enter_context(nc.sbuf_tensor(n, s, d))
        psum = lambda n, s, d: ctx.enter_context(nc.psum_tensor(n, s, d))
        bufs = {
            "xs": [sb("xs0", [128, D], F32), sb("xs1", [128, D], F32)],
            "stats": alloc_stats(nc, ctx, "st"),
            "xn_tm": [sb("xntm0", [128, D], BF16), sb("xntm1", [128, D], BF16)],
            "pt_ps": [psum("ptps0", [128, 1024], BF16),
                      psum("ptps1", [128, 1024], BF16)],
            "w_fm": [sb("wfm0", [128, 32 * 128], BF16),
                     sb("wfm1", [128, 32 * 128], BF16)],
            "ps_fm": [psum("psfm0", [128, 512], F32),
                      psum("psfm1", [128, 512], F32)],
            "o_fm": [sb("ofm0", [128, TPC], BF16), sb("ofm1", [128, TPC], BF16)],
            "w_tm": [sb("wtm0", [128, 32 * 512], BF16)] * 2,
            "ps_tm": [psum("pstm0", [128, 512], F32),
                      psum("pstm1", [128, 512], F32)],
            "v_tm": [sb("vtm0", [128, 512], BF16), sb("vtm1", [128, 512], BF16)],
        }
        xnt_sb = sb("xnt_sb", [128, 32 * TPC], BF16)
        ident_sb = sb("ident", [128, 128], BF16)
        bq_sb = sb("bq_sb", [128, 32], F32)
        bk_sb = sb("bk_sb", [128, 32], F32)
        bv_sb = sb("bv_sb", [1, D], F32)
        ones_sb = sb("ones_sb", [1, 128], F32)

        dscr = psum("dscr", [128, 1024], BF16)
        P = Plan(nc)
        P.settle_fn = lambda e: e.transpose(dscr[:, 0:128], ident_sb[:],
                                            ident_sb[:])
        hid = P.dma("gpsimd", ident_sb[:], identity)
        hbq = P.dma("sync", bq_sb[:], bqt)
        hbk = P.dma("sync", bk_sb[:], bkt)
        hbv = P.dma("sync", bv_sb[:], bvr)
        hones = P.op("vector", lambda e: e.memset(ones_sb[:], 1.0))

        ln_done = emit_ln(P, bufs, x, xnt_sb, ident_sb, ident_h=hid)
        lnf = fences_list(ln_done)
        hxnt = P.dma("sync", xnt.rearrange("(k p) t -> p k t", p=128),
                     xnt_sb[:].rearrange("p (k t) -> p k t", k=32), lnf)
        hq = emit_proj_fm(P, bufs, wqt, qt, lnf, xnt_sb, D,
                          bias_sb=bq_sb, bias_h=hbq, scale=1.0, tag="q")
        hk = emit_proj_fm(P, bufs, wkt, kt, lnf, xnt_sb, D,
                          bias_sb=bk_sb, bias_h=hbk, scale=1.0, tag="k")
        hv = emit_proj_tm(P, bufs, wvt, v, lnf, xnt_sb, D,
                          bias_row_sb=bv_sb, bias_h=hbv, ones_sb=ones_sb,
                          ones_h=hones, tag="v")
        P.barrier("sync", fences_list([hxnt] + hq + hk + hv))
        block = ctx.enter_context(nc.Block())
        P.emit(ctx, block)
    return nc



def _run(nc, in_maps):
    return run_bass_kernel_spmd(nc, in_maps, core_ids=list(range(8))).results


def kernel_unfused(L0_attn_x, L0_attn_Wq, L0_attn_bq, L0_attn_Wk, L0_attn_bk,
           L0_attn_Wv, L0_attn_bv, L0_attn_Wo, L0_attn_bo, L0_ffn_weight,
           L0_ffn_bias, L0_ffn_lora_A, L0_ffn_lora_B,
           L0_ffn_lora_start_indices, L1_attn_wq, L1_attn_wk, L1_attn_wv,
           L1_attn_wo, L1_ffn_A, L1_ffn_B, L1_ffn_Wr, L1_ffn_br):
    f32 = np.float32
    x = np.asarray(L0_attn_x, dtype=f32)
    Wq = np.asarray(L0_attn_Wq, dtype=f32); bq = np.asarray(L0_attn_bq, dtype=f32)
    Wk = np.asarray(L0_attn_Wk, dtype=f32); bk = np.asarray(L0_attn_bk, dtype=f32)
    Wv = np.asarray(L0_attn_Wv, dtype=f32); bv = np.asarray(L0_attn_bv, dtype=f32)
    Wo = np.asarray(L0_attn_Wo, dtype=f32); bo = np.asarray(L0_attn_bo, dtype=f32)
    Wf = np.asarray(L0_ffn_weight, dtype=f32)
    bfn = np.asarray(L0_ffn_bias, dtype=f32)
    lA = np.asarray(L0_ffn_lora_A, dtype=f32)
    lB = np.asarray(L0_ffn_lora_B, dtype=f32)
    lsi = np.asarray(L0_ffn_lora_start_indices).astype(np.int32).reshape(1, 4)
    wq1 = np.asarray(L1_attn_wq, dtype=f32)
    wk1 = np.asarray(L1_attn_wk, dtype=f32)
    wv1 = np.asarray(L1_attn_wv, dtype=f32)
    wo1 = np.asarray(L1_attn_wo, dtype=f32)
    A1 = np.asarray(L1_ffn_A, dtype=f32)
    B1 = np.asarray(L1_ffn_B, dtype=f32)
    Wr = np.asarray(L1_ffn_Wr, dtype=f32)
    br = np.asarray(L1_ffn_br, dtype=f32)

    mask_diag, mask_win = make_masks()
    ident = np.eye(128, dtype=f32)
    e8 = np.zeros((8, 128), dtype=f32)
    for h in range(8):
        e8[h, h * 16:(h + 1) * 16] = 1.0
    C = lambda a: np.ascontiguousarray(a)

    im = []
    for c in range(8):
        b, r = c // 4, c % 4
        im.append({"x": C(x[b, r * TPC:(r + 1) * TPC, :]),
                   "wqt": C(Wq.T), "wkt": C(Wk.T), "wvt": C(Wv.T),
                   "bqt": C(bq.reshape(32, 128).T),
                   "bkt": C(bk.reshape(32, 128).T),
                   "bvr": C(bv.reshape(1, -1)), "identity": ident})
    r1 = _run(build_launch1(), im)
    QT = [np.concatenate([r1[4 * b + r]["qt"] for r in range(4)], 1)
          for b in range(2)]
    KT = [np.concatenate([r1[4 * b + r]["kt"] for r in range(4)], 1)
          for b in range(2)]
    V = [np.concatenate([r1[4 * b + r]["v"] for r in range(4)], 0)
         for b in range(2)]

    im = []
    for c in range(8):
        b, r = c // 4, c % 4
        im.append({"q": C(QT[b][r * TPC:(r + 1) * TPC, :]),
                   "k": C(KT[b][r * TPC:(r + 1) * TPC, :]),
                   "v": C(V[b][:, r * TPC:(r + 1) * TPC]),
                   "mdg": mask_diag, "identity": ident})
    r2 = _run(build_launch2(), im)
    O = [np.concatenate([r2[4 * b + r]["o"] for r in range(4)], 1)
         for b in range(2)]

    im = []
    for c in range(8):
        b, r = c // 4, c % 4
        pos = np.arange(r * TPC, (r + 1) * TPC)
        rcos, rsin = make_rope_tables(pos, 1.0)
        im.append({
            "x": C(x[b, r * TPC:(r + 1) * TPC, :]),
            "xnt": r1[c]["xnt"],
            "ot": C(O[b][r * TPC:(r + 1) * TPC, :].T),
            "wot": C(Wo.T), "bor": C(bo.reshape(1, -1)),
            "wft": C(Wf.T), "bfr": C(bfn.reshape(1, -1)),
            "acat": C(lA.transpose(2, 0, 1).reshape(D, 48)),
            "bcat": C(lB.transpose(0, 2, 1).reshape(48, D)),
            "lsi": lsi, "bsel": np.array([[float(b)]], dtype=f32),
            "wq1": wq1, "wk1": wk1, "wv1": wv1,
            "rcos": rcos, "rsin": rsin, "identity": ident})
    r3 = _run(build_launch3(), im)
    Q1 = [np.concatenate([r3[4 * b + r]["q1t"] for r in range(4)], 1)
          for b in range(2)]
    K1 = [np.concatenate([r3[4 * b + r]["k1t"] for r in range(4)], 1)
          for b in range(2)]
    V1 = [np.concatenate([r3[4 * b + r]["v1"] for r in range(4)], 0)
          for b in range(2)]

    im = []
    for c in range(8):
        b, r = c // 4, c % 4
        im.append({"q": C(Q1[b][r * TPC:(r + 1) * TPC, :]),
                   "k": C(K1[b][r * 256:(r + 1) * 256, :]),
                   "v": C(V1[b][:, r * 256:(r + 1) * 256]),
                   "mdg": mask_diag, "mwn": mask_win, "identity": ident})
    r4 = _run(build_launch4(), im)
    O1 = [np.concatenate([r4[4 * b + r]["o"] for r in range(4)], 1)
          for b in range(2)]

    im = []
    for c in range(8):
        b, r = c // 4, c % 4
        im.append({
            "x1": r3[c]["x1"],
            "xn1t": r3[c]["xn1t"],
            "o1t": C(O1[b][r * TPC:(r + 1) * TPC, :].T),
            "wo1": wo1,
            "a1cat": C(A1.transpose(2, 0, 1).reshape(D, 128)),
            "b1cat": C(B1.transpose(0, 2, 1).reshape(128, D)),
            "wrt": C(Wr.T), "brr": C(br.reshape(1, 8)),
            "e8": e8, "identity": ident})
    r5 = _run(build_launch5(), im)

    out = np.empty((2, S, D), dtype=f32)
    for c in range(8):
        b, r = c // 4, c % 4
        out[b, r * TPC:(r + 1) * TPC, :] = r5[c]["out"]
    return out


def build_launch2():
    nc = bass.Bass("TRN2", target_bir_lowering=False)
    q = nc.dram_tensor("q", [8 * 128, S], BF16, kind="ExternalInput").ap()
    k = nc.dram_tensor("k", [8 * 128, S], BF16, kind="ExternalInput").ap()
    v = nc.dram_tensor("v", [S, 8 * 128], BF16, kind="ExternalInput").ap()
    mdg = nc.dram_tensor("mdg", [128, 128], F32, kind="ExternalInput").ap()
    identity = nc.dram_tensor("identity", [128, 128], F32,
                              kind="ExternalInput").ap()
    o = nc.dram_tensor("o", [S, 8 * 128], BF16, kind="ExternalOutput").ap()

    with ExitStack() as ctx:
        sb = lambda n, s, d: ctx.enter_context(nc.sbuf_tensor(n, s, d))
        ab = alloc_attn_bufs(nc, ctx)
        q_sb = [sb("q0", [128, S], BF16), sb("q1", [128, S], BF16)]
        k_sb = [sb("k0", [128, S], BF16), sb("k1", [128, S], BF16)]
        v_sb = [sb("v0", [128, 32 * 128], BF16), sb("v1", [128, 32 * 128], BF16)]
        md_sb = sb("md", [128, 128], F32)
        ident_sb = sb("ident", [128, 128], BF16)
        dscr = ctx.enter_context(nc.psum_tensor("dscr", [128, 1024], BF16))

        P = Plan(nc)
        P.settle_fn = lambda e: e.transpose(dscr[:, 0:128], ident_sb[:],
                                            ident_sb[:])
        hid = P.dma("gpsimd", ident_sb[:], identity)
        hmd = P.dma("gpsimd", md_sb[:], mdg)
        qk_free = [None, None]
        v_free = [None, None]
        odma = []
        for h in range(8):
            hb = h % 2
            hq = P.dma("sync", q_sb[hb][:], q[h * 128:(h + 1) * 128, :],
                       [qk_free[hb]])
            hk = P.dma("sync", k_sb[hb][:], k[h * 128:(h + 1) * 128, :],
                       [qk_free[hb]])
            hv = P.dma("sync", v_sb[hb][:].rearrange("p (kc c) -> p kc c", kc=32),
                       v.rearrange("(kc p) c -> p kc c", p=128)
                       [:, :, h * 128:(h + 1) * 128], [v_free[hb]])
            o_ap = o.rearrange("(j q) c -> q j c", q=128)[:, :, h * 128:(h + 1) * 128]
            qk_reads, v_reads, hdma = emit_attn_head(
                P, ab, h, 32, q_sb[hb], k_sb[hb], v_sb[hb],
                [hk, hv], hq, o_ap, md_sb, None, hmd,
                ident_sb, hid, windowed=False)
            qk_free[hb] = fences_list(qk_reads)[0] if qk_reads else None
            vf = fences_list(v_reads)
            v_free[hb] = vf[0] if vf else None
            odma.append(hdma)
        P.barrier("sync", fences_list(odma))
        block = ctx.enter_context(nc.Block())
        P.emit(ctx, block)
    return nc



def build_launch4():
    nc = bass.Bass("TRN2", target_bir_lowering=False)
    q = nc.dram_tensor("q", [8 * 128, S], BF16, kind="ExternalInput").ap()
    k = nc.dram_tensor("k", [2 * 128, S], BF16, kind="ExternalInput").ap()
    v = nc.dram_tensor("v", [S, 2 * 128], BF16, kind="ExternalInput").ap()
    mdg = nc.dram_tensor("mdg", [128, 128], F32, kind="ExternalInput").ap()
    mwn = nc.dram_tensor("mwn", [128, 128], F32, kind="ExternalInput").ap()
    identity = nc.dram_tensor("identity", [128, 128], F32,
                              kind="ExternalInput").ap()
    o = nc.dram_tensor("o", [S, 8 * 128], BF16, kind="ExternalOutput").ap()

    with ExitStack() as ctx:
        sb = lambda n, s, d: ctx.enter_context(nc.sbuf_tensor(n, s, d))
        ab = alloc_attn_bufs(nc, ctx)
        q_sb = [sb("q0", [128, S], BF16), sb("q1", [128, S], BF16)]
        k_sb = [sb("k0", [128, S], BF16), sb("k1", [128, S], BF16)]
        v_sb = [sb("v0", [128, 32 * 128], BF16), sb("v1", [128, 32 * 128], BF16)]
        md_sb = sb("md", [128, 128], F32)
        mw_sb = sb("mw", [128, 128], F32)
        ident_sb = sb("ident", [128, 128], BF16)
        dscr = ctx.enter_context(nc.psum_tensor("dscr", [128, 1024], BF16))

        P = Plan(nc)
        P.settle_fn = lambda e: e.transpose(dscr[:, 0:128], ident_sb[:],
                                            ident_sb[:])
        hid = P.dma("gpsimd", ident_sb[:], identity)
        hmd = P.dma("gpsimd", md_sb[:], mdg)
        hmw = P.dma("gpsimd", mw_sb[:], mwn)
        hks, hvs = [], []
        for kv in range(2):
            hks.append(P.dma("sync", k_sb[kv][:], k[kv * 128:(kv + 1) * 128, :]))
            hvs.append(P.dma(
                "sync", v_sb[kv][:].rearrange("p (kc c) -> p kc c", kc=32),
                v.rearrange("(kc p) c -> p kc c", p=128)
                [:, :, kv * 128:(kv + 1) * 128]))
        qbufs = [sb("qb0", [128, S], BF16), sb("qb1", [128, S], BF16)]
        q_free = [None, None]
        odma = []
        for h in range(8):
            hb = h % 2
            kv = h // 4
            hq = P.dma("sync", qbufs[hb][:], q[h * 128:(h + 1) * 128, :],
                       [q_free[hb]])
            o_ap = o.rearrange("(j q) c -> q j c", q=128)[:, :, h * 128:(h + 1) * 128]
            qk_reads, v_reads, hdma = emit_attn_head(
                P, ab, h, 32, qbufs[hb], k_sb[kv], v_sb[kv],
                [hks[kv], hvs[kv]], hq, o_ap, md_sb, mw_sb,
                hmd, ident_sb, hid, windowed=True, mask_h2=hmw)
            qf = fences_list(qk_reads)
            q_free[hb] = qf[0] if qf else None
            odma.append(hdma)
        P.barrier("sync", fences_list(odma))
        block = ctx.enter_context(nc.Block())
        P.emit(ctx, block)
    return nc


def build_launch3():
    nc = bass.Bass("TRN2", target_bir_lowering=False)
    di = lambda n, s, d: nc.dram_tensor(n, s, d, kind="ExternalInput").ap()
    do = lambda n, s, d: nc.dram_tensor(n, s, d, kind="ExternalOutput").ap()
    x = di("x", [TPC, D], F32)
    xnt = di("xnt", [D, TPC], BF16)
    ot = di("ot", [D, TPC], BF16)
    wot = di("wot", [D, D], F32)
    bor = di("bor", [1, D], F32)
    wft = di("wft", [D, D], F32)
    bfr = di("bfr", [1, D], F32)
    acat = di("acat", [D, 48], F32)
    bcat = di("bcat", [48, D], F32)
    lsi = di("lsi", [1, 4], I32)
    bsel = di("bsel", [1, 1], F32)
    wq1 = di("wq1", [D, D], F32)
    wk1 = di("wk1", [D, 1024], F32)
    wv1 = di("wv1", [D, 1024], F32)
    rcos = di("rcos", [128, TPC], F32)
    rsin = di("rsin", [128, TPC], F32)
    identity = di("identity", [128, 128], F32)
    x1 = do("x1", [TPC, D], F32)
    q1t = do("q1t", [D, TPC], BF16)
    k1t = do("k1t", [1024, TPC], BF16)
    v1 = do("v1", [TPC, 1024], BF16)
    xn1t = do("xn1t", [D, TPC], BF16)
    wdram = nc.dram_tensor("wdram", [1, 48], F32)

    with ExitStack() as ctx:
        sb = lambda n, s, d: ctx.enter_context(nc.sbuf_tensor(n, s, d))
        psum = lambda n, s, d: ctx.enter_context(nc.psum_tensor(n, s, d))
        xn1t_sb = sb("xn1t_sb", [128, 32 * TPC], BF16)
        xnt_tt = sb("xnt_tt", [128, 32 * 128], BF16)
        ot_tt = sb("ot_tt", [128, 32 * 128], BF16)
        wblk = sb("wblk", [128, 32 * 512], BF16)
        g_blk = sb("g_blk", [128, D], F32)
        xs0 = sb("xs0", [128, D], F32)
        xs1 = g_blk
        xntm0 = xnt_tt
        xntm1 = ot_tt
        rcos_sb = sb("rcos_sb", [128, TPC], F32)
        rsin_sb = sb("rsin_sb", [128, TPC], F32)
        wfm = sb("wfm", [128, 32 * 128], BF16)
        ofm0 = sb("ofm0", [128, TPC], BF16)
        ofm1 = sb("ofm1", [128, TPC], BF16)
        rhs0 = sb("rhs0", [128, 512], BF16)
        rhs1 = sb("rhs1", [128, 512], BF16)
        xt0 = sb("xt0", [128, 512], F32)
        xt1 = sb("xt1", [128, 512], F32)
        tmp0 = sb("tmp0", [128, 512], F32)
        tmp1 = sb("tmp1", [128, 512], F32)
        x1t0 = sb("x1t0", [128, 512], F32)
        x1t1 = sb("x1t1", [128, 512], F32)
        acat_sb = sb("acat_sb", [128, 32 * 48], BF16)
        bcat_sb = sb("bcat_sb", [48, D], BF16)
        rt_sb = sb("rt_sb", [48, TPC], BF16)
        w48_sb = sb("w48_sb", [48, 1], F32)
        bo_t = [sb("bo_t0", [1, 512], F32), sb("bo_t1", [1, 512], F32)]
        bf_t = [sb("bf_t0", [1, 512], F32), sb("bf_t1", [1, 512], F32)]
        ones_sb = sb("ones_sb", [1, 128], F32)
        ones16 = sb("ones16", [1, 16], F32)
        lsii = sb("lsii", [1, 4], I32)
        lsif = sb("lsif", [1, 4], F32)
        bsel_sb = sb("bsel_sb", [1, 1], F32)
        cnt_sb = sb("cnt_sb", [1, 1], F32)
        uc_sb = sb("uc_sb", [1, 1], F32)
        gate_sb = sb("gate_sb", [1, 1], F32)
        wa_sb = sb("wa_sb", [1, 3], F32)
        wfree = sb("wfree", [1, 48], F32)
        ident_sb = sb("ident_sb", [128, 128], BF16)
        stats = alloc_stats(nc, ctx, "st")
        ps_a = psum("ps_a", [128, 512], F32)
        ps_b = psum("ps_b", [128, 512], F32)
        ps_rt = psum("ps_rt", [48, 512], F32)
        ptps0 = psum("ptps0", [128, 1024], BF16)
        ptps1 = psum("ptps1", [128, 1024], BF16)
        ps_c = psum("ps_c", [128, 512], F32)
        ps_d = psum("ps_d", [128, 512], F32)
        dscr = psum("dscr", [128, 1024], BF16)

        P = Plan(nc)
        P.settle_fn = lambda e: e.transpose(dscr[:, 0:128], ident_sb[:],
                                            ident_sb[:])
        hid = P.dma("gpsimd", ident_sb[:], identity)

        hcos = P.dma("sync", rcos_sb[:], rcos)
        hsin = P.dma("sync", rsin_sb[:], rsin)
        hones = P.op("vector", lambda e: e.memset(ones_sb[:], 1.0))
        hones16 = P.op("vector", lambda e: e.memset(ones16[:], 1.0))
        hacat = P.dma("gpsimd", acat_sb[:].rearrange("p (k c) -> p k c", k=32),
                      acat.rearrange("(k p) c -> p k c", p=128))
        hbcat = P.dma("gpsimd", bcat_sb[:], bcat)

        hlsi = P.dma("sync", lsii[:], lsi)
        hbs = P.dma("sync", bsel_sb[:], bsel)
        hc0 = P.op("vector", lambda e: e.tensor_copy(lsif[:], lsii[:]), [hlsi])
        hc1 = P.op("vector", lambda e: e.tensor_scalar(
            lsif[:], lsif[:], bsel_sb[0:1, 0:1], None, ALU.is_le), [hc0, hbs])
        hc2 = P.op("vector", lambda e: e.reduce_sum(
            cnt_sb[:], lsif[:], axis=AX.X), [hc1])
        hc3 = P.op("vector", lambda e: e.tensor_scalar(
            uc_sb[:], cnt_sb[:], -2.0, 0.0, ALU.add, ALU.max), [hc2])
        hc4 = P.op("vector", lambda e: e.tensor_scalar(
            uc_sb[:], uc_sb[:], 2.0, None, ALU.min), [hc3])
        hc5 = P.op("vector", lambda e: e.tensor_scalar(
            gate_sb[:], cnt_sb[:], 2.0, None, ALU.is_ge), [hc2])
        hws = []
        for a in range(3):
            hw_ = P.op("vector", lambda e, a=a: e.tensor_scalar(
                wa_sb[0:1, a:a + 1], uc_sb[:], float(a),
                gate_sb[0:1, 0:1], ALU.is_equal, ALU.mult), [hc4, hc5])
            hws.append(hw_)
        hwf = []
        for a in range(3):
            hw2 = P.op("vector", lambda e, a=a: e.tensor_scalar(
                wfree[0:1, 16 * a:16 * (a + 1)], ones16[:],
                wa_sb[0:1, a:a + 1], None, ALU.mult), [hws[a], hones16])
            hwf.append(hw2)
        hwd = P.dma("sync", wdram.ap(), wfree[:], fences_list(hwf))
        hw48 = P.dma("sync", w48_sb[:], wdram.ap().rearrange("o (k c) -> k (o c)", k=48),
                     [hwd])

        rt_done = []
        hrt_prev = None
        for t2 in range(2):
            hmm = None
            for kc in range(32):
                rb_ = rhs0 if kc % 2 == 0 else rhs1
                hrl = P.dma("sync", rb_[:],
                            xnt[kc * 128:(kc + 1) * 128,
                                t2 * 512:(t2 + 1) * 512],
                            [P._war_rhs[kc % 2] if hasattr(P, "_war_rhs")
                             else None])
                if not hasattr(P, "_war_rhs"):
                    P._war_rhs = [None, None]
                hmm = P.op("tensor", lambda e, kc=kc, rb_=rb_: e.matmul(
                    ps_rt[:], acat_sb[:, kc * 48:(kc + 1) * 48], rb_[:],
                    start=(kc == 0), stop=(kc == 31)),
                    [hrl, hacat, hid, hrt_prev if kc == 0 else None])
                P._war_rhs[kc % 2] = hmm
            hev = P.op("vector", lambda e, t2=t2: e.tensor_scalar(
                rt_sb[:, t2 * 512:(t2 + 1) * 512], ps_rt[:], w48_sb[:],
                None, ALU.mult), [hmm, hw48])
            hrt_prev = hev
            rt_done.append(hev)
        hrt = fences_list(rt_done)

        war_xnt_tt = [None]
        war_ot_tt = [None]
        war_wblk = [None]
        war_ps = {"a": [None], "b": [None]}
        war_xt = [None, None]
        war_tmp = [None, None]
        war_x1t = [None, None]
        war_g = [None]
        x1_stores = []
        war_bf = [None, None]
        war_bo = [None, None]
        for ob in range(8):
            pb = ob % 2
            hwf_ = P.dma("gpsimd", wblk[:].rearrange("p (k c) -> p k c", k=32),
                         wft.rearrange("(k p) o -> p k o", p=128)
                         [:, :, ob * 512:(ob + 1) * 512], [war_wblk[0]])
            hbf = P.dma("sync", bf_t[pb][:],
                        bfr[0:1, ob * 512:(ob + 1) * 512], [war_bf[pb]])
            g_done = []
            for tt in range(8):
                hxl = P.dma("sync",
                            xnt_tt[:].rearrange("p (k c) -> p k c", k=32),
                            xnt.rearrange("(k p) t -> p k t", p=128)
                            [:, :, tt * 128:(tt + 1) * 128],
                            [war_xnt_tt[0]])
                pi = tt % 2
                pst = ps_a if pi == 0 else ps_b

                def ffnmm(e, tt=tt, ob=ob, pst=pst, pb=pb):
                    r = e.matmul(pst[:], ones_sb[:], bf_t[pb][:],
                                 start=True, stop=False)
                    for kc in range(32):
                        r = e.matmul(pst[:],
                                     xnt_tt[:, kc * 128:(kc + 1) * 128],
                                     wblk[:, kc * 512:(kc + 1) * 512],
                                     start=False, stop=False)
                    r = e.matmul(pst[:], rt_sb[:, tt * 128:(tt + 1) * 128],
                                 bcat_sb[:, ob * 512:(ob + 1) * 512],
                                 start=False, stop=True)
                    return r
                hmm = P.op("tensor", ffnmm,
                           [hxl, hwf_, war_ps["a" if pi == 0 else "b"][0],
                            hones, hbf, hbcat] + hrt)
                war_xnt_tt[0] = hmm
                hsil = P.op("scalar", lambda e, tt=tt, pst=pst: e.activation(
                    g_blk[:, tt * 512:(tt + 1) * 512], pst[:], AF.Silu),
                    [hmm, war_g[0]])
                war_ps["a" if pi == 0 else "b"][0] = hsil
                g_done.append(hsil)
            war_bf[pb] = hmm
            hwo_ = P.dma("gpsimd", wblk[:].rearrange("p (k c) -> p k c", k=32),
                         wot.rearrange("(k p) o -> p k o", p=128)
                         [:, :, ob * 512:(ob + 1) * 512],
                         [war_xnt_tt[0]])
            hbo = P.dma("sync", bo_t[pb][:],
                        bor[0:1, ob * 512:(ob + 1) * 512], [war_bo[pb]])
            g_all = fences_list(g_done)
            for tt in range(8):
                hol = P.dma("sync",
                            ot_tt[:].rearrange("p (k c) -> p k c", k=32),
                            ot.rearrange("(k p) t -> p k t", p=128)
                            [:, :, tt * 128:(tt + 1) * 128],
                            [war_ot_tt[0]])
                pi = tt % 2
                pst = ps_a if pi == 0 else ps_b

                def womm(e, tt=tt, ob=ob, pst=pst, pb=pb):
                    r = e.matmul(pst[:], ones_sb[:], bo_t[pb][:],
                                 start=True, stop=False)
                    for kc in range(32):
                        r = e.matmul(pst[:],
                                     ot_tt[:, kc * 128:(kc + 1) * 128],
                                     wblk[:, kc * 512:(kc + 1) * 512],
                                     start=False, stop=(kc == 31))
                    return r
                hmm = P.op("tensor", womm,
                           [hol, hwo_, war_ps["a" if pi == 0 else "b"][0],
                            hones, hbo])
                war_ot_tt[0] = hmm
                bi = tt % 2
                hxt = P.dma("sync", (xt0 if bi == 0 else xt1)[:],
                            x[tt * 128:(tt + 1) * 128,
                              ob * 512:(ob + 1) * 512], [war_xt[bi]])
                htm = P.op("vector", lambda e, tt=tt, pst=pst, bi=bi:
                           e.tensor_tensor((tmp0 if bi == 0 else tmp1)[:],
                                           pst[:],
                                           g_blk[:, tt * 512:(tt + 1) * 512],
                                           ALU.mult),
                           [hmm, war_tmp[bi]] + g_all)
                war_ps["a" if pi == 0 else "b"][0] = htm
                hx1 = P.op("vector", lambda e, bi=bi: e.tensor_tensor(
                    (x1t0 if bi == 0 else x1t1)[:],
                    (tmp0 if bi == 0 else tmp1)[:],
                    (xt0 if bi == 0 else xt1)[:], ALU.add),
                    [htm, hxt, war_x1t[bi]])
                war_tmp[bi] = hx1
                war_xt[bi] = hx1
                hst = P.dma("sync", x1[tt * 128:(tt + 1) * 128,
                                       ob * 512:(ob + 1) * 512],
                            (x1t0 if bi == 0 else x1t1)[:], [hx1])
                war_x1t[bi] = hst
                x1_stores.append(hst)
            war_bo[pb] = hmm
            war_wblk[0] = hmm
            war_g[0] = fences_list([war_ps["a"][0], war_ps["b"][0]])[0]
        hx1f = fences_list(x1_stores)

        ln_bufs = {
            "xs": [xs0, xs1], "stats": stats,
            "xn_tm": [xntm0, xntm1],
            "pt_ps": [ptps0, ptps1],
        }
        ln_done = emit_ln(P, ln_bufs, x1, xn1t_sb, ident_sb, ident_h=hid,
                          extra_dep=hx1f)
        lnf = fences_list(ln_done)
        hxn1 = P.dma("sync", xn1t.rearrange("(k p) t -> p k t", p=128),
                     xn1t_sb[:].rearrange("p (k t) -> p k t", k=32), lnf)

        pbufs = {
            "w_fm": [wfm, wfm],
            "ps_fm": [ps_c, ps_d],
            "o_fm": [ofm0, ofm1],
            "rope_a": tmp0, "rope_b": tmp1,
            "w_tm": [wblk, wblk],
            "ps_tm": [ps_a, ps_b],
            "v_tm": [rhs0, rhs1],
            "war_w_fm": [None, None], "war_ps_fm": [None, None],
            "war_o_fm": [None, None],
            "war_w_tm": [fences_list([war_wblk[0]])[0], None],
            "war_ps_tm": [war_ps["a"][0], war_ps["b"][0]],
            "war_v_tm": [P._war_rhs[0], P._war_rhs[1]],
            "rope_war": [fences_list([war_tmp[0], war_tmp[1]])[0]],
        }
        hv1 = emit_proj_tm(P, pbufs, wv1, v1, lnf, xn1t_sb, 1024, tag="v1")
        pbufs["war_w_fm"] = [None, None]
        hrope = [hcos, hsin]
        hq1 = emit_proj_fm(P, pbufs, wq1, q1t, lnf, xn1t_sb, D,
                           rope=(rcos_sb, rsin_sb), rope_dep=hrope, tag="q1")
        hk1 = emit_proj_fm(P, pbufs, wk1, k1t, lnf, xn1t_sb, 1024,
                           rope=(rcos_sb, rsin_sb), rope_dep=hrope, tag="k1")
        P.barrier("sync", fences_list([hxn1] + hv1 + hq1 + hk1 +
                                      x1_stores + [hcos, hsin, hid]))
        block = ctx.enter_context(nc.Block())
        P.emit(ctx, block)
    return nc


def build_launch5():
    nc = bass.Bass("TRN2", target_bir_lowering=False)
    di = lambda n, s, d: nc.dram_tensor(n, s, d, kind="ExternalInput").ap()
    do = lambda n, s, d: nc.dram_tensor(n, s, d, kind="ExternalOutput").ap()
    x1 = di("x1", [TPC, D], F32)
    xn1t = di("xn1t", [D, TPC], BF16)
    o1t = di("o1t", [D, TPC], BF16)
    wo1 = di("wo1", [D, D], F32)
    a1cat = di("a1cat", [D, 128], F32)
    b1cat = di("b1cat", [128, D], F32)
    wrt = di("wrt", [D, 8], F32)
    brr = di("brr", [1, 8], F32)
    e8 = di("e8", [8, 128], F32)
    identity = di("identity", [128, 128], F32)
    out = do("out", [TPC, D], F32)

    with ExitStack() as ctx:
        sb = lambda n, s, d: ctx.enter_context(nc.sbuf_tensor(n, s, d))
        psum = lambda n, s, d: ctx.enter_context(nc.psum_tensor(n, s, d))
        xn1t_sb = sb("xn1t_sb", [128, 32 * TPC], BF16)
        wblk = sb("wblk", [128, 32 * 512], BF16)
        g_blk = sb("g_blk", [128, D], F32)
        o1_tt = sb("o1_tt", [128, 32 * 128], BF16)
        a1_sb = sb("a1_sb", [128, 32 * 128], BF16)
        b1_sb = sb("b1_sb", [128, D], BF16)
        wr_sb = sb("wr_sb", [128, 32 * 8], BF16)
        e8_sb = sb("e8_sb", [8, 128], BF16)
        br_sb = sb("br_sb", [1, 8], F32)
        ones_sb = sb("ones_sb", [1, 128], F32)
        ident_sb = sb("ident_sb", [128, 128], BF16)
        p8_sb = sb("p8_sb", [128, 8], F32)
        negm8 = sb("negm8", [128, 1], F32)
        z8 = sb("z8", [128, 1], F32)
        rz8 = sb("rz8", [128, 1], F32)
        sc8 = sb("sc8", [128, 8], BF16)
        sct_sb = sb("sct_sb", [8, 128], BF16)
        scb_sb = sb("scb_sb", [128, 128], F32)
        lows_sb = sb("lows_sb", [128, TPC], BF16)
        xt0 = sb("xt0", [128, 512], F32)
        xt1 = sb("xt1", [128, 512], F32)
        tmp0 = sb("tmp0", [128, 512], F32)
        tmp1 = sb("tmp1", [128, 512], F32)
        ot0 = sb("ot0", [128, 512], F32)
        ot1 = sb("ot1", [128, 512], F32)
        ps_r1 = psum("ps_r1", [128, 512], F32)
        ps_r2 = psum("ps_r2", [128, 512], F32)
        ps_t8 = psum("ps_t8", [8, 128], BF16)
        ps_a = psum("ps_a", [128, 512], F32)
        ps_b = psum("ps_b", [128, 512], F32)
        dscr = psum("dscr", [128, 1024], BF16)

        P = Plan(nc)
        P.settle_fn = lambda e: e.transpose(dscr[:, 0:128], ident_sb[:],
                                            ident_sb[:])
        hid = P.dma("gpsimd", ident_sb[:], identity)
        hxn = P.dma("sync", xn1t_sb[:].rearrange("p (k t) -> p k t", k=32),
                    xn1t.rearrange("(k p) t -> p k t", p=128))
        ha1 = P.dma("gpsimd", a1_sb[:].rearrange("p (k c) -> p k c", k=32),
                    a1cat.rearrange("(k p) c -> p k c", p=128))
        hb1 = P.dma("gpsimd", b1_sb[:], b1cat)
        hwr = P.dma("gpsimd", wr_sb[:].rearrange("p (k c) -> p k c", k=32),
                    wrt.rearrange("(k p) c -> p k c", p=128))
        he8 = P.dma("gpsimd", e8_sb[:], e8)
        hbr = P.dma("sync", br_sb[:], brr)
        hones = P.op("vector", lambda e: e.memset(ones_sb[:], 1.0))

        war = {}
        lows_done = []
        for tt in range(8):
            def logmm(e, tt=tt):
                r = e.matmul(ps_r1[:, 0:8], ones_sb[:], br_sb[:],
                             start=True, stop=False)
                for kc in range(32):
                    r = e.matmul(ps_r1[:, 0:8],
                                 xn1t_sb[:, kc * TPC + tt * 128:
                                         kc * TPC + (tt + 1) * 128],
                                 wr_sb[:, kc * 8:(kc + 1) * 8],
                                 start=False, stop=(kc == 31))
                return r
            hlg = P.op("tensor", logmm,
                       [hxn, hwr, hbr, hones, hid, war.get("ps_r1")])
            hmx = P.op("vector", lambda e: e.reduce_max(
                negm8[:], ps_r1[:, 0:8], axis=AX.X, negate=True),
                [hlg, war.get("negm8")])
            hex_ = P.op("scalar", lambda e: e.activation(
                p8_sb[:], ps_r1[:, 0:8], AF.Exp, bias=negm8[:],
                accum_out=z8[:]), [hmx, war.get("p8")])
            war["negm8"] = hex_
            hrz = P.op("vector", lambda e: e.reciprocal(rz8[:], z8[:]),
                       [hex_, war.get("rz8")])
            hrz2 = P.op("vector", lambda e: e.tensor_scalar(
                rz8[:], rz8[:], 8.0, None, ALU.mult), [hrz])
            hsc = P.op("vector", lambda e: e.tensor_scalar(
                sc8[:], p8_sb[:], rz8[:], None, ALU.mult),
                [hrz2, war.get("sc8")])
            war["p8"] = hsc
            war["rz8"] = hsc
            htr = P.op("tensor", lambda e: e.transpose(
                ps_t8[:], sc8[:], ident_sb[:]),
                [hsc, hid, war.get("ps_t8")])
            war["sc8"] = htr
            hev = P.op("scalar", lambda e: e.copy(sct_sb[:], ps_t8[:]),
                       [htr, war.get("sct")])
            war["ps_t8"] = hev
            hsb = P.op("tensor", lambda e: e.matmul(
                ps_r1[:, 128:256], e8_sb[:], sct_sb[:],
                start=True, stop=True), [hev, he8, war.get("ps_r1b")])
            war["sct"] = hsb
            war["ps_r1"] = hsb
            hscb = P.op("vector", lambda e: e.tensor_copy(
                scb_sb[:], ps_r1[:, 128:256]), [hsb, war.get("scb")])
            war["ps_r1b"] = hscb

            def lowmm(e, tt=tt):
                r = None
                for kc in range(32):
                    r = e.matmul(ps_r2[:, 0:128],
                                 a1_sb[:, kc * 128:(kc + 1) * 128],
                                 xn1t_sb[:, kc * TPC + tt * 128:
                                         kc * TPC + (tt + 1) * 128],
                                 start=(kc == 0), stop=(kc == 31))
                return r
            hlo = P.op("tensor", lowmm, [hxn, ha1, war.get("ps_r2")])
            hls = P.op("vector", lambda e, tt=tt: e.tensor_tensor(
                lows_sb[:, tt * 128:(tt + 1) * 128], ps_r2[:, 0:128],
                scb_sb[:], ALU.mult), [hlo, hscb, war.get("lows")])
            war["ps_r2"] = hls
            war["scb"] = hls
            lows_done.append(hls)
        hlows = fences_list(lows_done)

        war_wblk = [None]
        war_o1 = [None]
        war_ps = [None, None]
        war_xt = [None, None]
        war_tmp = [None, None]
        war_ot = [None, None]
        war_g = [None]
        out_stores = []
        for ob in range(8):
            hwo_ = P.dma("gpsimd", wblk[:].rearrange("p (k c) -> p k c", k=32),
                         wo1.rearrange("(k p) o -> p k o", p=128)
                         [:, :, ob * 512:(ob + 1) * 512], [war_wblk[0]])
            g_done = []
            for tt in range(8):
                pi = tt % 2
                pst = ps_a if pi == 0 else ps_b
                hg = P.op("tensor", lambda e, tt=tt, ob=ob, pst=pst: e.matmul(
                    pst[:], lows_sb[:, tt * 128:(tt + 1) * 128],
                    b1_sb[:, ob * 512:(ob + 1) * 512],
                    start=True, stop=True),
                    hlows + [hb1, war_ps[pi]])
                hsil = P.op("scalar", lambda e, tt=tt, pst=pst: e.activation(
                    g_blk[:, tt * 512:(tt + 1) * 512], pst[:], AF.Silu,
                    scale=2.0), [hg, war_g[0]])
                war_ps[pi] = hsil
                g_done.append(hsil)
            g_all = fences_list(g_done)
            for tt in range(8):
                hol = P.dma("sync",
                            o1_tt[:].rearrange("p (k c) -> p k c", k=32),
                            o1t.rearrange("(k p) t -> p k t", p=128)
                            [:, :, tt * 128:(tt + 1) * 128], [war_o1[0]])
                pi = tt % 2
                pst = ps_a if pi == 0 else ps_b

                def womm(e, tt=tt, ob=ob, pst=pst):
                    r = None
                    for kc in range(32):
                        r = e.matmul(pst[:],
                                     o1_tt[:, kc * 128:(kc + 1) * 128],
                                     wblk[:, kc * 512:(kc + 1) * 512],
                                     start=(kc == 0), stop=(kc == 31))
                    return r
                hmm = P.op("tensor", womm, [hol, hwo_, war_ps[pi]])
                war_o1[0] = hmm
                bi = tt % 2
                hxt = P.dma("sync", (xt0 if bi == 0 else xt1)[:],
                            x1[tt * 128:(tt + 1) * 128,
                               ob * 512:(ob + 1) * 512], [war_xt[bi]])
                htm = P.op("vector", lambda e, tt=tt, pst=pst, bi=bi:
                           e.tensor_tensor((tmp0 if bi == 0 else tmp1)[:],
                                           pst[:],
                                           g_blk[:, tt * 512:(tt + 1) * 512],
                                           ALU.mult),
                           [hmm, war_tmp[bi]] + g_all)
                war_ps[pi] = htm
                hx1 = P.op("vector", lambda e, bi=bi: e.tensor_tensor(
                    (ot0 if bi == 0 else ot1)[:],
                    (tmp0 if bi == 0 else tmp1)[:],
                    (xt0 if bi == 0 else xt1)[:], ALU.add),
                    [htm, hxt, war_ot[bi]])
                war_tmp[bi] = hx1
                war_xt[bi] = hx1
                hst = P.dma("sync", out[tt * 128:(tt + 1) * 128,
                                        ob * 512:(ob + 1) * 512],
                            (ot0 if bi == 0 else ot1)[:], [hx1])
                war_ot[bi] = hst
                out_stores.append(hst)
            war_wblk[0] = hmm
            war_g[0] = fences_list([war_ps[0], war_ps[1]])[0]
        P.barrier("sync", fences_list(out_stores))
        block = ctx.enter_context(nc.Block())
        P.emit(ctx, block)
    return nc


RG = [[0, 1, 2, 3], [4, 5, 6, 7]]


def build_fused():
    nc = bass.Bass("TRN2", target_bir_lowering=False)
    di = lambda n, s, d: nc.dram_tensor(n, s, d, kind="ExternalInput").ap()
    do = lambda n, s, d: nc.dram_tensor(n, s, d, kind="ExternalOutput").ap()
    dint = lambda n, s, d: nc.dram_tensor(n, s, d)
    x = di("x", [TPC, D], F32)
    wqt = di("wqt", [D, D], F32)
    wkt = di("wkt", [D, D], F32)
    wvt = di("wvt", [D, D], F32)
    bqt = di("bqt", [128, 32], F32)
    bkt = di("bkt", [128, 32], F32)
    bvr = di("bvr", [1, D], F32)
    wot = di("wot", [D, D], F32)
    bor = di("bor", [1, D], F32)
    wft = di("wft", [D, D], F32)
    bfr = di("bfr", [1, D], F32)
    acat = di("acat", [D, 48], F32)
    bcat = di("bcat", [48, D], F32)
    lsi = di("lsi", [1, 4], I32)
    bsel = di("bsel", [1, 1], F32)
    wq1 = di("wq1", [D, D], F32)
    wk1 = di("wk1", [D, 1024], F32)
    wv1 = di("wv1", [D, 1024], F32)
    rcos = di("rcos", [128, TPC], F32)
    rsin = di("rsin", [128, TPC], F32)
    wo1 = di("wo1", [D, D], F32)
    a1cat = di("a1cat", [D, 128], F32)
    b1cat = di("b1cat", [128, D], F32)
    wrt = di("wrt", [D, 8], F32)
    brr = di("brr", [1, 8], F32)
    e8 = di("e8", [8, 128], F32)
    mdg = di("mdg", [128, 128], F32)
    mwn = di("mwn", [128, 128], F32)
    identity = di("identity", [128, 128], F32)
    rank = di("rank", [1, 1], I32)
    out = do("out", [TPC, D], F32)
    x1_d = dint("x1_d", [TPC, D], F32)
    wdram = dint("wdram", [1, 48], F32)
    qt_in = dint("qt_in", [D, TPC], BF16)
    kt_in = dint("kt_in", [D, TPC], BF16)
    v_in = dint("v_in", [TPC, D], BF16)
    qt_ag = dint("qt_ag", [4 * D, TPC], BF16)
    kt_ag = dint("kt_ag", [4 * D, TPC], BF16)
    v_ag = dint("v_ag", [S, D], BF16)
    ot_in = dint("ot_in", [1024, S], BF16)
    ot_ag = dint("ot_ag", [D, S], BF16)
    q1_in = dint("q1_in", [D, TPC], BF16)
    k1_in = dint("k1_in", [1024, TPC], BF16)
    v1_in = dint("v1_in", [TPC, 1024], BF16)
    q1_ag = dint("q1_ag", [4 * D, TPC], BF16)
    k1_ag = dint("k1_ag", [4096, TPC], BF16)
    v1_ag = dint("v1_ag", [S, 1024], BF16)
    o1_in = dint("o1_in", [1024, S], BF16)
    o1_ag = dint("o1_ag", [D, S], BF16)

    with ExitStack() as ctx:
        ar = bass.AutoArena(nc)
        T = ar.sbuf_tensor
        xnA = T("xnA", [128, 32 * TPC], BF16)
        ident_sb = T("identW", [128, 128], BF16)
        md_sb = T("md_sb", [128, 128], F32)
        mw_sb = T("mw_sb", [128, 128], F32)
        ones_sb = T("ones_sb", [1, 128], F32)
        ones16 = T("ones16", [1, 16], F32)
        rank_sb = T("rank_sb", [1, 1], I32)
        lsii = T("lsii", [1, 4], I32)
        lsif = T("lsif", [1, 4], F32)
        bsel_sb = T("bsel_sb", [1, 1], F32)
        cnt_sb = T("cnt_sb", [1, 1], F32)
        uc_sb = T("uc_sb", [1, 1], F32)
        gate_sb = T("gate_sb", [1, 1], F32)
        wa_sb = T("wa_sb", [1, 3], F32)
        wfree = T("wfree", [1, 48], F32)
        w48_sb = T("w48_sb", [48, 1], F32)
        rt_sb = T("rt_sb", [48, TPC], BF16)
        o_tmp = T("o_tmp", [128, 128], BF16)
        oacc_sb = T("oacc_sb", [128, 128], F32)
        stats = {n: T(f"st_{n}", [128, 1], F32)
                 for n in ["ssq", "ssum", "mean", "tmp", "vpe", "s0", "r0",
                           "t2", "s1", "rstd", "nmr"]}
        fl = {n: T(f"fl_{n}", [128, 1], F32)
              for n in ["negm", "negm2", "dtmp", "alpha", "zz", "zp", "rz",
                        "negmsc"]}
        mark = ar.addr
        xs0 = T("xs0", [128, D], F32)
        xs1 = T("xs1", [128, D], F32)
        xntm0 = T("xntm0", [128, D], BF16)
        xntm1 = T("xntm1", [128, D], BF16)
        wfm0 = T("wfm0", [128, 32 * 128], BF16)
        wfm1 = T("wfm1", [128, 32 * 128], BF16)
        ofm0 = T("ofm0", [128, TPC], BF16)
        ofm1 = T("ofm1", [128, TPC], BF16)
        wblkA = T("wblkA", [128, 32 * 512], BF16)
        vtm0 = T("vtm0", [128, 512], BF16)
        vtm1 = T("vtm1", [128, 512], BF16)
        bq_sb = T("bq_sb", [128, 32], F32)
        bk_sb = T("bk_sb", [128, 32], F32)
        bv_sb = T("bv_sb", [1, D], F32)
        wm1 = ar.addr
        ar.addr = mark
        q_b0 = T("q_b0", [128, S], BF16)
        q_b1 = T("q_b1", [128, S], BF16)
        k_b0 = T("k_b0", [128, S], BF16)
        k_b1 = T("k_b1", [128, S], BF16)
        v_b0 = T("v_b0", [128, 32 * 128], BF16)
        v_b1 = T("v_b1", [128, 32 * 128], BF16)
        p_b0 = T("p_b0", [128, 2048], BF16)
        p_b1 = T("p_b1", [128, 2048], BF16)
        pt_b0 = T("pt_b0", [128, 1024], BF16)
        pt_b1 = T("pt_b1", [128, 1024], BF16)
        oh_b0 = T("oh_b0", [128, 32 * 128], BF16)
        oh_b1 = T("oh_b1", [128, 32 * 128], BF16)
        wm2 = ar.addr
        ar.addr = mark
        wblkB = T("wblkB", [128, 32 * 512], BF16)
        tmp0 = T("tmp0", [128, 512], F32)
        tmp1 = T("tmp1", [128, 512], F32)
        rcos_sb = T("rcos_sb", [128, TPC], F32)
        rsin_sb = T("rsin_sb", [128, TPC], F32)
        wfmB0 = T("wfmB0", [128, 32 * 128], BF16)
        wfmB1 = T("wfmB1", [128, 32 * 128], BF16)
        ofmB0 = T("ofmB0", [128, TPC], BF16)
        ofmB1 = T("ofmB1", [128, TPC], BF16)
        vtmB0 = T("vtmB0", [128, 512], BF16)
        vtmB1 = T("vtmB1", [128, 512], BF16)
        mark3 = ar.addr
        gB = T("gB", [128, D], F32)
        ot_ttB = T("ot_ttB", [128, 32 * 128], BF16)
        xtB0 = T("xtB0", [128, 512], F32)
        xtB1 = T("xtB1", [128, 512], F32)
        x1tB0 = T("x1tB0", [128, 512], F32)
        x1tB1 = T("x1tB1", [128, 512], F32)
        acat_sb = T("acat_sb", [128, 32 * 48], BF16)
        bcat_sb = T("bcat_sb", [48, D], BF16)
        bo_t = [T("bo_t0", [1, 512], F32), T("bo_t1", [1, 512], F32)]
        bf_t = [T("bf_t0", [1, 512], F32), T("bf_t1", [1, 512], F32)]
        wm3a = ar.addr
        ar.addr = mark3
        xsB0 = T("xsB0", [128, D], F32)
        xsB1 = T("xsB1", [128, D], F32)
        xntmB0 = T("xntmB0", [128, D], BF16)
        xntmB1 = T("xntmB1", [128, D], BF16)
        wm3b = ar.addr
        ar.addr = max(wm3a, wm3b)
        wm3 = ar.addr
        ar.addr = mark
        wblkC = T("wblkC", [128, 32 * 512], BF16)
        gC = T("gC", [128, D], F32)
        ot_ttC = T("ot_ttC", [128, 32 * 128], BF16)
        xtC0 = T("xtC0", [128, 512], F32)
        xtC1 = T("xtC1", [128, 512], F32)
        tmpC0 = T("tmpC0", [128, 512], F32)
        tmpC1 = T("tmpC1", [128, 512], F32)
        x1tC0 = T("x1tC0", [128, 512], F32)
        x1tC1 = T("x1tC1", [128, 512], F32)
        a1_sb = T("a1_sb", [128, 32 * 128], BF16)
        b1_sb = T("b1_sb", [128, D], BF16)
        wr_sb = T("wr_sb", [128, 32 * 8], BF16)
        e8_sb = T("e8_sb", [8, 128], BF16)
        br_sb = T("br_sb", [1, 8], F32)
        p8_sb = T("p8_sb", [128, 8], F32)
        negm8 = T("negm8", [128, 1], F32)
        z8 = T("z8", [128, 1], F32)
        rz8 = T("rz8", [128, 1], F32)
        sc8 = T("sc8", [128, 8], BF16)
        sct_sb = T("sct_sb", [8, 128], BF16)
        scb_sb = T("scb_sb", [128, 128], F32)
        lows_sb = T("lows_sb", [128, TPC], BF16)
        ar.addr = max(ar.addr, wm1, wm2, wm3)
        ar.allocate()

        psum = lambda n, s, d: ctx.enter_context(nc.psum_tensor(n, s, d))
        s_ps = psum("s_ps", [128, 2048], F32)
        ptp0 = psum("ptp0", [128, 1024], BF16)
        ptp1 = psum("ptp1", [128, 1024], BF16)
        o_ps = psum("o_ps", [128, 512], F32)
        dscr = psum("dscr", [128, 1024], BF16)
        ps_a = s_ps[:, 0:512]
        ps_b = s_ps[:, 512:1024]
        ps_c = s_ps[:, 1024:1536]
        ps_d = s_ps[:, 1536:2048]
        rk_reg = ctx.enter_context(nc.gpsimd.register("rk_reg"))

        P = Plan(nc)
        SS = {}

        hid = P.dma("gpsimd", ident_sb[:], identity)
        hmd = P.dma("gpsimd", md_sb[:], mdg)
        hmw = P.dma("gpsimd", mw_sb[:], mwn)
        hrank = P.dma("gpsimd", rank_sb[:], rank)

        def pre(e):
            e.reg_load(rk_reg, rank_sb[0:1, 0:1])
            SS["rk"] = e.snap(rk_reg, min_val=0, max_val=3)
        P.raw("gpsimd", pre, [hrank])
        hbq = P.dma("sync", bq_sb[:], bqt)
        hbk = P.dma("sync", bk_sb[:], bkt)
        hbv = P.dma("sync", bv_sb[:], bvr)
        hones = P.op("vector", lambda e: e.memset(ones_sb[:], 1.0))
        hones16 = P.op("vector", lambda e: e.memset(ones16[:], 1.0))
        hlsi = P.dma("sync", lsii[:], lsi)
        hbs = P.dma("sync", bsel_sb[:], bsel)
        hc0 = P.op("vector", lambda e: e.tensor_copy(lsif[:], lsii[:]), [hlsi])
        hc1 = P.op("vector", lambda e: e.tensor_scalar(
            lsif[:], lsif[:], bsel_sb[0:1, 0:1], None, ALU.is_le), [hc0, hbs])
        hc2 = P.op("vector", lambda e: e.reduce_sum(
            cnt_sb[:], lsif[:], axis=AX.X), [hc1])
        hc3 = P.op("vector", lambda e: e.tensor_scalar(
            uc_sb[:], cnt_sb[:], -2.0, 0.0, ALU.add, ALU.max), [hc2])
        hc4 = P.op("vector", lambda e: e.tensor_scalar(
            uc_sb[:], uc_sb[:], 2.0, None, ALU.min), [hc3])
        hc5 = P.op("vector", lambda e: e.tensor_scalar(
            gate_sb[:], cnt_sb[:], 2.0, None, ALU.is_ge), [hc2])
        hws = []
        for a in range(3):
            hws.append(P.op("vector", lambda e, a=a: e.tensor_scalar(
                wa_sb[0:1, a:a + 1], uc_sb[:], float(a),
                gate_sb[0:1, 0:1], ALU.is_equal, ALU.mult), [hc4, hc5]))
        hwf = []
        for a in range(3):
            hwf.append(P.op("vector", lambda e, a=a: e.tensor_scalar(
                wfree[0:1, 16 * a:16 * (a + 1)], ones16[:],
                wa_sb[0:1, a:a + 1], None, ALU.mult), [hws[a], hones16]))
        hwd = P.dma("sync", wdram.ap(), wfree[:], fences_list(hwf))
        hw48 = P.dma("sync", w48_sb[:],
                     wdram.ap().rearrange("o (k c) -> k (o c)", k=48), [hwd])

        ln_bufs = {"xs": [xs0, xs1], "stats": stats,
                   "xn_tm": [xntm0, xntm1], "pt_ps": [ptp0, ptp1]}
        ln_done = emit_ln(P, ln_bufs, x, xnA, ident_sb, ident_h=hid)
        lnf = fences_list(ln_done)
        pbufs = {"w_fm": [wfm0, wfm1], "ps_fm": [ps_a, ps_b],
                 "o_fm": [ofm0, ofm1],
                 "w_tm": [wblkA, wblkA], "ps_tm": [ps_c, ps_d],
                 "v_tm": [vtm0, vtm1]}
        hq = emit_proj_fm(P, pbufs, wqt, qt_in.ap(), lnf, xnA, D,
                          bias_sb=bq_sb, bias_h=hbq, scale=1.0, tag="q")
        hk = emit_proj_fm(P, pbufs, wkt, kt_in.ap(), lnf, xnA, D,
                          bias_sb=bk_sb, bias_h=hbk, scale=1.0, tag="k")
        hv = emit_proj_tm(P, pbufs, wvt, v_in.ap(), lnf, xnA, D,
                          bias_row_sb=bv_sb, bias_h=hbv, ones_sb=ones_sb,
                          ones_h=hones, tag="v")
        ph1 = P.fence_all()

        def mkcc(name, in_t, out_t, deps):
            return P.op("gpsimd", lambda e: e.collective_compute(
                "AllGather", ALU.bypass, replica_groups=RG,
                ins=[in_t.ap().opt()], outs=[out_t.ap().opt()]),
                deps, kind=("cc", name))
        hcc_q = mkcc("ccq", qt_in, qt_ag, ph1)
        hcc_k = mkcc("cck", kt_in, kt_ag, ph1)
        hcc_v = mkcc("ccv", v_in, v_ag, ph1)

        ab = {"fl": fl, "oacc": oacc_sb, "o_tmp": o_tmp,
              "p_sb": [p_b0, p_b1], "pt_sb": [pt_b0, pt_b1],
              "ohead": [oh_b0, oh_b1], "s_ps": s_ps,
              "pt_ps": [ptp0, ptp1], "o_ps": o_ps,
              "otr_ps": dscr[:, 128:256], "war": {}}
        qagr = qt_ag.ap().rearrange("(r q) t -> q r t", r=4)
        kagr = kt_ag.ap().rearrange("(r q) t -> q r t", r=4)
        vagr = v_ag.ap().rearrange("(kc p) c -> p kc c", p=128)
        qk_free = [None, None]
        v_free = [None, None]
        att0_dma = []
        for h in range(8):
            hb = h % 2
            qsb = [q_b0, q_b1][hb]
            ksb = [k_b0, k_b1][hb]
            vsb = [v_b0, v_b1][hb]

            def ldq(e, h=h, qsb=qsb):
                return e.dma_start(
                    out=qsb[:].rearrange("p (r t) -> p r t", r=4),
                    in_=qagr[bass.ds(SS["rk"] * 1024 + 128 * h, 128), :, :])
            hql = P.op("gpsimd", ldq,
                       [hcc_q, qk_free[hb]] + (ph1 if h < 2 else []),
                       kind=("d", f"qb{hb}"))

            def ldk(e, h=h, ksb=ksb):
                return e.dma_start(
                    out=ksb[:].rearrange("p (r t) -> p r t", r=4),
                    in_=kagr[bass.ds(SS["rk"] * 1024 + 128 * h, 128), :, :])
            hkl = P.op("gpsimd", ldk, [hcc_k, qk_free[hb]],
                       kind=("d", f"kb{hb}"))

            def ldv(e, h=h, vsb=vsb):
                return e.dma_start(
                    out=vsb[:].rearrange("p (kc c) -> p kc c", kc=32),
                    in_=vagr[:, :, bass.ds(SS["rk"] * 1024 + 128 * h, 128)])
            hvl = P.op("gpsimd", ldv, [hcc_v, v_free[hb]],
                       kind=("d", f"vb{hb}"))
            o_ap = ot_in.ap()[h * 128:(h + 1) * 128, :]
            qk_reads, v_reads, hdma = emit_attn_head(
                P, ab, h, 32, qsb, ksb, vsb, [hkl, hvl], hql, o_ap,
                md_sb, None, hmd, ident_sb, hid, windowed=False,
                transpose_out=True, o_slot=f"st_oh{hb}")
            qf = fences_list(qk_reads)
            qk_free[hb] = qf[0] if qf else None
            vf = fences_list(v_reads)
            v_free[hb] = vf[0] if vf else None
            att0_dma.append(hdma)
        ph2 = P.fence_all()

        hcc_o = mkcc("cco", ot_in, ot_ag, ph2)
        hacat = P.dma("gpsimd", acat_sb[:].rearrange("p (k c) -> p k c", k=32),
                      acat.rearrange("(k p) c -> p k c", p=128), ph2)
        hbcat = P.dma("gpsimd", bcat_sb[:], bcat, ph2)
        rt_done = []
        hrt_prev = None
        for t2 in range(2):
            def rtmm(e, t2=t2):
                r = None
                for kc in range(32):
                    r = e.matmul(ps_c[0:48, :],
                                 acat_sb[:, kc * 48:(kc + 1) * 48],
                                 xnA[:, kc * TPC + t2 * 512:
                                     kc * TPC + (t2 + 1) * 512],
                                 start=(kc == 0), stop=(kc == 31))
                return r
            hmm_rt = P.op("tensor", rtmm, [hacat, hrt_prev] + lnf + ph2)
            hev_rt = P.op("vector", lambda e, t2=t2: e.tensor_scalar(
                rt_sb[:, t2 * 512:(t2 + 1) * 512], ps_c[0:48, :], w48_sb[:],
                None, ALU.mult), [hmm_rt, hw48])
            hrt_prev = hev_rt
            rt_done.append(hev_rt)
        hrt = fences_list(rt_done)

        otagr = ot_ag.ap().rearrange("(kc p) q -> p kc q", p=128)
        war_wblk = [None]
        war_ot = [None]
        war_psA = [hrt_prev]
        war_psB = [None]
        war_xt = [None, None]
        war_tmp = [None, None]
        war_x1t = [None, None]
        war_g = [None]
        war_bf = [None, None]
        war_bo = [None, None]
        x1_stores = []
        for ob in range(8):
            pb = ob % 2
            hwf_ = P.dma("gpsimd", wblkB[:].rearrange("p (k c) -> p k c", k=32),
                         wft.rearrange("(k p) o -> p k o", p=128)
                         [:, :, ob * 512:(ob + 1) * 512],
                         [war_wblk[0]] + (ph2 if ob == 0 else []))
            hbf = P.dma("sync", bf_t[pb][:],
                        bfr[0:1, ob * 512:(ob + 1) * 512],
                        [war_bf[pb]] + (ph2 if ob < 2 else []))
            g_done = []
            for tt in range(8):
                pi = tt % 2
                pst = ps_a if pi == 0 else ps_b

                def ffnmm(e, tt=tt, ob=ob, pst=pst, pb=pb):
                    r = e.matmul(pst, ones_sb[:], bf_t[pb][:],
                                 start=True, stop=False)
                    for kc in range(32):
                        r = e.matmul(pst,
                                     xnA[:, kc * TPC + tt * 128:
                                         kc * TPC + (tt + 1) * 128],
                                     wblkB[:, kc * 512:(kc + 1) * 512],
                                     start=False, stop=False)
                    r = e.matmul(pst, rt_sb[:, tt * 128:(tt + 1) * 128],
                                 bcat_sb[:, ob * 512:(ob + 1) * 512],
                                 start=False, stop=True)
                    return r
                hmm = P.op("tensor", ffnmm,
                           [hwf_, hbf, hbcat, hones,
                            war_psA[0] if pi == 0 else war_psB[0]]
                           + hrt + lnf)
                hsil = P.op("scalar", lambda e, tt=tt, pst=pst: e.activation(
                    gB[:, tt * 512:(tt + 1) * 512], pst, AF.Silu),
                    [hmm, war_g[0]])
                if pi == 0:
                    war_psA[0] = hsil
                else:
                    war_psB[0] = hsil
                g_done.append(hsil)
            war_bf[pb] = hmm
            hwo_ = P.dma("gpsimd", wblkB[:].rearrange("p (k c) -> p k c", k=32),
                         wot.rearrange("(k p) o -> p k o", p=128)
                         [:, :, ob * 512:(ob + 1) * 512], [hmm])
            hbo = P.dma("sync", bo_t[pb][:],
                        bor[0:1, ob * 512:(ob + 1) * 512],
                        [war_bo[pb]] + (ph2 if ob < 2 else []))
            g_all = fences_list(g_done)
            for tt in range(8):
                def ldot(e, tt=tt):
                    return e.dma_start(
                        out=ot_ttB[:].rearrange("p (k c) -> p k c", k=32),
                        in_=otagr[:, :, bass.ds(SS["rk"] * 1024 + 128 * tt,
                                                128)])
                hol = P.op("gpsimd", ldot, [war_ot[0], hcc_o],
                           kind=("d", "ot_ttB"))
                pi = tt % 2
                pst = ps_a if pi == 0 else ps_b

                def womm(e, tt=tt, ob=ob, pst=pst, pb=pb):
                    r = e.matmul(pst, ones_sb[:], bo_t[pb][:],
                                 start=True, stop=False)
                    for kc in range(32):
                        r = e.matmul(pst,
                                     ot_ttB[:, kc * 128:(kc + 1) * 128],
                                     wblkB[:, kc * 512:(kc + 1) * 512],
                                     start=False, stop=(kc == 31))
                    return r
                hmm = P.op("tensor", womm,
                           [hol, hwo_, hbo, hones,
                            war_psA[0] if pi == 0 else war_psB[0]])
                war_ot[0] = hmm
                bi = tt % 2
                hxt = P.dma("sync", [xtB0, xtB1][bi][:],
                            x[tt * 128:(tt + 1) * 128,
                              ob * 512:(ob + 1) * 512],
                            [war_xt[bi]] + (ph2 if ob == 0 else []))
                htm = P.op("vector", lambda e, tt=tt, pst=pst, bi=bi:
                           e.tensor_tensor([tmp0, tmp1][bi][:], pst,
                                           gB[:, tt * 512:(tt + 1) * 512],
                                           ALU.mult),
                           [hmm, war_tmp[bi]] + g_all)
                if pi == 0:
                    war_psA[0] = htm
                else:
                    war_psB[0] = htm
                hx1 = P.op("vector", lambda e, bi=bi: e.tensor_tensor(
                    [x1tB0, x1tB1][bi][:], [tmp0, tmp1][bi][:],
                    [xtB0, xtB1][bi][:], ALU.add),
                    [htm, hxt, war_x1t[bi]])
                war_tmp[bi] = hx1
                war_xt[bi] = hx1
                hst = P.dma("sync", x1_d.ap()[tt * 128:(tt + 1) * 128,
                                              ob * 512:(ob + 1) * 512],
                            [x1tB0, x1tB1][bi][:], [hx1])
                war_x1t[bi] = hst
                x1_stores.append(hst)
            war_bo[pb] = hmm
            war_wblk[0] = hmm
            war_g[0] = fences_list([war_psA[0], war_psB[0]])[0]
        hx1f = fences_list(x1_stores)

        ln2_extra = hx1f + [war_g[0], war_psA[0], war_psB[0], war_bf[0],
                            war_bf[1], war_bo[0], war_bo[1], war_ot[0],
                            war_x1t[0], war_x1t[1]]
        ln_bufs2 = {"xs": [xsB0, xsB1], "stats": stats,
                    "xn_tm": [xntmB0, xntmB1], "pt_ps": [ptp0, ptp1]}
        ln_done2 = emit_ln(P, ln_bufs2, x1_d.ap(), xnA, ident_sb,
                           ident_h=hid, extra_dep=fences_list(ln2_extra))
        lnf2 = fences_list(ln_done2)

        hcos = P.dma("sync", rcos_sb[:], rcos, ph2)
        hsin = P.dma("sync", rsin_sb[:], rsin, ph2)
        pbufs2 = {"w_fm": [wfmB0, wfmB1], "ps_fm": [ps_a, ps_b],
                  "o_fm": [ofmB0, ofmB1], "rope_a": tmp0, "rope_b": tmp1,
                  "w_tm": [wblkB, wblkB], "ps_tm": [ps_c, ps_d],
                  "v_tm": [vtmB0, vtmB1],
                  "war_w_fm": [None, None],
                  "war_ps_fm": [war_psA[0], war_psB[0]],
                  "war_o_fm": [None, None],
                  "war_w_tm": [war_wblk[0], None],
                  "war_ps_tm": [hrt_prev, None],
                  "war_v_tm": [None, None],
                  "rope_war": [fences_list([war_tmp[0], war_tmp[1]])[0]]}
        hv1 = emit_proj_tm(P, pbufs2, wv1, v1_in.ap(), lnf2, xnA, 1024,
                           tag="v1")
        hrope = [hcos, hsin]
        hq1 = emit_proj_fm(P, pbufs2, wq1, q1_in.ap(), lnf2, xnA, D,
                           rope=(rcos_sb, rsin_sb), rope_dep=hrope, tag="q1")
        hk1 = emit_proj_fm(P, pbufs2, wk1, k1_in.ap(), lnf2, xnA, 1024,
                           rope=(rcos_sb, rsin_sb), rope_dep=hrope, tag="k1")
        ph3 = P.fence_all()

        hcc_q1 = mkcc("ccq1", q1_in, q1_ag, ph3)
        hcc_k1 = mkcc("cck1", k1_in, k1_ag, ph3)
        hcc_v1 = mkcc("ccv1", v1_in, v1_ag, ph3)
        ab2 = dict(ab)
        ab2["war"] = {}
        q1agr = q1_ag.ap().rearrange("(r q) t -> q r t", r=4)
        k1agr = k1_ag.ap().rearrange("(r q) t -> q r t", r=4)
        v1agr = v1_ag.ap().rearrange("(kc p) c -> p kc c", p=128)
        hkls, hvls = [], []
        for kv in range(2):
            def ldk1(e, kv=kv):
                return e.dma_start(
                    out=[k_b0, k_b1][kv][:].rearrange("p (r t) -> p r t", r=4),
                    in_=k1agr[bass.ds(SS["rk"] * 256 + 128 * kv, 128), :, :])
            hkls.append(P.op("gpsimd", ldk1, [hcc_k1] + ph3,
                             kind=("d", f"kb{kv}")))

            def ldv1(e, kv=kv):
                return e.dma_start(
                    out=[v_b0, v_b1][kv][:].rearrange("p (kc c) -> p kc c",
                                                      kc=32),
                    in_=v1agr[:, :, bass.ds(SS["rk"] * 256 + 128 * kv, 128)])
            hvls.append(P.op("gpsimd", ldv1, [hcc_v1] + ph3,
                             kind=("d", f"vb{kv}")))
        q_free = [None, None]
        att1_dma = []
        for h in range(8):
            hb = h % 2
            kv = h // 4
            qsb = [q_b0, q_b1][hb]

            def ldq1(e, h=h, qsb=qsb):
                return e.dma_start(
                    out=qsb[:].rearrange("p (r t) -> p r t", r=4),
                    in_=q1agr[bass.ds(SS["rk"] * 1024 + 128 * h, 128), :, :])
            hql = P.op("gpsimd", ldq1,
                       [hcc_q1, q_free[hb]] + (ph3 if h < 2 else []),
                       kind=("d", f"qb{hb}"))
            o_ap = o1_in.ap()[h * 128:(h + 1) * 128, :]
            qk_reads, v_reads, hdma = emit_attn_head(
                P, ab2, h, 32, qsb, [k_b0, k_b1][kv], [v_b0, v_b1][kv],
                [hkls[kv], hvls[kv]], hql, o_ap, md_sb, mw_sb, hmd,
                ident_sb, hid, windowed=True, mask_h2=hmw,
                transpose_out=True, o_slot=f"st_oh{hb}")
            qf = fences_list(qk_reads)
            q_free[hb] = qf[0] if qf else None
            att1_dma.append(hdma)
        ph4 = P.fence_all()

        hcc_o1 = mkcc("cco1", o1_in, o1_ag, ph4)
        ha1 = P.dma("gpsimd", a1_sb[:].rearrange("p (k c) -> p k c", k=32),
                    a1cat.rearrange("(k p) c -> p k c", p=128), ph4)
        hb1 = P.dma("gpsimd", b1_sb[:], b1cat, ph4)
        hwr = P.dma("gpsimd", wr_sb[:].rearrange("p (k c) -> p k c", k=32),
                    wrt.rearrange("(k p) c -> p k c", p=128), ph4)
        he8 = P.dma("gpsimd", e8_sb[:], e8, ph4)
        hbr = P.dma("sync", br_sb[:], brr, ph4)
        war5 = {}
        lows_done = []
        ps_t8 = ptp0[0:8, 0:128]
        for tt in range(8):
            def logmm(e, tt=tt):
                r = e.matmul(ps_a[:, 0:8], ones_sb[:], br_sb[:],
                             start=True, stop=False)
                for kc in range(32):
                    r = e.matmul(ps_a[:, 0:8],
                                 xnA[:, kc * TPC + tt * 128:
                                     kc * TPC + (tt + 1) * 128],
                                 wr_sb[:, kc * 8:(kc + 1) * 8],
                                 start=False, stop=(kc == 31))
                return r
            hlg = P.op("tensor", logmm,
                       [hwr, hbr, hones, hid, war5.get("ps_a")] + ph4)
            hmx = P.op("vector", lambda e: e.reduce_max(
                negm8[:], ps_a[:, 0:8], axis=AX.X, negate=True),
                [hlg, war5.get("negm8")])
            hex_ = P.op("scalar", lambda e: e.activation(
                p8_sb[:], ps_a[:, 0:8], AF.Exp, bias=negm8[:],
                accum_out=z8[:]), [hmx, war5.get("p8")])
            war5["negm8"] = hex_
            hrz = P.op("vector", lambda e: e.reciprocal(rz8[:], z8[:]),
                       [hex_, war5.get("rz8")])
            hrz2 = P.op("vector", lambda e: e.tensor_scalar(
                rz8[:], rz8[:], 8.0, None, ALU.mult), [hrz])
            hsc = P.op("vector", lambda e: e.tensor_scalar(
                sc8[:], p8_sb[:], rz8[:], None, ALU.mult),
                [hrz2, war5.get("sc8")])
            war5["p8"] = hsc
            war5["rz8"] = hsc
            htr = P.op("tensor", lambda e: e.transpose(
                ps_t8, sc8[:], ident_sb[:]), [hsc, hid, war5.get("ps_t8")])
            war5["sc8"] = htr
            hev = P.op("scalar", lambda e: e.copy(sct_sb[:], ps_t8),
                       [htr, war5.get("sct")])
            war5["ps_t8"] = hev
            hsb = P.op("tensor", lambda e: e.matmul(
                ps_b[:, 0:128], e8_sb[:], sct_sb[:],
                start=True, stop=True), [hev, he8, war5.get("ps_b")])
            war5["sct"] = hsb
            war5["ps_a"] = hsb
            hscb = P.op("vector", lambda e: e.tensor_copy(
                scb_sb[:], ps_b[:, 0:128]), [hsb, war5.get("scb")])
            war5["ps_b"] = hscb

            def lowmm(e, tt=tt):
                r = None
                for kc in range(32):
                    r = e.matmul(ps_c[:, 0:128],
                                 a1_sb[:, kc * 128:(kc + 1) * 128],
                                 xnA[:, kc * TPC + tt * 128:
                                     kc * TPC + (tt + 1) * 128],
                                 start=(kc == 0), stop=(kc == 31))
                return r
            hlo = P.op("tensor", lowmm, [ha1, war5.get("ps_c")])
            hls = P.op("vector", lambda e, tt=tt: e.tensor_tensor(
                lows_sb[:, tt * 128:(tt + 1) * 128], ps_c[:, 0:128],
                scb_sb[:], ALU.mult), [hlo, hscb])
            war5["ps_c"] = hls
            war5["scb"] = hls
            lows_done.append(hls)
        hlows = fences_list(lows_done)

        o1agr = o1_ag.ap().rearrange("(kc p) q -> p kc q", p=128)
        war_wblk5 = [None]
        war_ot5 = [None]
        war_ps5 = [war5["ps_a"], war5["ps_b"]]
        war_xt5 = [None, None]
        war_tmp5 = [None, None]
        war_o5 = [None, None]
        war_g5 = [None]
        out_stores = []
        for ob in range(8):
            hwo_ = P.dma("gpsimd", wblkC[:].rearrange("p (k c) -> p k c", k=32),
                         wo1.rearrange("(k p) o -> p k o", p=128)
                         [:, :, ob * 512:(ob + 1) * 512],
                         [war_wblk5[0]] + (ph4 if ob == 0 else []))
            g_done = []
            for tt in range(8):
                pi = tt % 2
                pst = ps_a if pi == 0 else ps_b
                hg = P.op("tensor", lambda e, tt=tt, ob=ob, pst=pst: e.matmul(
                    pst, lows_sb[:, tt * 128:(tt + 1) * 128],
                    b1_sb[:, ob * 512:(ob + 1) * 512],
                    start=True, stop=True),
                    hlows + [hb1, war_ps5[pi]])
                hsil = P.op("scalar", lambda e, tt=tt, pst=pst: e.activation(
                    gC[:, tt * 512:(tt + 1) * 512], pst, AF.Silu,
                    scale=2.0), [hg, war_g5[0]])
                war_ps5[pi] = hsil
                g_done.append(hsil)
            g_all = fences_list(g_done)
            for tt in range(8):
                def ldo1(e, tt=tt):
                    return e.dma_start(
                        out=ot_ttC[:].rearrange("p (k c) -> p k c", k=32),
                        in_=o1agr[:, :, bass.ds(SS["rk"] * 1024 + 128 * tt,
                                                128)])
                hol = P.op("gpsimd", ldo1, [war_ot5[0], hcc_o1],
                           kind=("d", "ot_ttC"))
                pi = tt % 2
                pst = ps_a if pi == 0 else ps_b

                def womm5(e, tt=tt, ob=ob, pst=pst):
                    r = None
                    for kc in range(32):
                        r = e.matmul(pst,
                                     ot_ttC[:, kc * 128:(kc + 1) * 128],
                                     wblkC[:, kc * 512:(kc + 1) * 512],
                                     start=(kc == 0), stop=(kc == 31))
                    return r
                hmm = P.op("tensor", womm5, [hol, hwo_, war_ps5[pi]])
                war_ot5[0] = hmm
                bi = tt % 2
                hxt = P.dma("sync", [xtC0, xtC1][bi][:],
                            x1_d.ap()[tt * 128:(tt + 1) * 128,
                                      ob * 512:(ob + 1) * 512],
                            [war_xt5[bi]] + hx1f + (ph4 if ob == 0 else []))
                htm = P.op("vector", lambda e, tt=tt, pst=pst, bi=bi:
                           e.tensor_tensor([tmpC0, tmpC1][bi][:], pst,
                                           gC[:, tt * 512:(tt + 1) * 512],
                                           ALU.mult),
                           [hmm, war_tmp5[bi]] + g_all)
                war_ps5[pi] = htm
                hx1 = P.op("vector", lambda e, bi=bi: e.tensor_tensor(
                    [x1tC0, x1tC1][bi][:], [tmpC0, tmpC1][bi][:],
                    [xtC0, xtC1][bi][:], ALU.add),
                    [htm, hxt, war_o5[bi]])
                war_tmp5[bi] = hx1
                war_xt5[bi] = hx1
                hst = P.dma("sync", out[tt * 128:(tt + 1) * 128,
                                        ob * 512:(ob + 1) * 512],
                            [x1tC0, x1tC1][bi][:], [hx1])
                war_o5[bi] = hst
                out_stores.append(hst)
            war_wblk5[0] = hmm
            war_g5[0] = fences_list([war_ps5[0], war_ps5[1]])[0]
        P.barrier("sync", fences_list(out_stores))
        block = ctx.enter_context(nc.Block())
        P.emit(ctx, block)
    return nc

def kernel(L0_attn_x, L0_attn_Wq, L0_attn_bq, L0_attn_Wk, L0_attn_bk,
           L0_attn_Wv, L0_attn_bv, L0_attn_Wo, L0_attn_bo, L0_ffn_weight,
           L0_ffn_bias, L0_ffn_lora_A, L0_ffn_lora_B,
           L0_ffn_lora_start_indices, L1_attn_wq, L1_attn_wk, L1_attn_wv,
           L1_attn_wo, L1_ffn_A, L1_ffn_B, L1_ffn_Wr, L1_ffn_br):
    f32 = np.float32
    x = np.asarray(L0_attn_x, dtype=f32)
    Wq = np.asarray(L0_attn_Wq, dtype=f32); bq = np.asarray(L0_attn_bq, dtype=f32)
    Wk = np.asarray(L0_attn_Wk, dtype=f32); bk = np.asarray(L0_attn_bk, dtype=f32)
    Wv = np.asarray(L0_attn_Wv, dtype=f32); bv = np.asarray(L0_attn_bv, dtype=f32)
    Wo = np.asarray(L0_attn_Wo, dtype=f32); bo = np.asarray(L0_attn_bo, dtype=f32)
    Wf = np.asarray(L0_ffn_weight, dtype=f32)
    bfn = np.asarray(L0_ffn_bias, dtype=f32)
    lA = np.asarray(L0_ffn_lora_A, dtype=f32)
    lB = np.asarray(L0_ffn_lora_B, dtype=f32)
    lsi = np.asarray(L0_ffn_lora_start_indices).astype(np.int32).reshape(1, 4)
    wq1 = np.asarray(L1_attn_wq, dtype=f32)
    wk1 = np.asarray(L1_attn_wk, dtype=f32)
    wv1 = np.asarray(L1_attn_wv, dtype=f32)
    wo1 = np.asarray(L1_attn_wo, dtype=f32)
    A1 = np.asarray(L1_ffn_A, dtype=f32)
    B1 = np.asarray(L1_ffn_B, dtype=f32)
    Wr = np.asarray(L1_ffn_Wr, dtype=f32)
    br = np.asarray(L1_ffn_br, dtype=f32)

    mask_diag, mask_win = make_masks()
    ident = np.eye(128, dtype=f32)
    e8 = np.zeros((8, 128), dtype=f32)
    for h in range(8):
        e8[h, h * 16:(h + 1) * 16] = 1.0
    C = lambda a: np.ascontiguousarray(a)

    shared = {
        "wqt": C(Wq.T), "wkt": C(Wk.T), "wvt": C(Wv.T),
        "bqt": C(bq.reshape(32, 128).T), "bkt": C(bk.reshape(32, 128).T),
        "bvr": C(bv.reshape(1, -1)),
        "wot": C(Wo.T), "bor": C(bo.reshape(1, -1)),
        "wft": C(Wf.T), "bfr": C(bfn.reshape(1, -1)),
        "acat": C(lA.transpose(2, 0, 1).reshape(D, 48)),
        "bcat": C(lB.transpose(0, 2, 1).reshape(48, D)),
        "lsi": lsi,
        "wq1": wq1, "wk1": wk1, "wv1": wv1,
        "wo1": wo1,
        "a1cat": C(A1.transpose(2, 0, 1).reshape(D, 128)),
        "b1cat": C(B1.transpose(0, 2, 1).reshape(128, D)),
        "wrt": C(Wr.T), "brr": C(br.reshape(1, 8)),
        "e8": e8, "mdg": mask_diag, "mwn": mask_win, "identity": ident,
    }
    im = []
    for c in range(8):
        b, r = c // 4, c % 4
        pos = np.arange(r * TPC, (r + 1) * TPC)
        rcos, rsin = make_rope_tables(pos, 1.0)
        d = dict(shared)
        d["x"] = C(x[b, r * TPC:(r + 1) * TPC, :])
        d["bsel"] = np.array([[float(b)]], dtype=f32)
        d["rank"] = np.array([[r]], dtype=np.int32)
        d["rcos"] = rcos
        d["rsin"] = rsin
        im.append(d)
    r5 = _run(build_fused(), im)
    out = np.empty((2, S, D), dtype=f32)
    for c in range(8):
        b, r = c // 4, c % 4
        out[b, r * TPC:(r + 1) * TPC, :] = r5[c]["out"]
    return out



# version 05
# speedup vs baseline: 1.0370x; 1.0370x over previous
import numpy as np
import ml_dtypes
from contextlib import ExitStack

import concourse.bass as bass
import concourse.mybir as mybir
from concourse.bass_utils import run_bass_kernel_spmd

F32 = mybir.dt.float32
BF16 = mybir.dt.bfloat16
I32 = mybir.dt.int32
AF = mybir.ActivationFunctionType
ALU = mybir.AluOpType
AX = mybir.AxisListType

D = 4096
S = 4096
TPC = 1024
NEG = -1.0e30
EPS = 1e-5
ASCALE = 1.0 / np.sqrt(128.0)

ENGINES = ("sync", "scalar", "vector", "tensor", "gpsimd")



class H:
    __slots__ = ("eng", "kind", "val")

    def __init__(self, eng, kind, val):
        self.eng, self.kind, self.val = eng, kind, val


class Plan:

    def __init__(self, nc):
        self.nc = nc
        self.q = {e: [] for e in ENGINES}
        self.cnt = {}
        for e in ENGINES:
            self.cnt[(e, "c")] = 0
        self.settle_fn = None

    def op(self, eng, fn, deps=(), kind="c"):
        amt = 16 if (isinstance(kind, tuple) and kind[0] == "d") else 1
        key = (eng, kind)
        self.cnt[key] = self.cnt.get(key, 0) + amt
        h = H(eng, kind, self.cnt[key])
        self.q[eng].append((fn, tuple(d for d in deps if d is not None), kind))
        return h

    def fence_all(self):
        return [H(e, k, v) for (e, k), v in self.cnt.items() if v > 0]

    def barrier(self, eng, deps):
        self.q[eng].append((lambda e: None,
                            tuple(d for d in deps if d is not None), "c"))

    def raw(self, eng, fn, deps=()):
        self.q[eng].append((lambda e, f=fn: (f(e), None)[1],
                            tuple(d for d in deps if d is not None), "c"))

    def dma(self, eng, out, in_, deps=(), slot=None):
        if slot is None:
            to = out.tensor if hasattr(out, "tensor") else out
            if type(to).__name__ == "DRamTensorHandle":
                ti = in_.tensor if hasattr(in_, "tensor") else in_
                slot = "st_" + getattr(ti, "name", "anon")
            else:
                slot = getattr(to, "name", "anon")
        return self.op(eng, lambda e, o=out, i=in_: e.dma_start(out=o, in_=i),
                       deps, kind=("d", slot))

    def emit(self, ctx, block):
        sems = {}
        for (e, k), v in self.cnt.items():
            if v > 0:
                nm = f"sem_{e}_{k if isinstance(k, str) else 'd_' + k[1]}"
                sems[(e, k)] = ctx.enter_context(self.nc.semaphore(nm))

        def mk(e):
            def section(eng):
                waited = {}
                for fn, deps, kind in self.q[e]:
                    for dd in deps:
                        key = (dd.eng, dd.kind)
                        if waited.get(key, 0) >= dd.val:
                            continue
                        eng.wait_ge(sems[key], dd.val)
                        waited[key] = dd.val
                    inst = fn(eng)
                    if inst is not None:
                        if e == "tensor" and self.settle_fn is not None:
                            inst = self.settle_fn(eng)
                        inst.then_inc(
                            sems[(e, kind)],
                            16 if (isinstance(kind, tuple) and kind[0] == "d")
                            else 1)
            return section

        for e in ENGINES:
            if self.q[e]:
                getattr(block, e)(mk(e))
        return sems



def make_masks():
    i = np.arange(128)[:, None]
    j = np.arange(128)[None, :]
    mask_diag = np.where(j <= i, 0.0, NEG).astype(np.float32)
    mask_win = np.where(j > i, 0.0, NEG).astype(np.float32)
    return mask_diag, mask_win


def make_rope_tables(pos, scale):
    half = 64
    inv_freq = 10000.0 ** (-(np.arange(half, dtype=np.float64) / half))
    ang = pos[:, None].astype(np.float64) * inv_freq[None, :]
    cos = np.cos(ang).T
    sin = np.sin(ang).T
    cosf = np.concatenate([cos, cos], 0) * scale
    sinf = np.concatenate([sin, -sin], 0) * scale
    return cosf.astype(np.float32), sinf.astype(np.float32)



def alloc_stats(nc, ctx, tag):
    names = ["ssq", "ssum", "mean", "tmp", "vpe", "s0", "r0", "t2", "s1",
             "rstd", "nmr"]
    return {n: ctx.enter_context(nc.sbuf_tensor(f"{tag}_{n}", [128, 1], F32))
            for n in names}


def emit_ln(P, bufs, x_dram, xnt_sb, ident_sb, ident_h=None, extra_dep=()):
    ntiles = TPC // 128
    xs = bufs["xs"]
    st = bufs["stats"]
    xn_tm = bufs["xn_tm"]
    pt_ps = bufs["pt_ps"]
    ident = ident_sb
    last_xs = [None, None]
    last_tm = [None, None]
    last_pt = [None, None]
    prev_stats = None
    done = []
    for t in range(ntiles):
        bi = t % 2
        dep = list(extra_dep) + [last_xs[bi]]
        hld = P.dma("sync", xs[bi][:], x_dram[t * 128:(t + 1) * 128, :], dep)
        hsq = P.op("scalar", lambda e, b=bi: e.activation(
            xn_tm[b][:], xs[b][:], AF.Square, accum_out=st["ssq"][:]),
            [hld, prev_stats, last_tm[bi]])
        hsum = P.op("vector", lambda e, b=bi: e.reduce_sum(
            st["ssum"][:], xs[b][:], axis=AX.X), [hld, prev_stats])
        hmean = P.op("vector", lambda e: e.tensor_scalar(
            st["mean"][:], st["ssum"][:], 1.0 / D, None, ALU.mult), [hsum])
        htmp = P.op("vector", lambda e: e.tensor_scalar(
            st["tmp"][:], st["mean"][:], st["mean"][:], EPS,
            ALU.mult, ALU.subtract), [hmean])
        hvpe = P.op("vector", lambda e: e.tensor_scalar(
            st["vpe"][:], st["ssq"][:], 1.0 / D, st["tmp"][:],
            ALU.mult, ALU.subtract), [htmp, hsq])
        hs0 = P.op("scalar", lambda e: e.activation(
            st["s0"][:], st["vpe"][:], AF.Sqrt), [hvpe])
        hr0 = P.op("vector", lambda e: e.reciprocal(st["r0"][:], st["s0"][:]),
                   [hs0])
        ht2 = P.op("vector", lambda e: e.tensor_scalar(
            st["t2"][:], st["vpe"][:], st["r0"][:], None, ALU.mult),
            [hr0])
        hs1 = P.op("vector", lambda e: e.tensor_scalar(
            st["s1"][:], st["s0"][:], st["t2"][:], 0.5, ALU.add, ALU.mult),
            [ht2])
        hrstd = P.op("vector", lambda e: e.reciprocal(
            st["rstd"][:], st["s1"][:]), [hs1])
        hnmr = P.op("vector", lambda e: e.tensor_scalar(
            st["nmr"][:], st["mean"][:], st["rstd"][:], -1.0,
            ALU.mult, ALU.mult), [hrstd])
        hxn = P.op("scalar", lambda e, b=bi: e.activation(
            xn_tm[b][:], xs[b][:], AF.Identity, bias=st["nmr"][:],
            scale=st["rstd"][:]), [hnmr, last_tm[bi]])
        prev_stats = hxn
        last_xs[bi] = hxn
        for g in range(4):
            pi = (t * 4 + g) % 2

            def trg(e, b=bi, g=g, pi=pi):
                r = None
                for c in range(8):
                    kc = g * 8 + c
                    r = e.transpose(pt_ps[pi][:, c * 128:(c + 1) * 128],
                                    xn_tm[b][:, kc * 128:(kc + 1) * 128],
                                    ident[:])
                return r
            htr = P.op("tensor", trg, [hxn, last_pt[pi], ident_h])

            def evg(e, t=t, g=g, pi=pi):
                r = None
                for c in range(8):
                    kc = g * 8 + c
                    dst = xnt_sb[:, kc * TPC + t * 128: kc * TPC + (t + 1) * 128]
                    src = pt_ps[pi][:, c * 128:(c + 1) * 128]
                    r = e.tensor_copy(dst, src) if hasattr(e, "tensor_copy") \
                        else e.copy(dst, src)
                return r
            hev = P.op("vector" if g % 2 == 0 else "scalar", evg, [htr])
            last_pt[pi] = hev
            last_tm[bi] = hev
            done.append(hev)
    return done


def emit_proj_fm(P, bufs, wt_dram, out_dram, xnt_done, xnt_sb, n_out,
                 bias_sb=None, bias_h=None, scale=1.0, rope=None,
                 rope_dep=None, tag=""):
    nc = P.nc
    w_sb = bufs["w_fm"]
    ps = bufs["ps_fm"]
    ob = bufs["o_fm"]
    ra = bufs.get("rope_a")
    rb = bufs.get("rope_b")
    if "rope_war" not in bufs:
        bufs["rope_war"] = [None]
    last_w = bufs.setdefault("war_w_fm", [None, None])
    last_ps = bufs.setdefault("war_ps_fm", [None, None])
    last_ob = bufs.setdefault("war_o_fm", [None, None])
    single_w = len(w_sb) > 1 and (w_sb[0] is w_sb[1])
    nchunk = n_out // 128
    out_handles = []
    for oc in range(nchunk):
        wb = 0 if single_w else oc % 2
        hw = P.dma("gpsimd", w_sb[wb][:].rearrange("p (k c) -> p k c", k=32),
                   wt_dram.rearrange("(k p) o -> p k o", p=128)
                   [:, :, oc * 128:(oc + 1) * 128],
                   [last_w[wb]])
        tile_done = []
        for tt in range(TPC // 512):
            pi = (oc * (TPC // 512) + tt) % 2

            def mmg(e, wb=wb, tt=tt, pi=pi):
                r = None
                for kc in range(32):
                    r = e.matmul(ps[pi][:],
                                 w_sb[wb][:, kc * 128:(kc + 1) * 128],
                                 xnt_sb[:, kc * TPC + tt * 512:
                                        kc * TPC + (tt + 1) * 512],
                                 start=(kc == 0), stop=(kc == 31))
                return r
            hmm = P.op("tensor", mmg,
                       fences_list(xnt_done) + [hw, last_ps[pi]])
            if rope is None:
                bias_ap = (bias_sb[:, oc:oc + 1]
                           if bias_sb is not None else 0.0)
                deps = [hmm, last_ob[wb]]
                if bias_h is not None:
                    deps.append(bias_h)

                def ev(e, wb=wb, tt=tt, pi=pi, bias_ap=bias_ap):
                    return e.activation(
                        ob[wb][:, tt * 512:(tt + 1) * 512], ps[pi][:],
                        AF.Identity, bias=bias_ap, scale=scale)
                hev = P.op("scalar", ev, deps)
                last_ps[pi] = hev
                tile_done.append(hev)
            else:
                cos_sb, sin_sb = rope
                c_ap = cos_sb[:, tt * 512:(tt + 1) * 512]
                sL = sin_sb[0:64, tt * 512:(tt + 1) * 512]
                sH = sin_sb[64:128, tt * 512:(tt + 1) * 512]
                rw = bufs.get("rope_war")

                def r1(e, pi=pi, c_ap=c_ap):
                    return e.tensor_tensor(ra[:], ps[pi][:], c_ap, ALU.mult)
                h1 = P.op("vector", r1, [hmm, rw[0]] + list(rope_dep or []))

                def r2(e, pi=pi, sH=sH):
                    return e.tensor_tensor(rb[0:64, :], ps[pi][64:128, :], sH,
                                           ALU.mult)
                h2 = P.op("vector", r2, [hmm, rw[0]] + list(rope_dep or []))

                def r3(e, pi=pi, sL=sL):
                    return e.tensor_tensor(rb[64:128, :], ps[pi][0:64, :], sL,
                                           ALU.mult)
                h3 = P.op("vector", r3, [hmm, rw[0]] + list(rope_dep or []))

                def r4(e, wb=wb, tt=tt):
                    return e.tensor_tensor(ob[wb][:, tt * 512:(tt + 1) * 512],
                                           ra[:], rb[:], ALU.add)
                h4 = P.op("vector", r4, [h1, h2, h3, last_ob[wb]])
                rw[0] = h4
                last_ps[pi] = h4
                tile_done.append(h4)
        hst = P.dma("sync", out_dram[oc * 128:(oc + 1) * 128, :], ob[wb][:],
                    tile_done)
        last_ob[wb] = hst
        last_w[wb] = hmm
        out_handles.append(hst)
    return out_handles


def fences_list(handles):
    best = {}
    for h in handles:
        if h is None:
            continue
        k = (h.eng, h.kind)
        if k not in best or best[k].val < h.val:
            best[k] = h
    return list(best.values())


def emit_proj_tm(P, bufs, w_dram, out_dram, xnt_done, xnt_sb, n_out,
                 bias_row_sb=None, bias_h=None, ones_sb=None, ones_h=None,
                 bias_dram=None, tag="", W=512):
    w_sb = bufs["w_tm"]
    single_w = w_sb[0] is w_sb[1] if len(w_sb) > 1 else True
    ps = bufs["ps_tm"]
    vb = bufs["v_tm"]
    b_tm = bufs.get("b_tm")
    last_w = bufs.setdefault("war_w_tm", [None, None])
    last_ps = bufs.setdefault("war_ps_tm", [None, None])
    last_vb = bufs.setdefault("war_v_tm", [None, None])
    last_b = bufs.setdefault("war_b_tm", [None, None])
    out_handles = []
    nblk = n_out // W
    for obk in range(nblk):
        wb = 0 if single_w else obk % 2
        hw = P.dma("gpsimd", w_sb[wb][:].rearrange("p (k c) -> p k c", k=32),
                   w_dram.rearrange("(k p) o -> p k o", p=128)
                   [:, :, obk * W:(obk + 1) * W],
                   [last_w[wb]])
        hb = None
        if bias_dram is not None:
            hb = P.dma("sync", b_tm[obk % 2][0:1, 0:W],
                       bias_dram[0:1, obk * W:(obk + 1) * W],
                       [last_b[obk % 2]])
        for tt in range(TPC // 128):
            pi = (obk * (TPC // 128) + tt) % 2

            def mmg(e, wb=wb, tt=tt, pi=pi, obk=obk):
                r = None
                if bias_dram is not None:
                    r = e.matmul(ps[pi][:, 0:W], ones_sb[:],
                                 b_tm[obk % 2][0:1, 0:W],
                                 start=True, stop=False)
                elif bias_row_sb is not None:
                    r = e.matmul(ps[pi][:, 0:W], ones_sb[:],
                                 bias_row_sb[:, obk * W:(obk + 1) * W],
                                 start=True, stop=False)
                has_bias = bias_dram is not None or bias_row_sb is not None
                for kc in range(32):
                    r = e.matmul(ps[pi][:, 0:W],
                                 xnt_sb[:, kc * TPC + tt * 128:
                                        kc * TPC + (tt + 1) * 128],
                                 w_sb[wb][:, kc * W:(kc + 1) * W],
                                 start=(kc == 0 and not has_bias),
                                 stop=(kc == 31))
                return r
            deps = fences_list(xnt_done) + [hw, last_ps[pi], hb]
            if bias_h is not None:
                deps += [bias_h, ones_h]
            if bias_dram is not None:
                deps += [ones_h]
            hmm = P.op("tensor", mmg, deps)
            bj = tt % 2

            def ev(e, pi=pi, bj=bj):
                return e.tensor_copy(vb[bj][:, 0:W], ps[pi][:, 0:W]) \
                    if hasattr(e, "tensor_copy") \
                    else e.copy(vb[bj][:, 0:W], ps[pi][:, 0:W])
            hev = P.op("vector" if tt % 2 == 0 else "scalar", ev,
                       [hmm, last_vb[bj]])
            last_ps[pi] = hev
            hst = P.dma("sync",
                        out_dram[tt * 128:(tt + 1) * 128,
                                 obk * W:(obk + 1) * W],
                        vb[bj][:, 0:W], [hev])
            last_vb[bj] = hst
            out_handles.append(hst)
        last_w[wb] = hmm
        if hb is not None:
            last_b[obk % 2] = hmm
    return out_handles





def alloc_attn_bufs(nc, ctx, tag=""):
    sb = lambda n, s, d: ctx.enter_context(nc.sbuf_tensor(tag + n, s, d))
    psum = lambda n, s, d: ctx.enter_context(nc.psum_tensor(tag + n, s, d))
    names = ["negm", "negm2", "dtmp", "alpha", "zz", "zp", "rz", "negmsc"]
    fl = {n: sb("fl_" + n, [128, 1], F32) for n in names}
    return {
        "fl": fl,
        "oacc": sb("oacc", [128, 128], F32),
        "p_sb": [sb("p0", [128, 2048], BF16), sb("p1", [128, 2048], BF16)],
        "pt_sb": [sb("pt0", [128, 1024], BF16), sb("pt1", [128, 1024], BF16)],
        "ohead": [sb("oh0", [128, 32 * 128], BF16),
                  sb("oh1", [128, 32 * 128], BF16)],
        "s_ps": psum("s_ps", [128, 2048], F32),
        "pt_ps": [psum("ptp0", [128, 1024], BF16),
                  psum("ptp1", [128, 1024], BF16)],
        "o_ps": psum("o_ps", [128, 512], F32),
        "war": {},
    }


def _s_matmuls(e, ab, q_sb, k_sb, j, kstart, p0, chunks):
    r = None
    for (c0, cw) in chunks:
        r = e.matmul(ab["s_ps"][:, c0:c0 + cw],
                     q_sb[:, j * 128:(j + 1) * 128],
                     k_sb[:, kstart + p0 + c0: kstart + p0 + c0 + cw],
                     start=True, stop=True)
    return r


def _tr_group(e, ab, pp, gp, g0, nch, ident_sb):
    r = None
    for c in range(nch):
        r = e.transpose(ab["pt_ps"][gp][:, c * 128:(c + 1) * 128],
                        ab["p_sb"][pp][:, (g0 + c) * 128:(g0 + c + 1) * 128],
                        ident_sb[:])
    return r


def _av_group(e, ab, v_sb, gp, g0, nch, kbase_c, first, lastg):
    r = None
    for c in range(nch):
        kc = kbase_c + g0 + c
        r = e.matmul(ab["o_ps"][:, 0:128],
                     ab["pt_sb"][gp][:, c * 128:(c + 1) * 128],
                     v_sb[:, kc * 128:(kc + 1) * 128],
                     start=(first and c == 0),
                     stop=(lastg and c == nch - 1))
    return r


def emit_attn_head(P, ab, h, n_jtiles, q_sb, k_sb, v_sb, kv_dep, q_dep,
                   o_dram_ap, mask_diag_sb, mask_win_sb, mask_h, ident_sb,
                   ident_h, windowed, sscale=ASCALE, mask_h2=None,
                   transpose_out=False, o_store_eng="sync", o_slot=None):
    fl = ab["fl"]
    oacc = ab["oacc"]
    war = ab["war"]
    hp = h % 2
    ohead = ab["ohead"][hp]
    qk_reads = []
    v_reads = []
    for j in range(n_jtiles):
        Q0 = j * 128
        kend = Q0 + 128
        kstart = max(0, kend - 1152) if windowed else 0
        kn = kend - kstart
        pbounds = list(range(0, kn, 2048)) + [kn]
        npass = len(pbounds) - 1
        hZ = None
        hO = None
        halpha = None
        for ip in range(npass):
            first = ip == 0
            lastp = ip == npass - 1
            p0, p1 = pbounds[ip], pbounds[ip + 1]
            w = p1 - p0
            chunks = []
            c0 = 0
            while c0 < w:
                cw = min(512, w - c0)
                chunks.append((c0, cw))
                c0 += cw
            hs = P.op("tensor",
                      (lambda e, j=j, ks=kstart, p0=p0, ch=tuple(chunks):
                       _s_matmuls(e, ab, q_sb, k_sb, j, ks, p0, ch)),
                      [war.get("s_ps_free"), q_dep, ident_h] + list(kv_dep))
            qk_reads.append(hs)
            deps_red = [hs]
            if lastp:
                hmd = P.op("vector", lambda e, w=w: e.tensor_tensor(
                    ab["s_ps"][:, w - 128:w], ab["s_ps"][:, w - 128:w],
                    mask_diag_sb[:], ALU.add), [hs, mask_h])
                deps_red.append(hmd)
            if first and windowed and kstart > 0:
                hmw = P.op("vector", lambda e: e.tensor_tensor(
                    ab["s_ps"][:, 0:128], ab["s_ps"][:, 0:128],
                    mask_win_sb[:], ALU.add), [hs, mask_h2 or mask_h])
                deps_red.append(hmw)
            hred = P.op("vector", lambda e, w=w: e.reduce_max(
                fl["negm2"][:], ab["s_ps"][:, 0:w], axis=AX.X, negate=True),
                deps_red + [war.get("negm2_free")])
            if first:
                halpha = None
                hngm = P.op("vector", lambda e: e.tensor_copy(
                    fl["negm"][:], fl["negm2"][:]),
                    [hred, war.get("negm_free")])
            else:
                hdm = P.op("vector", lambda e: e.tensor_tensor(
                    fl["dtmp"][:], fl["negm"][:], fl["negm2"][:], ALU.min),
                    [hred, war.get("dtmp_free")])
                hdf = P.op("vector", lambda e: e.tensor_tensor(
                    fl["negm2"][:], fl["dtmp"][:], fl["negm"][:],
                    ALU.subtract), [hdm])
                halpha = P.op("scalar", lambda e: e.activation(
                    fl["alpha"][:], fl["negm2"][:], AF.Exp, scale=sscale),
                    [hdf, war.get("alpha_free")])
                hngm = P.op("vector", lambda e: e.tensor_copy(
                    fl["negm"][:], fl["dtmp"][:]),
                    [hdf, war.get("negm_free")])
                war["dtmp_free"] = hngm
            war["negm2_free"] = halpha or hngm
            pp = war.setdefault("pp_ctr", [0])[0] % 2
            war["pp_ctr"][0] += 1
            hnsc = P.op("vector", lambda e: e.tensor_scalar(
                fl["negmsc"][:], fl["negm"][:], sscale, None, ALU.mult),
                [hngm, war.get("negmsc_free")])
            hexp = P.op("scalar", lambda e, pp=pp, w=w: e.activation(
                ab["p_sb"][pp][:, 0:w], ab["s_ps"][:, 0:w], AF.Exp,
                scale=sscale, bias=fl["negmsc"][:], accum_out=fl["zp"][:]),
                [hnsc, halpha, war.get(("p_free", pp)), war.get("zp_free")])
            war["negmsc_free"] = hexp
            war["s_ps_free"] = hexp
            war["negm_free"] = hexp
            if first:
                hZ = P.op("vector", lambda e: e.tensor_copy(
                    fl["zz"][:], fl["zp"][:]), [hexp, war.get("zz_free")])
            else:
                hZ = P.op("vector", lambda e: e.scalar_tensor_tensor(
                    fl["zz"][:], fl["zz"][:], fl["alpha"][:], fl["zp"][:],
                    ALU.mult, ALU.add), [hexp, hZ, halpha])
            war["zp_free"] = hZ
            nkc = w // 128
            kbase_c = (kstart + p0) // 128
            hav = None
            htr_last = None
            first_av = True
            for g0 in range(0, nkc, 8):
                nch = min(8, nkc - g0)
                lastg = g0 + 8 >= nkc
                gp = war.setdefault("gp_ctr", [0])[0] % 2
                war["gp_ctr"][0] += 1
                htr = P.op("tensor",
                           (lambda e, pp=pp, gp=gp, g0=g0, nch=nch:
                            _tr_group(e, ab, pp, gp, g0, nch, ident_sb)),
                           [hexp, ident_h, war.get(("ptps_free", gp))])
                htr_last = htr
                hev = P.op("vector" if (g0 // 8) % 2 == 0 else "scalar",
                           (lambda e, gp=gp, nch=nch:
                            (e.tensor_copy if hasattr(e, "tensor_copy")
                             else e.copy)(ab["pt_sb"][gp][:, 0:nch * 128],
                                          ab["pt_ps"][gp][:, 0:nch * 128])),
                           [htr, war.get(("ptsb_free", gp))])
                war[("ptps_free", gp)] = hev
                hav = P.op("tensor",
                           (lambda e, gp=gp, g0=g0, nch=nch, kb=kbase_c,
                            fa=first_av, lg=lastg:
                            _av_group(e, ab, v_sb, gp, g0, nch, kb, fa, lg)),
                           [hev, war.get("o_ps_free")] + list(kv_dep))
                war[("ptsb_free", gp)] = hav
                v_reads.append(hav)
                first_av = False
            war[("p_free", pp)] = htr_last
            if first:
                hO = P.op("vector", lambda e: e.tensor_copy(
                    oacc[:], ab["o_ps"][:, 0:128]),
                    [hav, war.get("oacc_free")])
            else:
                hO = P.op("vector", lambda e: e.scalar_tensor_tensor(
                    oacc[:], oacc[:], fl["alpha"][:], ab["o_ps"][:, 0:128],
                    ALU.mult, ALU.add), [hav, hO, halpha])
                war["alpha_free"] = hO
            war["o_ps_free"] = hO
        hrz = P.op("vector", lambda e: e.reciprocal(
            fl["rz"][:], fl["zz"][:]), [hZ, war.get("rz_free")])
        war["zz_free"] = hrz
        if not transpose_out:
            hfin = P.op("vector", lambda e, j=j: e.tensor_scalar(
                ohead[:, j * 128:(j + 1) * 128], oacc[:], fl["rz"][:], None,
                ALU.mult), [hrz, hO, war.get(("ohead_free", hp))])
        else:
            hfn0 = P.op("vector", lambda e: e.tensor_scalar(
                ab["o_tmp"][:], oacc[:], fl["rz"][:], None, ALU.mult),
                [hrz, hO, war.get("o_tmp_free")])
            hotr = P.op("tensor", lambda e: e.transpose(
                ab["otr_ps"][:], ab["o_tmp"][:], ident_sb[:]),
                [hfn0, ident_h, war.get("otr_free")])
            war["o_tmp_free"] = hotr
            hfin = P.op("scalar", lambda e, j=j: e.copy(
                ohead[:, j * 128:(j + 1) * 128], ab["otr_ps"][:]),
                [hotr, war.get(("ohead_free", hp))])
            war["otr_free"] = hfin
        war["rz_free"] = hfin
        war["oacc_free"] = hfin
        war.setdefault("fins", []).append(hfin)
    hdma = P.dma(o_store_eng, o_dram_ap,
                 ohead[:].rearrange("p (j c) -> p j c", j=n_jtiles),
                 war["fins"], slot=o_slot)
    war["fins"] = []
    war[("ohead_free", hp)] = hdma
    return qk_reads, v_reads, hdma



def build_launch1():
    nc = bass.Bass("TRN2", target_bir_lowering=False)
    x = nc.dram_tensor("x", [TPC, D], F32, kind="ExternalInput").ap()
    wqt = nc.dram_tensor("wqt", [D, D], F32, kind="ExternalInput").ap()
    wkt = nc.dram_tensor("wkt", [D, D], F32, kind="ExternalInput").ap()
    wvt = nc.dram_tensor("wvt", [D, D], F32, kind="ExternalInput").ap()
    bqt = nc.dram_tensor("bqt", [128, 32], F32, kind="ExternalInput").ap()
    bkt = nc.dram_tensor("bkt", [128, 32], F32, kind="ExternalInput").ap()
    bvr = nc.dram_tensor("bvr", [1, D], F32, kind="ExternalInput").ap()
    identity = nc.dram_tensor("identity", [128, 128], F32,
                              kind="ExternalInput").ap()
    qt = nc.dram_tensor("qt", [D, TPC], BF16, kind="ExternalOutput").ap()
    kt = nc.dram_tensor("kt", [D, TPC], BF16, kind="ExternalOutput").ap()
    v = nc.dram_tensor("v", [TPC, D], BF16, kind="ExternalOutput").ap()
    xnt = nc.dram_tensor("xnt", [D, TPC], BF16, kind="ExternalOutput").ap()

    with ExitStack() as ctx:
        sb = lambda n, s, d: ctx.enter_context(nc.sbuf_tensor(n, s, d))
        psum = lambda n, s, d: ctx.enter_context(nc.psum_tensor(n, s, d))
        bufs = {
            "xs": [sb("xs0", [128, D], F32), sb("xs1", [128, D], F32)],
            "stats": alloc_stats(nc, ctx, "st"),
            "xn_tm": [sb("xntm0", [128, D], BF16), sb("xntm1", [128, D], BF16)],
            "pt_ps": [psum("ptps0", [128, 1024], BF16),
                      psum("ptps1", [128, 1024], BF16)],
            "w_fm": [sb("wfm0", [128, 32 * 128], BF16),
                     sb("wfm1", [128, 32 * 128], BF16)],
            "ps_fm": [psum("psfm0", [128, 512], F32),
                      psum("psfm1", [128, 512], F32)],
            "o_fm": [sb("ofm0", [128, TPC], BF16), sb("ofm1", [128, TPC], BF16)],
            "w_tm": [sb("wtm0", [128, 32 * 512], BF16)] * 2,
            "ps_tm": [psum("pstm0", [128, 512], F32),
                      psum("pstm1", [128, 512], F32)],
            "v_tm": [sb("vtm0", [128, 512], BF16), sb("vtm1", [128, 512], BF16)],
        }
        xnt_sb = sb("xnt_sb", [128, 32 * TPC], BF16)
        ident_sb = sb("ident", [128, 128], BF16)
        bq_sb = sb("bq_sb", [128, 32], F32)
        bk_sb = sb("bk_sb", [128, 32], F32)
        bv_sb = sb("bv_sb", [1, D], F32)
        ones_sb = sb("ones_sb", [1, 128], F32)

        dscr = psum("dscr", [128, 1024], BF16)
        P = Plan(nc)
        P.settle_fn = lambda e: e.transpose(dscr[:, 0:128], ident_sb[:],
                                            ident_sb[:])
        hid = P.dma("gpsimd", ident_sb[:], identity)
        hbq = P.dma("sync", bq_sb[:], bqt)
        hbk = P.dma("sync", bk_sb[:], bkt)
        hbv = P.dma("sync", bv_sb[:], bvr)
        hones = P.op("vector", lambda e: e.memset(ones_sb[:], 1.0))

        ln_done = emit_ln(P, bufs, x, xnt_sb, ident_sb, ident_h=hid)
        lnf = fences_list(ln_done)
        hxnt = P.dma("sync", xnt.rearrange("(k p) t -> p k t", p=128),
                     xnt_sb[:].rearrange("p (k t) -> p k t", k=32), lnf)
        hq = emit_proj_fm(P, bufs, wqt, qt, lnf, xnt_sb, D,
                          bias_sb=bq_sb, bias_h=hbq, scale=1.0, tag="q")
        hk = emit_proj_fm(P, bufs, wkt, kt, lnf, xnt_sb, D,
                          bias_sb=bk_sb, bias_h=hbk, scale=1.0, tag="k")
        hv = emit_proj_tm(P, bufs, wvt, v, lnf, xnt_sb, D,
                          bias_row_sb=bv_sb, bias_h=hbv, ones_sb=ones_sb,
                          ones_h=hones, tag="v")
        P.barrier("sync", fences_list([hxnt] + hq + hk + hv))
        block = ctx.enter_context(nc.Block())
        P.emit(ctx, block)
    return nc



def _run(nc, in_maps):
    return run_bass_kernel_spmd(nc, in_maps, core_ids=list(range(8))).results


def kernel_unfused(L0_attn_x, L0_attn_Wq, L0_attn_bq, L0_attn_Wk, L0_attn_bk,
           L0_attn_Wv, L0_attn_bv, L0_attn_Wo, L0_attn_bo, L0_ffn_weight,
           L0_ffn_bias, L0_ffn_lora_A, L0_ffn_lora_B,
           L0_ffn_lora_start_indices, L1_attn_wq, L1_attn_wk, L1_attn_wv,
           L1_attn_wo, L1_ffn_A, L1_ffn_B, L1_ffn_Wr, L1_ffn_br):
    f32 = np.float32
    x = np.asarray(L0_attn_x, dtype=f32)
    Wq = np.asarray(L0_attn_Wq, dtype=f32); bq = np.asarray(L0_attn_bq, dtype=f32)
    Wk = np.asarray(L0_attn_Wk, dtype=f32); bk = np.asarray(L0_attn_bk, dtype=f32)
    Wv = np.asarray(L0_attn_Wv, dtype=f32); bv = np.asarray(L0_attn_bv, dtype=f32)
    Wo = np.asarray(L0_attn_Wo, dtype=f32); bo = np.asarray(L0_attn_bo, dtype=f32)
    Wf = np.asarray(L0_ffn_weight, dtype=f32)
    bfn = np.asarray(L0_ffn_bias, dtype=f32)
    lA = np.asarray(L0_ffn_lora_A, dtype=f32)
    lB = np.asarray(L0_ffn_lora_B, dtype=f32)
    lsi = np.asarray(L0_ffn_lora_start_indices).astype(np.int32).reshape(1, 4)
    wq1 = np.asarray(L1_attn_wq, dtype=f32)
    wk1 = np.asarray(L1_attn_wk, dtype=f32)
    wv1 = np.asarray(L1_attn_wv, dtype=f32)
    wo1 = np.asarray(L1_attn_wo, dtype=f32)
    A1 = np.asarray(L1_ffn_A, dtype=f32)
    B1 = np.asarray(L1_ffn_B, dtype=f32)
    Wr = np.asarray(L1_ffn_Wr, dtype=f32)
    br = np.asarray(L1_ffn_br, dtype=f32)

    mask_diag, mask_win = make_masks()
    ident = np.eye(128, dtype=f32)
    e8 = np.zeros((8, 128), dtype=f32)
    for h in range(8):
        e8[h, h * 16:(h + 1) * 16] = 1.0
    C = lambda a: np.ascontiguousarray(a)

    im = []
    for c in range(8):
        b, r = c // 4, c % 4
        im.append({"x": C(x[b, r * TPC:(r + 1) * TPC, :]),
                   "wqt": C(Wq.T), "wkt": C(Wk.T), "wvt": C(Wv.T),
                   "bqt": C(bq.reshape(32, 128).T),
                   "bkt": C(bk.reshape(32, 128).T),
                   "bvr": C(bv.reshape(1, -1)), "identity": ident})
    r1 = _run(build_launch1(), im)
    QT = [np.concatenate([r1[4 * b + r]["qt"] for r in range(4)], 1)
          for b in range(2)]
    KT = [np.concatenate([r1[4 * b + r]["kt"] for r in range(4)], 1)
          for b in range(2)]
    V = [np.concatenate([r1[4 * b + r]["v"] for r in range(4)], 0)
         for b in range(2)]

    im = []
    for c in range(8):
        b, r = c // 4, c % 4
        im.append({"q": C(QT[b][r * TPC:(r + 1) * TPC, :]),
                   "k": C(KT[b][r * TPC:(r + 1) * TPC, :]),
                   "v": C(V[b][:, r * TPC:(r + 1) * TPC]),
                   "mdg": mask_diag, "identity": ident})
    r2 = _run(build_launch2(), im)
    O = [np.concatenate([r2[4 * b + r]["o"] for r in range(4)], 1)
         for b in range(2)]

    im = []
    for c in range(8):
        b, r = c // 4, c % 4
        pos = np.arange(r * TPC, (r + 1) * TPC)
        rcos, rsin = make_rope_tables(pos, 1.0)
        im.append({
            "x": C(x[b, r * TPC:(r + 1) * TPC, :]),
            "xnt": r1[c]["xnt"],
            "ot": C(O[b][r * TPC:(r + 1) * TPC, :].T),
            "wot": C(Wo.T), "bor": C(bo.reshape(1, -1)),
            "wft": C(Wf.T), "bfr": C(bfn.reshape(1, -1)),
            "acat": C(lA.transpose(2, 0, 1).reshape(D, 48)),
            "bcat": C(lB.transpose(0, 2, 1).reshape(48, D)),
            "lsi": lsi, "bsel": np.array([[float(b)]], dtype=f32),
            "wq1": wq1, "wk1": wk1, "wv1": wv1,
            "rcos": rcos, "rsin": rsin, "identity": ident})
    r3 = _run(build_launch3(), im)
    Q1 = [np.concatenate([r3[4 * b + r]["q1t"] for r in range(4)], 1)
          for b in range(2)]
    K1 = [np.concatenate([r3[4 * b + r]["k1t"] for r in range(4)], 1)
          for b in range(2)]
    V1 = [np.concatenate([r3[4 * b + r]["v1"] for r in range(4)], 0)
          for b in range(2)]

    im = []
    for c in range(8):
        b, r = c // 4, c % 4
        im.append({"q": C(Q1[b][r * TPC:(r + 1) * TPC, :]),
                   "k": C(K1[b][r * 256:(r + 1) * 256, :]),
                   "v": C(V1[b][:, r * 256:(r + 1) * 256]),
                   "mdg": mask_diag, "mwn": mask_win, "identity": ident})
    r4 = _run(build_launch4(), im)
    O1 = [np.concatenate([r4[4 * b + r]["o"] for r in range(4)], 1)
          for b in range(2)]

    im = []
    for c in range(8):
        b, r = c // 4, c % 4
        im.append({
            "x1": r3[c]["x1"],
            "xn1t": r3[c]["xn1t"],
            "o1t": C(O1[b][r * TPC:(r + 1) * TPC, :].T),
            "wo1": wo1,
            "a1cat": C(A1.transpose(2, 0, 1).reshape(D, 128)),
            "b1cat": C(B1.transpose(0, 2, 1).reshape(128, D)),
            "wrt": C(Wr.T), "brr": C(br.reshape(1, 8)),
            "e8": e8, "identity": ident})
    r5 = _run(build_launch5(), im)

    out = np.empty((2, S, D), dtype=f32)
    for c in range(8):
        b, r = c // 4, c % 4
        out[b, r * TPC:(r + 1) * TPC, :] = r5[c]["out"]
    return out


def build_launch2():
    nc = bass.Bass("TRN2", target_bir_lowering=False)
    q = nc.dram_tensor("q", [8 * 128, S], BF16, kind="ExternalInput").ap()
    k = nc.dram_tensor("k", [8 * 128, S], BF16, kind="ExternalInput").ap()
    v = nc.dram_tensor("v", [S, 8 * 128], BF16, kind="ExternalInput").ap()
    mdg = nc.dram_tensor("mdg", [128, 128], F32, kind="ExternalInput").ap()
    identity = nc.dram_tensor("identity", [128, 128], F32,
                              kind="ExternalInput").ap()
    o = nc.dram_tensor("o", [S, 8 * 128], BF16, kind="ExternalOutput").ap()

    with ExitStack() as ctx:
        sb = lambda n, s, d: ctx.enter_context(nc.sbuf_tensor(n, s, d))
        ab = alloc_attn_bufs(nc, ctx)
        q_sb = [sb("q0", [128, S], BF16), sb("q1", [128, S], BF16)]
        k_sb = [sb("k0", [128, S], BF16), sb("k1", [128, S], BF16)]
        v_sb = [sb("v0", [128, 32 * 128], BF16), sb("v1", [128, 32 * 128], BF16)]
        md_sb = sb("md", [128, 128], F32)
        ident_sb = sb("ident", [128, 128], BF16)
        dscr = ctx.enter_context(nc.psum_tensor("dscr", [128, 1024], BF16))

        P = Plan(nc)
        P.settle_fn = lambda e: e.transpose(dscr[:, 0:128], ident_sb[:],
                                            ident_sb[:])
        hid = P.dma("gpsimd", ident_sb[:], identity)
        hmd = P.dma("gpsimd", md_sb[:], mdg)
        qk_free = [None, None]
        v_free = [None, None]
        odma = []
        for h in range(8):
            hb = h % 2
            hq = P.dma("sync", q_sb[hb][:], q[h * 128:(h + 1) * 128, :],
                       [qk_free[hb]])
            hk = P.dma("sync", k_sb[hb][:], k[h * 128:(h + 1) * 128, :],
                       [qk_free[hb]])
            hv = P.dma("sync", v_sb[hb][:].rearrange("p (kc c) -> p kc c", kc=32),
                       v.rearrange("(kc p) c -> p kc c", p=128)
                       [:, :, h * 128:(h + 1) * 128], [v_free[hb]])
            o_ap = o.rearrange("(j q) c -> q j c", q=128)[:, :, h * 128:(h + 1) * 128]
            qk_reads, v_reads, hdma = emit_attn_head(
                P, ab, h, 32, q_sb[hb], k_sb[hb], v_sb[hb],
                [hk, hv], hq, o_ap, md_sb, None, hmd,
                ident_sb, hid, windowed=False)
            qk_free[hb] = fences_list(qk_reads)[0] if qk_reads else None
            vf = fences_list(v_reads)
            v_free[hb] = vf[0] if vf else None
            odma.append(hdma)
        P.barrier("sync", fences_list(odma))
        block = ctx.enter_context(nc.Block())
        P.emit(ctx, block)
    return nc



def build_launch4():
    nc = bass.Bass("TRN2", target_bir_lowering=False)
    q = nc.dram_tensor("q", [8 * 128, S], BF16, kind="ExternalInput").ap()
    k = nc.dram_tensor("k", [2 * 128, S], BF16, kind="ExternalInput").ap()
    v = nc.dram_tensor("v", [S, 2 * 128], BF16, kind="ExternalInput").ap()
    mdg = nc.dram_tensor("mdg", [128, 128], F32, kind="ExternalInput").ap()
    mwn = nc.dram_tensor("mwn", [128, 128], F32, kind="ExternalInput").ap()
    identity = nc.dram_tensor("identity", [128, 128], F32,
                              kind="ExternalInput").ap()
    o = nc.dram_tensor("o", [S, 8 * 128], BF16, kind="ExternalOutput").ap()

    with ExitStack() as ctx:
        sb = lambda n, s, d: ctx.enter_context(nc.sbuf_tensor(n, s, d))
        ab = alloc_attn_bufs(nc, ctx)
        q_sb = [sb("q0", [128, S], BF16), sb("q1", [128, S], BF16)]
        k_sb = [sb("k0", [128, S], BF16), sb("k1", [128, S], BF16)]
        v_sb = [sb("v0", [128, 32 * 128], BF16), sb("v1", [128, 32 * 128], BF16)]
        md_sb = sb("md", [128, 128], F32)
        mw_sb = sb("mw", [128, 128], F32)
        ident_sb = sb("ident", [128, 128], BF16)
        dscr = ctx.enter_context(nc.psum_tensor("dscr", [128, 1024], BF16))

        P = Plan(nc)
        P.settle_fn = lambda e: e.transpose(dscr[:, 0:128], ident_sb[:],
                                            ident_sb[:])
        hid = P.dma("gpsimd", ident_sb[:], identity)
        hmd = P.dma("gpsimd", md_sb[:], mdg)
        hmw = P.dma("gpsimd", mw_sb[:], mwn)
        hks, hvs = [], []
        for kv in range(2):
            hks.append(P.dma("sync", k_sb[kv][:], k[kv * 128:(kv + 1) * 128, :]))
            hvs.append(P.dma(
                "sync", v_sb[kv][:].rearrange("p (kc c) -> p kc c", kc=32),
                v.rearrange("(kc p) c -> p kc c", p=128)
                [:, :, kv * 128:(kv + 1) * 128]))
        qbufs = [sb("qb0", [128, S], BF16), sb("qb1", [128, S], BF16)]
        q_free = [None, None]
        odma = []
        for h in range(8):
            hb = h % 2
            kv = h // 4
            hq = P.dma("sync", qbufs[hb][:], q[h * 128:(h + 1) * 128, :],
                       [q_free[hb]])
            o_ap = o.rearrange("(j q) c -> q j c", q=128)[:, :, h * 128:(h + 1) * 128]
            qk_reads, v_reads, hdma = emit_attn_head(
                P, ab, h, 32, qbufs[hb], k_sb[kv], v_sb[kv],
                [hks[kv], hvs[kv]], hq, o_ap, md_sb, mw_sb,
                hmd, ident_sb, hid, windowed=True, mask_h2=hmw)
            qf = fences_list(qk_reads)
            q_free[hb] = qf[0] if qf else None
            odma.append(hdma)
        P.barrier("sync", fences_list(odma))
        block = ctx.enter_context(nc.Block())
        P.emit(ctx, block)
    return nc


def build_launch3():
    nc = bass.Bass("TRN2", target_bir_lowering=False)
    di = lambda n, s, d: nc.dram_tensor(n, s, d, kind="ExternalInput").ap()
    do = lambda n, s, d: nc.dram_tensor(n, s, d, kind="ExternalOutput").ap()
    x = di("x", [TPC, D], F32)
    xnt = di("xnt", [D, TPC], BF16)
    ot = di("ot", [D, TPC], BF16)
    wot = di("wot", [D, D], F32)
    bor = di("bor", [1, D], F32)
    wft = di("wft", [D, D], F32)
    bfr = di("bfr", [1, D], F32)
    acat = di("acat", [D, 48], F32)
    bcat = di("bcat", [48, D], F32)
    lsi = di("lsi", [1, 4], I32)
    bsel = di("bsel", [1, 1], F32)
    wq1 = di("wq1", [D, D], F32)
    wk1 = di("wk1", [D, 1024], F32)
    wv1 = di("wv1", [D, 1024], F32)
    rcos = di("rcos", [128, TPC], F32)
    rsin = di("rsin", [128, TPC], F32)
    identity = di("identity", [128, 128], F32)
    x1 = do("x1", [TPC, D], F32)
    q1t = do("q1t", [D, TPC], BF16)
    k1t = do("k1t", [1024, TPC], BF16)
    v1 = do("v1", [TPC, 1024], BF16)
    xn1t = do("xn1t", [D, TPC], BF16)
    wdram = nc.dram_tensor("wdram", [1, 48], F32)

    with ExitStack() as ctx:
        sb = lambda n, s, d: ctx.enter_context(nc.sbuf_tensor(n, s, d))
        psum = lambda n, s, d: ctx.enter_context(nc.psum_tensor(n, s, d))
        xn1t_sb = sb("xn1t_sb", [128, 32 * TPC], BF16)
        xnt_tt = sb("xnt_tt", [128, 32 * 128], BF16)
        ot_tt = sb("ot_tt", [128, 32 * 128], BF16)
        wblk = sb("wblk", [128, 32 * 512], BF16)
        g_blk = sb("g_blk", [128, D], F32)
        xs0 = sb("xs0", [128, D], F32)
        xs1 = g_blk
        xntm0 = xnt_tt
        xntm1 = ot_tt
        rcos_sb = sb("rcos_sb", [128, TPC], F32)
        rsin_sb = sb("rsin_sb", [128, TPC], F32)
        wfm = sb("wfm", [128, 32 * 128], BF16)
        ofm0 = sb("ofm0", [128, TPC], BF16)
        ofm1 = sb("ofm1", [128, TPC], BF16)
        rhs0 = sb("rhs0", [128, 512], BF16)
        rhs1 = sb("rhs1", [128, 512], BF16)
        xt0 = sb("xt0", [128, 512], F32)
        xt1 = sb("xt1", [128, 512], F32)
        tmp0 = sb("tmp0", [128, 512], F32)
        tmp1 = sb("tmp1", [128, 512], F32)
        x1t0 = sb("x1t0", [128, 512], F32)
        x1t1 = sb("x1t1", [128, 512], F32)
        acat_sb = sb("acat_sb", [128, 32 * 48], BF16)
        bcat_sb = sb("bcat_sb", [48, D], BF16)
        rt_sb = sb("rt_sb", [48, TPC], BF16)
        w48_sb = sb("w48_sb", [48, 1], F32)
        bo_t = [sb("bo_t0", [1, 512], F32), sb("bo_t1", [1, 512], F32)]
        bf_t = [sb("bf_t0", [1, 512], F32), sb("bf_t1", [1, 512], F32)]
        ones_sb = sb("ones_sb", [1, 128], F32)
        ones16 = sb("ones16", [1, 16], F32)
        lsii = sb("lsii", [1, 4], I32)
        lsif = sb("lsif", [1, 4], F32)
        bsel_sb = sb("bsel_sb", [1, 1], F32)
        cnt_sb = sb("cnt_sb", [1, 1], F32)
        uc_sb = sb("uc_sb", [1, 1], F32)
        gate_sb = sb("gate_sb", [1, 1], F32)
        wa_sb = sb("wa_sb", [1, 3], F32)
        wfree = sb("wfree", [1, 48], F32)
        ident_sb = sb("ident_sb", [128, 128], BF16)
        stats = alloc_stats(nc, ctx, "st")
        ps_a = psum("ps_a", [128, 512], F32)
        ps_b = psum("ps_b", [128, 512], F32)
        ps_rt = psum("ps_rt", [48, 512], F32)
        ptps0 = psum("ptps0", [128, 1024], BF16)
        ptps1 = psum("ptps1", [128, 1024], BF16)
        ps_c = psum("ps_c", [128, 512], F32)
        ps_d = psum("ps_d", [128, 512], F32)
        dscr = psum("dscr", [128, 1024], BF16)

        P = Plan(nc)
        P.settle_fn = lambda e: e.transpose(dscr[:, 0:128], ident_sb[:],
                                            ident_sb[:])
        hid = P.dma("gpsimd", ident_sb[:], identity)

        hcos = P.dma("sync", rcos_sb[:], rcos)
        hsin = P.dma("sync", rsin_sb[:], rsin)
        hones = P.op("vector", lambda e: e.memset(ones_sb[:], 1.0))
        hones16 = P.op("vector", lambda e: e.memset(ones16[:], 1.0))
        hacat = P.dma("gpsimd", acat_sb[:].rearrange("p (k c) -> p k c", k=32),
                      acat.rearrange("(k p) c -> p k c", p=128))
        hbcat = P.dma("gpsimd", bcat_sb[:], bcat)

        hlsi = P.dma("sync", lsii[:], lsi)
        hbs = P.dma("sync", bsel_sb[:], bsel)
        hc0 = P.op("vector", lambda e: e.tensor_copy(lsif[:], lsii[:]), [hlsi])
        hc1 = P.op("vector", lambda e: e.tensor_scalar(
            lsif[:], lsif[:], bsel_sb[0:1, 0:1], None, ALU.is_le), [hc0, hbs])
        hc2 = P.op("vector", lambda e: e.reduce_sum(
            cnt_sb[:], lsif[:], axis=AX.X), [hc1])
        hc3 = P.op("vector", lambda e: e.tensor_scalar(
            uc_sb[:], cnt_sb[:], -2.0, 0.0, ALU.add, ALU.max), [hc2])
        hc4 = P.op("vector", lambda e: e.tensor_scalar(
            uc_sb[:], uc_sb[:], 2.0, None, ALU.min), [hc3])
        hc5 = P.op("vector", lambda e: e.tensor_scalar(
            gate_sb[:], cnt_sb[:], 2.0, None, ALU.is_ge), [hc2])
        hws = []
        for a in range(3):
            hw_ = P.op("vector", lambda e, a=a: e.tensor_scalar(
                wa_sb[0:1, a:a + 1], uc_sb[:], float(a),
                gate_sb[0:1, 0:1], ALU.is_equal, ALU.mult), [hc4, hc5])
            hws.append(hw_)
        hwf = []
        for a in range(3):
            hw2 = P.op("vector", lambda e, a=a: e.tensor_scalar(
                wfree[0:1, 16 * a:16 * (a + 1)], ones16[:],
                wa_sb[0:1, a:a + 1], None, ALU.mult), [hws[a], hones16])
            hwf.append(hw2)
        hwd = P.dma("sync", wdram.ap(), wfree[:], fences_list(hwf))
        hw48 = P.dma("sync", w48_sb[:], wdram.ap().rearrange("o (k c) -> k (o c)", k=48),
                     [hwd])

        rt_done = []
        hrt_prev = None
        for t2 in range(2):
            hmm = None
            for kc in range(32):
                rb_ = rhs0 if kc % 2 == 0 else rhs1
                hrl = P.dma("sync", rb_[:],
                            xnt[kc * 128:(kc + 1) * 128,
                                t2 * 512:(t2 + 1) * 512],
                            [P._war_rhs[kc % 2] if hasattr(P, "_war_rhs")
                             else None])
                if not hasattr(P, "_war_rhs"):
                    P._war_rhs = [None, None]
                hmm = P.op("tensor", lambda e, kc=kc, rb_=rb_: e.matmul(
                    ps_rt[:], acat_sb[:, kc * 48:(kc + 1) * 48], rb_[:],
                    start=(kc == 0), stop=(kc == 31)),
                    [hrl, hacat, hid, hrt_prev if kc == 0 else None])
                P._war_rhs[kc % 2] = hmm
            hev = P.op("vector", lambda e, t2=t2: e.tensor_scalar(
                rt_sb[:, t2 * 512:(t2 + 1) * 512], ps_rt[:], w48_sb[:],
                None, ALU.mult), [hmm, hw48])
            hrt_prev = hev
            rt_done.append(hev)
        hrt = fences_list(rt_done)

        war_xnt_tt = [None]
        war_ot_tt = [None]
        war_wblk = [None]
        war_ps = {"a": [None], "b": [None]}
        war_xt = [None, None]
        war_tmp = [None, None]
        war_x1t = [None, None]
        war_g = [None]
        x1_stores = []
        war_bf = [None, None]
        war_bo = [None, None]
        for ob in range(8):
            pb = ob % 2
            hwf_ = P.dma("gpsimd", wblk[:].rearrange("p (k c) -> p k c", k=32),
                         wft.rearrange("(k p) o -> p k o", p=128)
                         [:, :, ob * 512:(ob + 1) * 512], [war_wblk[0]])
            hbf = P.dma("sync", bf_t[pb][:],
                        bfr[0:1, ob * 512:(ob + 1) * 512], [war_bf[pb]])
            g_done = []
            for tt in range(8):
                hxl = P.dma("sync",
                            xnt_tt[:].rearrange("p (k c) -> p k c", k=32),
                            xnt.rearrange("(k p) t -> p k t", p=128)
                            [:, :, tt * 128:(tt + 1) * 128],
                            [war_xnt_tt[0]])
                pi = tt % 2
                pst = ps_a if pi == 0 else ps_b

                def ffnmm(e, tt=tt, ob=ob, pst=pst, pb=pb):
                    r = e.matmul(pst[:], ones_sb[:], bf_t[pb][:],
                                 start=True, stop=False)
                    for kc in range(32):
                        r = e.matmul(pst[:],
                                     xnt_tt[:, kc * 128:(kc + 1) * 128],
                                     wblk[:, kc * 512:(kc + 1) * 512],
                                     start=False, stop=False)
                    r = e.matmul(pst[:], rt_sb[:, tt * 128:(tt + 1) * 128],
                                 bcat_sb[:, ob * 512:(ob + 1) * 512],
                                 start=False, stop=True)
                    return r
                hmm = P.op("tensor", ffnmm,
                           [hxl, hwf_, war_ps["a" if pi == 0 else "b"][0],
                            hones, hbf, hbcat] + hrt)
                war_xnt_tt[0] = hmm
                hsil = P.op("scalar", lambda e, tt=tt, pst=pst: e.activation(
                    g_blk[:, tt * 512:(tt + 1) * 512], pst[:], AF.Silu),
                    [hmm, war_g[0]])
                war_ps["a" if pi == 0 else "b"][0] = hsil
                g_done.append(hsil)
            war_bf[pb] = hmm
            hwo_ = P.dma("gpsimd", wblk[:].rearrange("p (k c) -> p k c", k=32),
                         wot.rearrange("(k p) o -> p k o", p=128)
                         [:, :, ob * 512:(ob + 1) * 512],
                         [war_xnt_tt[0]])
            hbo = P.dma("sync", bo_t[pb][:],
                        bor[0:1, ob * 512:(ob + 1) * 512], [war_bo[pb]])
            g_all = fences_list(g_done)
            for tt in range(8):
                hol = P.dma("sync",
                            ot_tt[:].rearrange("p (k c) -> p k c", k=32),
                            ot.rearrange("(k p) t -> p k t", p=128)
                            [:, :, tt * 128:(tt + 1) * 128],
                            [war_ot_tt[0]])
                pi = tt % 2
                pst = ps_a if pi == 0 else ps_b

                def womm(e, tt=tt, ob=ob, pst=pst, pb=pb):
                    r = e.matmul(pst[:], ones_sb[:], bo_t[pb][:],
                                 start=True, stop=False)
                    for kc in range(32):
                        r = e.matmul(pst[:],
                                     ot_tt[:, kc * 128:(kc + 1) * 128],
                                     wblk[:, kc * 512:(kc + 1) * 512],
                                     start=False, stop=(kc == 31))
                    return r
                hmm = P.op("tensor", womm,
                           [hol, hwo_, war_ps["a" if pi == 0 else "b"][0],
                            hones, hbo])
                war_ot_tt[0] = hmm
                bi = tt % 2
                hxt = P.dma("sync", (xt0 if bi == 0 else xt1)[:],
                            x[tt * 128:(tt + 1) * 128,
                              ob * 512:(ob + 1) * 512], [war_xt[bi]])
                htm = P.op("vector", lambda e, tt=tt, pst=pst, bi=bi:
                           e.tensor_tensor((tmp0 if bi == 0 else tmp1)[:],
                                           pst[:],
                                           g_blk[:, tt * 512:(tt + 1) * 512],
                                           ALU.mult),
                           [hmm, war_tmp[bi]] + g_all)
                war_ps["a" if pi == 0 else "b"][0] = htm
                hx1 = P.op("vector", lambda e, bi=bi: e.tensor_tensor(
                    (x1t0 if bi == 0 else x1t1)[:],
                    (tmp0 if bi == 0 else tmp1)[:],
                    (xt0 if bi == 0 else xt1)[:], ALU.add),
                    [htm, hxt, war_x1t[bi]])
                war_tmp[bi] = hx1
                war_xt[bi] = hx1
                hst = P.dma("sync", x1[tt * 128:(tt + 1) * 128,
                                       ob * 512:(ob + 1) * 512],
                            (x1t0 if bi == 0 else x1t1)[:], [hx1])
                war_x1t[bi] = hst
                x1_stores.append(hst)
            war_bo[pb] = hmm
            war_wblk[0] = hmm
            war_g[0] = fences_list([war_ps["a"][0], war_ps["b"][0]])[0]
        hx1f = fences_list(x1_stores)

        ln_bufs = {
            "xs": [xs0, xs1], "stats": stats,
            "xn_tm": [xntm0, xntm1],
            "pt_ps": [ptps0, ptps1],
        }
        ln_done = emit_ln(P, ln_bufs, x1, xn1t_sb, ident_sb, ident_h=hid,
                          extra_dep=hx1f)
        lnf = fences_list(ln_done)
        hxn1 = P.dma("sync", xn1t.rearrange("(k p) t -> p k t", p=128),
                     xn1t_sb[:].rearrange("p (k t) -> p k t", k=32), lnf)

        pbufs = {
            "w_fm": [wfm, wfm],
            "ps_fm": [ps_c, ps_d],
            "o_fm": [ofm0, ofm1],
            "rope_a": tmp0, "rope_b": tmp1,
            "w_tm": [wblk, wblk],
            "ps_tm": [ps_a, ps_b],
            "v_tm": [rhs0, rhs1],
            "war_w_fm": [None, None], "war_ps_fm": [None, None],
            "war_o_fm": [None, None],
            "war_w_tm": [fences_list([war_wblk[0]])[0], None],
            "war_ps_tm": [war_ps["a"][0], war_ps["b"][0]],
            "war_v_tm": [P._war_rhs[0], P._war_rhs[1]],
            "rope_war": [fences_list([war_tmp[0], war_tmp[1]])[0]],
        }
        hv1 = emit_proj_tm(P, pbufs, wv1, v1, lnf, xn1t_sb, 1024, tag="v1")
        pbufs["war_w_fm"] = [None, None]
        hrope = [hcos, hsin]
        hq1 = emit_proj_fm(P, pbufs, wq1, q1t, lnf, xn1t_sb, D,
                           rope=(rcos_sb, rsin_sb), rope_dep=hrope, tag="q1")
        hk1 = emit_proj_fm(P, pbufs, wk1, k1t, lnf, xn1t_sb, 1024,
                           rope=(rcos_sb, rsin_sb), rope_dep=hrope, tag="k1")
        P.barrier("sync", fences_list([hxn1] + hv1 + hq1 + hk1 +
                                      x1_stores + [hcos, hsin, hid]))
        block = ctx.enter_context(nc.Block())
        P.emit(ctx, block)
    return nc


def build_launch5():
    nc = bass.Bass("TRN2", target_bir_lowering=False)
    di = lambda n, s, d: nc.dram_tensor(n, s, d, kind="ExternalInput").ap()
    do = lambda n, s, d: nc.dram_tensor(n, s, d, kind="ExternalOutput").ap()
    x1 = di("x1", [TPC, D], F32)
    xn1t = di("xn1t", [D, TPC], BF16)
    o1t = di("o1t", [D, TPC], BF16)
    wo1 = di("wo1", [D, D], F32)
    a1cat = di("a1cat", [D, 128], F32)
    b1cat = di("b1cat", [128, D], F32)
    wrt = di("wrt", [D, 8], F32)
    brr = di("brr", [1, 8], F32)
    e8 = di("e8", [8, 128], F32)
    identity = di("identity", [128, 128], F32)
    out = do("out", [TPC, D], F32)

    with ExitStack() as ctx:
        sb = lambda n, s, d: ctx.enter_context(nc.sbuf_tensor(n, s, d))
        psum = lambda n, s, d: ctx.enter_context(nc.psum_tensor(n, s, d))
        xn1t_sb = sb("xn1t_sb", [128, 32 * TPC], BF16)
        wblk = sb("wblk", [128, 32 * 512], BF16)
        g_blk = sb("g_blk", [128, D], F32)
        o1_tt = sb("o1_tt", [128, 32 * 128], BF16)
        a1_sb = sb("a1_sb", [128, 32 * 128], BF16)
        b1_sb = sb("b1_sb", [128, D], BF16)
        wr_sb = sb("wr_sb", [128, 32 * 8], BF16)
        e8_sb = sb("e8_sb", [8, 128], BF16)
        br_sb = sb("br_sb", [1, 8], F32)
        ones_sb = sb("ones_sb", [1, 128], F32)
        ident_sb = sb("ident_sb", [128, 128], BF16)
        p8_sb = sb("p8_sb", [128, 8], F32)
        negm8 = sb("negm8", [128, 1], F32)
        z8 = sb("z8", [128, 1], F32)
        rz8 = sb("rz8", [128, 1], F32)
        sc8 = sb("sc8", [128, 8], BF16)
        sct_sb = sb("sct_sb", [8, 128], BF16)
        scb_sb = sb("scb_sb", [128, 128], F32)
        lows_sb = sb("lows_sb", [128, TPC], BF16)
        xt0 = sb("xt0", [128, 512], F32)
        xt1 = sb("xt1", [128, 512], F32)
        tmp0 = sb("tmp0", [128, 512], F32)
        tmp1 = sb("tmp1", [128, 512], F32)
        ot0 = sb("ot0", [128, 512], F32)
        ot1 = sb("ot1", [128, 512], F32)
        ps_r1 = psum("ps_r1", [128, 512], F32)
        ps_r2 = psum("ps_r2", [128, 512], F32)
        ps_t8 = psum("ps_t8", [8, 128], BF16)
        ps_a = psum("ps_a", [128, 512], F32)
        ps_b = psum("ps_b", [128, 512], F32)
        dscr = psum("dscr", [128, 1024], BF16)

        P = Plan(nc)
        P.settle_fn = lambda e: e.transpose(dscr[:, 0:128], ident_sb[:],
                                            ident_sb[:])
        hid = P.dma("gpsimd", ident_sb[:], identity)
        hxn = P.dma("sync", xn1t_sb[:].rearrange("p (k t) -> p k t", k=32),
                    xn1t.rearrange("(k p) t -> p k t", p=128))
        ha1 = P.dma("gpsimd", a1_sb[:].rearrange("p (k c) -> p k c", k=32),
                    a1cat.rearrange("(k p) c -> p k c", p=128))
        hb1 = P.dma("gpsimd", b1_sb[:], b1cat)
        hwr = P.dma("gpsimd", wr_sb[:].rearrange("p (k c) -> p k c", k=32),
                    wrt.rearrange("(k p) c -> p k c", p=128))
        he8 = P.dma("gpsimd", e8_sb[:], e8)
        hbr = P.dma("sync", br_sb[:], brr)
        hones = P.op("vector", lambda e: e.memset(ones_sb[:], 1.0))

        war = {}
        lows_done = []
        for tt in range(8):
            def logmm(e, tt=tt):
                r = e.matmul(ps_r1[:, 0:8], ones_sb[:], br_sb[:],
                             start=True, stop=False)
                for kc in range(32):
                    r = e.matmul(ps_r1[:, 0:8],
                                 xn1t_sb[:, kc * TPC + tt * 128:
                                         kc * TPC + (tt + 1) * 128],
                                 wr_sb[:, kc * 8:(kc + 1) * 8],
                                 start=False, stop=(kc == 31))
                return r
            hlg = P.op("tensor", logmm,
                       [hxn, hwr, hbr, hones, hid, war.get("ps_r1")])
            hmx = P.op("vector", lambda e: e.reduce_max(
                negm8[:], ps_r1[:, 0:8], axis=AX.X, negate=True),
                [hlg, war.get("negm8")])
            hex_ = P.op("scalar", lambda e: e.activation(
                p8_sb[:], ps_r1[:, 0:8], AF.Exp, bias=negm8[:],
                accum_out=z8[:]), [hmx, war.get("p8")])
            war["negm8"] = hex_
            hrz = P.op("vector", lambda e: e.reciprocal(rz8[:], z8[:]),
                       [hex_, war.get("rz8")])
            hrz2 = P.op("vector", lambda e: e.tensor_scalar(
                rz8[:], rz8[:], 8.0, None, ALU.mult), [hrz])
            hsc = P.op("vector", lambda e: e.tensor_scalar(
                sc8[:], p8_sb[:], rz8[:], None, ALU.mult),
                [hrz2, war.get("sc8")])
            war["p8"] = hsc
            war["rz8"] = hsc
            htr = P.op("tensor", lambda e: e.transpose(
                ps_t8[:], sc8[:], ident_sb[:]),
                [hsc, hid, war.get("ps_t8")])
            war["sc8"] = htr
            hev = P.op("scalar", lambda e: e.copy(sct_sb[:], ps_t8[:]),
                       [htr, war.get("sct")])
            war["ps_t8"] = hev
            hsb = P.op("tensor", lambda e: e.matmul(
                ps_r1[:, 128:256], e8_sb[:], sct_sb[:],
                start=True, stop=True), [hev, he8, war.get("ps_r1b")])
            war["sct"] = hsb
            war["ps_r1"] = hsb
            hscb = P.op("vector", lambda e: e.tensor_copy(
                scb_sb[:], ps_r1[:, 128:256]), [hsb, war.get("scb")])
            war["ps_r1b"] = hscb

            def lowmm(e, tt=tt):
                r = None
                for kc in range(32):
                    r = e.matmul(ps_r2[:, 0:128],
                                 a1_sb[:, kc * 128:(kc + 1) * 128],
                                 xn1t_sb[:, kc * TPC + tt * 128:
                                         kc * TPC + (tt + 1) * 128],
                                 start=(kc == 0), stop=(kc == 31))
                return r
            hlo = P.op("tensor", lowmm, [hxn, ha1, war.get("ps_r2")])
            hls = P.op("vector", lambda e, tt=tt: e.tensor_tensor(
                lows_sb[:, tt * 128:(tt + 1) * 128], ps_r2[:, 0:128],
                scb_sb[:], ALU.mult), [hlo, hscb, war.get("lows")])
            war["ps_r2"] = hls
            war["scb"] = hls
            lows_done.append(hls)
        hlows = fences_list(lows_done)

        war_wblk = [None]
        war_o1 = [None]
        war_ps = [None, None]
        war_xt = [None, None]
        war_tmp = [None, None]
        war_ot = [None, None]
        war_g = [None]
        out_stores = []
        for ob in range(8):
            hwo_ = P.dma("gpsimd", wblk[:].rearrange("p (k c) -> p k c", k=32),
                         wo1.rearrange("(k p) o -> p k o", p=128)
                         [:, :, ob * 512:(ob + 1) * 512], [war_wblk[0]])
            g_done = []
            for tt in range(8):
                pi = tt % 2
                pst = ps_a if pi == 0 else ps_b
                hg = P.op("tensor", lambda e, tt=tt, ob=ob, pst=pst: e.matmul(
                    pst[:], lows_sb[:, tt * 128:(tt + 1) * 128],
                    b1_sb[:, ob * 512:(ob + 1) * 512],
                    start=True, stop=True),
                    hlows + [hb1, war_ps[pi]])
                hsil = P.op("scalar", lambda e, tt=tt, pst=pst: e.activation(
                    g_blk[:, tt * 512:(tt + 1) * 512], pst[:], AF.Silu,
                    scale=2.0), [hg, war_g[0]])
                war_ps[pi] = hsil
                g_done.append(hsil)
            g_all = fences_list(g_done)
            for tt in range(8):
                hol = P.dma("sync",
                            o1_tt[:].rearrange("p (k c) -> p k c", k=32),
                            o1t.rearrange("(k p) t -> p k t", p=128)
                            [:, :, tt * 128:(tt + 1) * 128], [war_o1[0]])
                pi = tt % 2
                pst = ps_a if pi == 0 else ps_b

                def womm(e, tt=tt, ob=ob, pst=pst):
                    r = None
                    for kc in range(32):
                        r = e.matmul(pst[:],
                                     o1_tt[:, kc * 128:(kc + 1) * 128],
                                     wblk[:, kc * 512:(kc + 1) * 512],
                                     start=(kc == 0), stop=(kc == 31))
                    return r
                hmm = P.op("tensor", womm, [hol, hwo_, war_ps[pi]])
                war_o1[0] = hmm
                bi = tt % 2
                hxt = P.dma("sync", (xt0 if bi == 0 else xt1)[:],
                            x1[tt * 128:(tt + 1) * 128,
                               ob * 512:(ob + 1) * 512], [war_xt[bi]])
                htm = P.op("vector", lambda e, tt=tt, pst=pst, bi=bi:
                           e.tensor_tensor((tmp0 if bi == 0 else tmp1)[:],
                                           pst[:],
                                           g_blk[:, tt * 512:(tt + 1) * 512],
                                           ALU.mult),
                           [hmm, war_tmp[bi]] + g_all)
                war_ps[pi] = htm
                hx1 = P.op("vector", lambda e, bi=bi: e.tensor_tensor(
                    (ot0 if bi == 0 else ot1)[:],
                    (tmp0 if bi == 0 else tmp1)[:],
                    (xt0 if bi == 0 else xt1)[:], ALU.add),
                    [htm, hxt, war_ot[bi]])
                war_tmp[bi] = hx1
                war_xt[bi] = hx1
                hst = P.dma("sync", out[tt * 128:(tt + 1) * 128,
                                        ob * 512:(ob + 1) * 512],
                            (ot0 if bi == 0 else ot1)[:], [hx1])
                war_ot[bi] = hst
                out_stores.append(hst)
            war_wblk[0] = hmm
            war_g[0] = fences_list([war_ps[0], war_ps[1]])[0]
        P.barrier("sync", fences_list(out_stores))
        block = ctx.enter_context(nc.Block())
        P.emit(ctx, block)
    return nc


RG = [[0, 1, 2, 3], [4, 5, 6, 7]]


def build_fused():
    nc = bass.Bass("TRN2", target_bir_lowering=False)
    di = lambda n, s, d: nc.dram_tensor(n, s, d, kind="ExternalInput").ap()
    do = lambda n, s, d: nc.dram_tensor(n, s, d, kind="ExternalOutput").ap()
    dint = lambda n, s, d: nc.dram_tensor(n, s, d)
    x = di("x", [TPC, D], F32)
    wqt = di("wqt", [D, D], F32)
    wkt = di("wkt", [D, D], F32)
    wvt = di("wvt", [D, D], F32)
    bqt = di("bqt", [128, 32], F32)
    bkt = di("bkt", [128, 32], F32)
    bvr = di("bvr", [1, D], F32)
    wot = di("wot", [D, D], F32)
    bor = di("bor", [1, D], F32)
    wft = di("wft", [D, D], F32)
    bfr = di("bfr", [1, D], F32)
    acat = di("acat", [D, 48], F32)
    bcat = di("bcat", [48, D], F32)
    lsi = di("lsi", [1, 4], I32)
    bsel = di("bsel", [1, 1], F32)
    wq1 = di("wq1", [D, D], F32)
    wk1 = di("wk1", [D, 1024], F32)
    wv1 = di("wv1", [D, 1024], F32)
    rcos = di("rcos", [128, TPC], F32)
    rsin = di("rsin", [128, TPC], F32)
    wo1 = di("wo1", [D, D], F32)
    a1cat = di("a1cat", [D, 128], F32)
    b1cat = di("b1cat", [128, D], F32)
    wrt = di("wrt", [D, 8], F32)
    brr = di("brr", [1, 8], F32)
    e8 = di("e8", [8, 128], F32)
    mdg = di("mdg", [128, 128], F32)
    mwn = di("mwn", [128, 128], F32)
    identity = di("identity", [128, 128], F32)
    rank = di("rank", [1, 1], I32)
    out = do("out", [TPC, D], F32)
    x1_d = dint("x1_d", [TPC, D], F32)
    wdram = dint("wdram", [1, 48], F32)
    qt_in = dint("qt_in", [D, TPC], BF16)
    kt_in = dint("kt_in", [D, TPC], BF16)
    v_in = dint("v_in", [TPC, D], BF16)
    qt_ag = dint("qt_ag", [4 * D, TPC], BF16)
    kt_ag = dint("kt_ag", [4 * D, TPC], BF16)
    v_ag = dint("v_ag", [S, D], BF16)
    ot_in_a = dint("ot_in_a", [512, S], BF16)
    ot_in_b = dint("ot_in_b", [512, S], BF16)
    ot_ag_a = dint("ot_ag_a", [2048, S], BF16)
    ot_ag_b = dint("ot_ag_b", [2048, S], BF16)
    q1_in = dint("q1_in", [D, TPC], BF16)
    k1_in = dint("k1_in", [1024, TPC], BF16)
    v1_in = dint("v1_in", [TPC, 1024], BF16)
    q1_ag = dint("q1_ag", [4 * D, TPC], BF16)
    k1_ag = dint("k1_ag", [4096, TPC], BF16)
    v1_ag = dint("v1_ag", [S, 1024], BF16)
    o1_in_a = dint("o1_in_a", [512, S], BF16)
    o1_in_b = dint("o1_in_b", [512, S], BF16)
    o1_ag_a = dint("o1_ag_a", [2048, S], BF16)
    o1_ag_b = dint("o1_ag_b", [2048, S], BF16)

    with ExitStack() as ctx:
        ar = bass.AutoArena(nc)
        T = ar.sbuf_tensor
        xnA = T("xnA", [128, 32 * TPC], BF16)
        ident_sb = T("identW", [128, 128], BF16)
        md_sb = T("md_sb", [128, 128], F32)
        mw_sb = T("mw_sb", [128, 128], F32)
        ones_sb = T("ones_sb", [1, 128], F32)
        ones16 = T("ones16", [1, 16], F32)
        rank_sb = T("rank_sb", [1, 1], I32)
        lsii = T("lsii", [1, 4], I32)
        lsif = T("lsif", [1, 4], F32)
        bsel_sb = T("bsel_sb", [1, 1], F32)
        cnt_sb = T("cnt_sb", [1, 1], F32)
        uc_sb = T("uc_sb", [1, 1], F32)
        gate_sb = T("gate_sb", [1, 1], F32)
        wa_sb = T("wa_sb", [1, 3], F32)
        wfree = T("wfree", [1, 48], F32)
        w48_sb = T("w48_sb", [48, 1], F32)
        rt_sb = T("rt_sb", [48, TPC], BF16)
        o_tmp = T("o_tmp", [128, 128], BF16)
        oacc_sb = T("oacc_sb", [128, 128], F32)
        stats = {n: T(f"st_{n}", [128, 1], F32)
                 for n in ["ssq", "ssum", "mean", "tmp", "vpe", "s0", "r0",
                           "t2", "s1", "rstd", "nmr"]}
        fl = {n: T(f"fl_{n}", [128, 1], F32)
              for n in ["negm", "negm2", "dtmp", "alpha", "zz", "zp", "rz",
                        "negmsc"]}
        mark = ar.addr
        xs0 = T("xs0", [128, D], F32)
        xs1 = T("xs1", [128, D], F32)
        xntm0 = T("xntm0", [128, D], BF16)
        xntm1 = T("xntm1", [128, D], BF16)
        wfm0 = T("wfm0", [128, 32 * 128], BF16)
        wfm1 = T("wfm1", [128, 32 * 128], BF16)
        ofm0 = T("ofm0", [128, TPC], BF16)
        ofm1 = T("ofm1", [128, TPC], BF16)
        wblkA0 = T("wblkA0", [128, 32 * 256], BF16)
        wblkA1 = T("wblkA1", [128, 32 * 256], BF16)
        vtm0 = T("vtm0", [128, 512], BF16)
        vtm1 = T("vtm1", [128, 512], BF16)
        bq_sb = T("bq_sb", [128, 32], F32)
        bk_sb = T("bk_sb", [128, 32], F32)
        bvt0 = T("bvt0", [1, 256], F32)
        bvt1 = T("bvt1", [1, 256], F32)
        wm1 = ar.addr
        ar.addr = mark
        q_b0 = T("q_b0", [128, S], BF16)
        q_b1 = T("q_b1", [128, S], BF16)
        k_b0 = T("k_b0", [128, S], BF16)
        k_b1 = T("k_b1", [128, S], BF16)
        v_b0 = T("v_b0", [128, 32 * 128], BF16)
        v_b1 = T("v_b1", [128, 32 * 128], BF16)
        p_b0 = T("p_b0", [128, 2048], BF16)
        p_b1 = T("p_b1", [128, 2048], BF16)
        pt_b0 = T("pt_b0", [128, 1024], BF16)
        pt_b1 = T("pt_b1", [128, 1024], BF16)
        oh_b0 = T("oh_b0", [128, 32 * 128], BF16)
        oh_b1 = T("oh_b1", [128, 32 * 128], BF16)
        wm2 = ar.addr
        ar.addr = mark
        wblkB = T("wblkB", [128, 32 * 512], BF16)
        tmp0 = T("tmp0", [128, 512], F32)
        tmp1 = T("tmp1", [128, 512], F32)
        rcos_sb = T("rcos_sb", [128, TPC], F32)
        rsin_sb = T("rsin_sb", [128, TPC], F32)
        wfmB0 = T("wfmB0", [128, 32 * 128], BF16)
        wfmB1 = T("wfmB1", [128, 32 * 128], BF16)
        ofmB0 = T("ofmB0", [128, TPC], BF16)
        ofmB1 = T("ofmB1", [128, TPC], BF16)
        vtmB0 = T("vtmB0", [128, 512], BF16)
        vtmB1 = T("vtmB1", [128, 512], BF16)
        mark3 = ar.addr
        gB = T("gB", [128, D], F32)
        ot_ttB0 = T("ot_ttB0", [128, 32 * 128], BF16)
        ot_ttB1 = T("ot_ttB1", [128, 32 * 128], BF16)
        xtB0 = T("xtB0", [128, 512], F32)
        xtB1 = T("xtB1", [128, 512], F32)
        x1tB0 = T("x1tB0", [128, 512], F32)
        x1tB1 = T("x1tB1", [128, 512], F32)
        acat_sb = T("acat_sb", [128, 32 * 48], BF16)
        bcat_sb = T("bcat_sb", [48, D], BF16)
        bo_t = [T("bo_t0", [1, 512], F32), T("bo_t1", [1, 512], F32)]
        bf_t = [T("bf_t0", [1, 512], F32), T("bf_t1", [1, 512], F32)]
        wm3a = ar.addr
        ar.addr = mark3
        xsB0 = T("xsB0", [128, D], F32)
        xsB1 = T("xsB1", [128, D], F32)
        xntmB0 = T("xntmB0", [128, D], BF16)
        xntmB1 = T("xntmB1", [128, D], BF16)
        wm3b = ar.addr
        ar.addr = max(wm3a, wm3b)
        wm3 = ar.addr
        ar.addr = mark
        wblkC = T("wblkC", [128, 32 * 512], BF16)
        gC = T("gC", [128, D], F32)
        ot_ttC0 = T("ot_ttC0", [128, 32 * 128], BF16)
        ot_ttC1 = T("ot_ttC1", [128, 32 * 128], BF16)
        xtC0 = T("xtC0", [128, 512], F32)
        xtC1 = T("xtC1", [128, 512], F32)
        tmpC0 = T("tmpC0", [128, 512], F32)
        tmpC1 = T("tmpC1", [128, 512], F32)
        x1tC0 = T("x1tC0", [128, 512], F32)
        x1tC1 = T("x1tC1", [128, 512], F32)
        a1_sb = T("a1_sb", [128, 32 * 128], BF16)
        b1_sb = T("b1_sb", [128, D], BF16)
        wr_sb = T("wr_sb", [128, 32 * 8], BF16)
        e8_sb = T("e8_sb", [8, 128], BF16)
        br_sb = T("br_sb", [1, 8], F32)
        p8_sb = T("p8_sb", [128, 8], F32)
        negm8 = T("negm8", [128, 1], F32)
        z8 = T("z8", [128, 1], F32)
        rz8 = T("rz8", [128, 1], F32)
        sc8 = T("sc8", [128, 8], BF16)
        sct_sb = T("sct_sb", [8, 128], BF16)
        scb_sb = T("scb_sb", [128, 128], F32)
        lows_sb = T("lows_sb", [128, TPC], BF16)
        ar.addr = max(ar.addr, wm1, wm2, wm3)
        ar.allocate()

        psum = lambda n, s, d: ctx.enter_context(nc.psum_tensor(n, s, d))
        s_ps = psum("s_ps", [128, 2048], F32)
        ptp0 = psum("ptp0", [128, 1024], BF16)
        ptp1 = psum("ptp1", [128, 1024], BF16)
        o_ps = psum("o_ps", [128, 512], F32)
        dscr = psum("dscr", [128, 1024], BF16)
        ps_a = s_ps[:, 0:512]
        ps_b = s_ps[:, 512:1024]
        ps_c = s_ps[:, 1024:1536]
        ps_d = s_ps[:, 1536:2048]
        rk_reg = ctx.enter_context(nc.gpsimd.register("rk_reg"))

        P = Plan(nc)
        SS = {}

        hid = P.dma("gpsimd", ident_sb[:], identity)
        hmd = P.dma("gpsimd", md_sb[:], mdg)
        hmw = P.dma("gpsimd", mw_sb[:], mwn)
        hrank = P.dma("gpsimd", rank_sb[:], rank)

        def pre(e):
            e.reg_load(rk_reg, rank_sb[0:1, 0:1])
            SS["rk"] = e.snap(rk_reg, min_val=0, max_val=3)
        P.raw("gpsimd", pre, [hrank])
        hbq = P.dma("sync", bq_sb[:], bqt)
        hbk = P.dma("sync", bk_sb[:], bkt)
        hones = P.op("vector", lambda e: e.memset(ones_sb[:], 1.0))
        hones16 = P.op("vector", lambda e: e.memset(ones16[:], 1.0))
        hlsi = P.dma("sync", lsii[:], lsi)
        hbs = P.dma("sync", bsel_sb[:], bsel)
        hc0 = P.op("vector", lambda e: e.tensor_copy(lsif[:], lsii[:]), [hlsi])
        hc1 = P.op("vector", lambda e: e.tensor_scalar(
            lsif[:], lsif[:], bsel_sb[0:1, 0:1], None, ALU.is_le), [hc0, hbs])
        hc2 = P.op("vector", lambda e: e.reduce_sum(
            cnt_sb[:], lsif[:], axis=AX.X), [hc1])
        hc3 = P.op("vector", lambda e: e.tensor_scalar(
            uc_sb[:], cnt_sb[:], -2.0, 0.0, ALU.add, ALU.max), [hc2])
        hc4 = P.op("vector", lambda e: e.tensor_scalar(
            uc_sb[:], uc_sb[:], 2.0, None, ALU.min), [hc3])
        hc5 = P.op("vector", lambda e: e.tensor_scalar(
            gate_sb[:], cnt_sb[:], 2.0, None, ALU.is_ge), [hc2])
        hws = []
        for a in range(3):
            hws.append(P.op("vector", lambda e, a=a: e.tensor_scalar(
                wa_sb[0:1, a:a + 1], uc_sb[:], float(a),
                gate_sb[0:1, 0:1], ALU.is_equal, ALU.mult), [hc4, hc5]))
        hwf = []
        for a in range(3):
            hwf.append(P.op("vector", lambda e, a=a: e.tensor_scalar(
                wfree[0:1, 16 * a:16 * (a + 1)], ones16[:],
                wa_sb[0:1, a:a + 1], None, ALU.mult), [hws[a], hones16]))
        hwd = P.dma("sync", wdram.ap(), wfree[:], fences_list(hwf))
        hw48 = P.dma("sync", w48_sb[:],
                     wdram.ap().rearrange("o (k c) -> k (o c)", k=48), [hwd])

        ln_bufs = {"xs": [xs0, xs1], "stats": stats,
                   "xn_tm": [xntm0, xntm1], "pt_ps": [ptp0, ptp1]}
        ln_done = emit_ln(P, ln_bufs, x, xnA, ident_sb, ident_h=hid)
        lnf = fences_list(ln_done)
        def mkcc(name, in_t, out_t, deps):
            return P.op("gpsimd", lambda e: e.collective_compute(
                "AllGather", ALU.bypass, replica_groups=RG,
                ins=[in_t.ap().opt()], outs=[out_t.ap().opt()]),
                deps, kind=("cc", name))

        pbufs = {"w_fm": [wfm0, wfm1], "ps_fm": [ps_a, ps_b],
                 "o_fm": [ofm0, ofm1],
                 "w_tm": [wblkA0, wblkA1], "ps_tm": [ps_c, ps_d],
                 "v_tm": [vtm0, vtm1], "b_tm": [bvt0, bvt1]}
        hq = emit_proj_fm(P, pbufs, wqt, qt_in.ap(), lnf, xnA, D,
                          bias_sb=bq_sb, bias_h=hbq, scale=1.0, tag="q")
        hcc_q = mkcc("ccq", qt_in, qt_ag, fences_list(hq))
        hk = emit_proj_fm(P, pbufs, wkt, kt_in.ap(), lnf, xnA, D,
                          bias_sb=bk_sb, bias_h=hbk, scale=1.0, tag="k")
        hcc_k = mkcc("cck", kt_in, kt_ag, fences_list(hk))
        hv = emit_proj_tm(P, pbufs, wvt, v_in.ap(), lnf, xnA, D,
                          bias_dram=bvr, ones_sb=ones_sb,
                          ones_h=hones, tag="v", W=256)
        hcc_v = mkcc("ccv", v_in, v_ag, fences_list(hv))

        def nocc(f):
            return [h for h in f
                    if not (isinstance(h.kind, tuple) and h.kind[0] == "cc")]
        ph1 = nocc(P.fence_all())

        ab = {"fl": fl, "oacc": oacc_sb, "o_tmp": o_tmp,
              "p_sb": [p_b0, p_b1], "pt_sb": [pt_b0, pt_b1],
              "ohead": [oh_b0, oh_b1], "s_ps": s_ps,
              "pt_ps": [ptp0, ptp1], "o_ps": o_ps,
              "otr_ps": dscr[:, 128:256], "war": {}}
        qagr = qt_ag.ap().rearrange("(r q) t -> q r t", r=4)
        kagr = kt_ag.ap().rearrange("(r q) t -> q r t", r=4)
        vagr = v_ag.ap().rearrange("(kc p) c -> p kc c", p=128)
        qk_free = [None, None]
        v_free = [None, None]
        att0_dma = []
        for h in range(8):
            hb = h % 2
            qsb = [q_b0, q_b1][hb]
            ksb = [k_b0, k_b1][hb]
            vsb = [v_b0, v_b1][hb]

            def ldq(e, h=h, qsb=qsb):
                return e.dma_start(
                    out=qsb[:].rearrange("p (r t) -> p r t", r=4),
                    in_=qagr[bass.ds(SS["rk"] * 1024 + 128 * h, 128), :, :])
            hql = P.op("gpsimd", ldq,
                       [hcc_q, qk_free[hb]] + (ph1 if h < 2 else []),
                       kind=("d", f"qb{hb}"))

            def ldk(e, h=h, ksb=ksb):
                return e.dma_start(
                    out=ksb[:].rearrange("p (r t) -> p r t", r=4),
                    in_=kagr[bass.ds(SS["rk"] * 1024 + 128 * h, 128), :, :])
            hkl = P.op("gpsimd", ldk,
                       [hcc_k, qk_free[hb]] + (ph1 if h < 2 else []),
                       kind=("d", f"kb{hb}"))

            def ldv(e, h=h, vsb=vsb):
                return e.dma_start(
                    out=vsb[:].rearrange("p (kc c) -> p kc c", kc=32),
                    in_=vagr[:, :, bass.ds(SS["rk"] * 1024 + 128 * h, 128)])
            hvl = P.op("gpsimd", ldv,
                       [hcc_v, v_free[hb]] + (ph1 if h < 2 else []),
                       kind=("d", f"vb{hb}"))
            o_t = ot_in_a if h < 4 else ot_in_b
            o_ap = o_t.ap()[(h % 4) * 128:(h % 4 + 1) * 128, :]
            qk_reads, v_reads, hdma = emit_attn_head(
                P, ab, h, 32, qsb, ksb, vsb, [hkl, hvl], hql, o_ap,
                md_sb, None, hmd, ident_sb, hid, windowed=False,
                transpose_out=True, o_slot=f"st_oh{hb}")
            qf = fences_list(qk_reads)
            qk_free[hb] = qf[0] if qf else None
            vf = fences_list(v_reads)
            v_free[hb] = vf[0] if vf else None
            att0_dma.append(hdma)
            if h == 4:
                hcc_oa = mkcc("ccoa", ot_in_a, ot_ag_a,
                              fences_list(att0_dma[0:4]))
        hcc_ob = mkcc("ccob", ot_in_b, ot_ag_b, fences_list(att0_dma[4:8]))
        ph2 = nocc(P.fence_all())

        hacat = P.dma("gpsimd", acat_sb[:].rearrange("p (k c) -> p k c", k=32),
                      acat.rearrange("(k p) c -> p k c", p=128), ph2)
        hbcat = P.dma("gpsimd", bcat_sb[:], bcat, ph2)
        rt_done = []
        hrt_prev = None
        for t2 in range(2):
            def rtmm(e, t2=t2):
                r = None
                for kc in range(32):
                    r = e.matmul(ps_c[0:48, :],
                                 acat_sb[:, kc * 48:(kc + 1) * 48],
                                 xnA[:, kc * TPC + t2 * 512:
                                     kc * TPC + (t2 + 1) * 512],
                                 start=(kc == 0), stop=(kc == 31))
                return r
            hmm_rt = P.op("tensor", rtmm, [hacat, hrt_prev] + lnf + ph2)
            hev_rt = P.op("vector", lambda e, t2=t2: e.tensor_scalar(
                rt_sb[:, t2 * 512:(t2 + 1) * 512], ps_c[0:48, :], w48_sb[:],
                None, ALU.mult), [hmm_rt, hw48])
            hrt_prev = hev_rt
            rt_done.append(hev_rt)
        hrt = fences_list(rt_done)

        otagr_a = ot_ag_a.ap().rearrange("(m p) q -> p m q", p=128)
        otagr_b = ot_ag_b.ap().rearrange("(m p) q -> p m q", p=128)
        GA = [(m, (m // 4) * 8 + m % 4) for m in range(16)]
        GB = [(m, (m // 4) * 8 + 4 + m % 4) for m in range(16)]
        war_wblk = [None]
        war_ot = [None, None]
        war_psA = [hrt_prev]
        war_psB = [None]
        war_xt = [None, None]
        war_tmp = [None, None]
        war_x1t = [None, None]
        war_g = [None]
        war_bf = [None, None]
        war_bo = [None, None]
        x1_stores = []
        for ob in range(8):
            pb = ob % 2
            hwf_ = P.dma("gpsimd", wblkB[:].rearrange("p (k c) -> p k c", k=32),
                         wft.rearrange("(k p) o -> p k o", p=128)
                         [:, :, ob * 512:(ob + 1) * 512],
                         [war_wblk[0]] + (ph2 if ob == 0 else []))
            hbf = P.dma("sync", bf_t[pb][:],
                        bfr[0:1, ob * 512:(ob + 1) * 512],
                        [war_bf[pb]] + (ph2 if ob < 2 else []))
            g_done = []
            for tt in range(8):
                pi = tt % 2
                pst = ps_a if pi == 0 else ps_b

                def ffnmm(e, tt=tt, ob=ob, pst=pst, pb=pb):
                    r = e.matmul(pst, ones_sb[:], bf_t[pb][:],
                                 start=True, stop=False)
                    for kc in range(32):
                        r = e.matmul(pst,
                                     xnA[:, kc * TPC + tt * 128:
                                         kc * TPC + (tt + 1) * 128],
                                     wblkB[:, kc * 512:(kc + 1) * 512],
                                     start=False, stop=False)
                    r = e.matmul(pst, rt_sb[:, tt * 128:(tt + 1) * 128],
                                 bcat_sb[:, ob * 512:(ob + 1) * 512],
                                 start=False, stop=True)
                    return r
                hmm = P.op("tensor", ffnmm,
                           [hwf_, hbf, hbcat, hones,
                            war_psA[0] if pi == 0 else war_psB[0]]
                           + hrt + lnf)
                hsil = P.op("scalar", lambda e, tt=tt, pst=pst: e.activation(
                    gB[:, tt * 512:(tt + 1) * 512], pst, AF.Silu),
                    [hmm, war_g[0]])
                if pi == 0:
                    war_psA[0] = hsil
                else:
                    war_psB[0] = hsil
                g_done.append(hsil)
            war_bf[pb] = hmm
            hwo_ = P.dma("gpsimd", wblkB[:].rearrange("p (k c) -> p k c", k=32),
                         wot.rearrange("(k p) o -> p k o", p=128)
                         [:, :, ob * 512:(ob + 1) * 512], [hmm])
            hbo = P.dma("sync", bo_t[pb][:],
                        bor[0:1, ob * 512:(ob + 1) * 512],
                        [war_bo[pb]] + (ph2 if ob < 2 else []))
            g_all = fences_list(g_done)
            for tt in range(8):
                bi = tt % 2
                otb = [ot_ttB0, ot_ttB1][bi]

                def ldot_a(e, tt=tt, otb=otb):
                    return e.dma_start(
                        out=otb[:, 0:2048].rearrange("p (m c) -> p m c",
                                                     m=16),
                        in_=otagr_a[:, :,
                                    bass.ds(SS["rk"] * 1024 + 128 * tt, 128)])
                hol_a = P.op("gpsimd", ldot_a, [war_ot[bi], hcc_oa],
                             kind=("d", f"otB{bi}a"))

                def ldot_b(e, tt=tt, otb=otb):
                    return e.dma_start(
                        out=otb[:, 2048:4096].rearrange("p (m c) -> p m c",
                                                        m=16),
                        in_=otagr_b[:, :,
                                    bass.ds(SS["rk"] * 1024 + 128 * tt, 128)])
                hol_b = P.op("gpsimd", ldot_b, [war_ot[bi], hcc_ob],
                             kind=("d", f"otB{bi}b"))
                pi = tt % 2
                pst = ps_a if pi == 0 else ps_b

                def womm_a(e, tt=tt, ob=ob, pst=pst, pb=pb, otb=otb):
                    r = e.matmul(pst, ones_sb[:], bo_t[pb][:],
                                 start=True, stop=False)
                    for m, g in GA:
                        r = e.matmul(pst,
                                     otb[:, m * 128:(m + 1) * 128],
                                     wblkB[:, g * 512:(g + 1) * 512],
                                     start=False, stop=False)
                    return r
                hmm_a = P.op("tensor", womm_a,
                             [hol_a, hwo_, hbo, hones,
                              war_psA[0] if pi == 0 else war_psB[0]])

                def womm_b(e, tt=tt, ob=ob, pst=pst, otb=otb):
                    r = None
                    for i, (m, g) in enumerate(GB):
                        r = e.matmul(pst,
                                     otb[:, 2048 + m * 128:2048 + (m + 1) * 128],
                                     wblkB[:, g * 512:(g + 1) * 512],
                                     start=False, stop=(i == len(GB) - 1))
                    return r
                hmm = P.op("tensor", womm_b, [hol_b, hmm_a])
                war_ot[bi] = hmm
                bi = tt % 2
                hxt = P.dma("sync", [xtB0, xtB1][bi][:],
                            x[tt * 128:(tt + 1) * 128,
                              ob * 512:(ob + 1) * 512],
                            [war_xt[bi]] + (ph2 if ob == 0 else []))
                htm = P.op("vector", lambda e, tt=tt, pst=pst, bi=bi:
                           e.tensor_tensor([tmp0, tmp1][bi][:], pst,
                                           gB[:, tt * 512:(tt + 1) * 512],
                                           ALU.mult),
                           [hmm, war_tmp[bi]] + g_all)
                if pi == 0:
                    war_psA[0] = htm
                else:
                    war_psB[0] = htm
                hx1 = P.op("vector", lambda e, bi=bi: e.tensor_tensor(
                    [x1tB0, x1tB1][bi][:], [tmp0, tmp1][bi][:],
                    [xtB0, xtB1][bi][:], ALU.add),
                    [htm, hxt, war_x1t[bi]])
                war_tmp[bi] = hx1
                war_xt[bi] = hx1
                hst = P.dma("sync", x1_d.ap()[tt * 128:(tt + 1) * 128,
                                              ob * 512:(ob + 1) * 512],
                            [x1tB0, x1tB1][bi][:], [hx1])
                war_x1t[bi] = hst
                x1_stores.append(hst)
            war_bo[pb] = hmm
            war_wblk[0] = hmm
            war_g[0] = fences_list([war_psA[0], war_psB[0]])[0]
        hx1f = fences_list(x1_stores)

        ln2_extra = hx1f + [war_g[0], war_psA[0], war_psB[0], war_bf[0],
                            war_bf[1], war_bo[0], war_bo[1], war_ot[0],
                            war_ot[1], war_x1t[0], war_x1t[1]]
        ln_bufs2 = {"xs": [xsB0, xsB1], "stats": stats,
                    "xn_tm": [xntmB0, xntmB1], "pt_ps": [ptp0, ptp1]}
        ln_done2 = emit_ln(P, ln_bufs2, x1_d.ap(), xnA, ident_sb,
                           ident_h=hid, extra_dep=fences_list(ln2_extra))
        lnf2 = fences_list(ln_done2)

        hcos = P.dma("sync", rcos_sb[:], rcos, ph2)
        hsin = P.dma("sync", rsin_sb[:], rsin, ph2)
        pbufs2 = {"w_fm": [wfmB0, wfmB1], "ps_fm": [ps_a, ps_b],
                  "o_fm": [ofmB0, ofmB1], "rope_a": tmp0, "rope_b": tmp1,
                  "w_tm": [wblkB, wblkB], "ps_tm": [ps_c, ps_d],
                  "v_tm": [vtmB0, vtmB1],
                  "war_w_fm": [None, None],
                  "war_ps_fm": [war_psA[0], war_psB[0]],
                  "war_o_fm": [None, None],
                  "war_w_tm": [war_wblk[0], None],
                  "war_ps_tm": [hrt_prev, None],
                  "war_v_tm": [None, None],
                  "rope_war": [fences_list([war_tmp[0], war_tmp[1]])[0]]}
        hv1 = emit_proj_tm(P, pbufs2, wv1, v1_in.ap(), lnf2, xnA, 1024,
                           tag="v1")
        hcc_v1 = mkcc("ccv1", v1_in, v1_ag, fences_list(hv1))
        hrope = [hcos, hsin]
        hq1 = emit_proj_fm(P, pbufs2, wq1, q1_in.ap(), lnf2, xnA, D,
                           rope=(rcos_sb, rsin_sb), rope_dep=hrope, tag="q1")
        hcc_q1 = mkcc("ccq1", q1_in, q1_ag, fences_list(hq1))
        hk1 = emit_proj_fm(P, pbufs2, wk1, k1_in.ap(), lnf2, xnA, 1024,
                           rope=(rcos_sb, rsin_sb), rope_dep=hrope, tag="k1")
        hcc_k1 = mkcc("cck1", k1_in, k1_ag, fences_list(hk1))
        ph3 = nocc(P.fence_all())

        ab2 = dict(ab)
        ab2["war"] = {}
        q1agr = q1_ag.ap().rearrange("(r q) t -> q r t", r=4)
        k1agr = k1_ag.ap().rearrange("(r q) t -> q r t", r=4)
        v1agr = v1_ag.ap().rearrange("(kc p) c -> p kc c", p=128)
        hkls, hvls = [], []
        for kv in range(2):
            def ldk1(e, kv=kv):
                return e.dma_start(
                    out=[k_b0, k_b1][kv][:].rearrange("p (r t) -> p r t", r=4),
                    in_=k1agr[bass.ds(SS["rk"] * 256 + 128 * kv, 128), :, :])
            hkls.append(P.op("gpsimd", ldk1, [hcc_k1] + ph3,
                             kind=("d", f"kb{kv}")))

            def ldv1(e, kv=kv):
                return e.dma_start(
                    out=[v_b0, v_b1][kv][:].rearrange("p (kc c) -> p kc c",
                                                      kc=32),
                    in_=v1agr[:, :, bass.ds(SS["rk"] * 256 + 128 * kv, 128)])
            hvls.append(P.op("gpsimd", ldv1, [hcc_v1] + ph3,
                             kind=("d", f"vb{kv}")))
        q_free = [None, None]
        att1_dma = []
        for h in range(8):
            hb = h % 2
            kv = h // 4
            qsb = [q_b0, q_b1][hb]

            def ldq1(e, h=h, qsb=qsb):
                return e.dma_start(
                    out=qsb[:].rearrange("p (r t) -> p r t", r=4),
                    in_=q1agr[bass.ds(SS["rk"] * 1024 + 128 * h, 128), :, :])
            hql = P.op("gpsimd", ldq1,
                       [hcc_q1, q_free[hb]] + (ph3 if h < 2 else []),
                       kind=("d", f"qb{hb}"))
            o_t = o1_in_a if h < 4 else o1_in_b
            o_ap = o_t.ap()[(h % 4) * 128:(h % 4 + 1) * 128, :]
            qk_reads, v_reads, hdma = emit_attn_head(
                P, ab2, h, 32, qsb, [k_b0, k_b1][kv], [v_b0, v_b1][kv],
                [hkls[kv], hvls[kv]], hql, o_ap, md_sb, mw_sb, hmd,
                ident_sb, hid, windowed=True, mask_h2=hmw,
                transpose_out=True, o_slot=f"st_oh{hb}")
            qf = fences_list(qk_reads)
            q_free[hb] = qf[0] if qf else None
            att1_dma.append(hdma)
            if h == 4:
                hcc_o1a = mkcc("cco1a", o1_in_a, o1_ag_a,
                               fences_list(att1_dma[0:4]))
        hcc_o1b = mkcc("cco1b", o1_in_b, o1_ag_b, fences_list(att1_dma[4:8]))
        ph4 = nocc(P.fence_all())

        ha1 = P.dma("gpsimd", a1_sb[:].rearrange("p (k c) -> p k c", k=32),
                    a1cat.rearrange("(k p) c -> p k c", p=128), ph4)
        hb1 = P.dma("gpsimd", b1_sb[:], b1cat, ph4)
        hwr = P.dma("gpsimd", wr_sb[:].rearrange("p (k c) -> p k c", k=32),
                    wrt.rearrange("(k p) c -> p k c", p=128), ph4)
        he8 = P.dma("gpsimd", e8_sb[:], e8, ph4)
        hbr = P.dma("sync", br_sb[:], brr, ph4)
        war5 = {}
        lows_done = []
        ps_t8 = ptp0[0:8, 0:128]
        for tt in range(8):
            def logmm(e, tt=tt):
                r = e.matmul(ps_a[:, 0:8], ones_sb[:], br_sb[:],
                             start=True, stop=False)
                for kc in range(32):
                    r = e.matmul(ps_a[:, 0:8],
                                 xnA[:, kc * TPC + tt * 128:
                                     kc * TPC + (tt + 1) * 128],
                                 wr_sb[:, kc * 8:(kc + 1) * 8],
                                 start=False, stop=(kc == 31))
                return r
            hlg = P.op("tensor", logmm,
                       [hwr, hbr, hones, hid, war5.get("ps_a")] + ph4)
            hmx = P.op("vector", lambda e: e.reduce_max(
                negm8[:], ps_a[:, 0:8], axis=AX.X, negate=True),
                [hlg, war5.get("negm8")])
            hex_ = P.op("scalar", lambda e: e.activation(
                p8_sb[:], ps_a[:, 0:8], AF.Exp, bias=negm8[:],
                accum_out=z8[:]), [hmx, war5.get("p8")])
            war5["negm8"] = hex_
            hrz = P.op("vector", lambda e: e.reciprocal(rz8[:], z8[:]),
                       [hex_, war5.get("rz8")])
            hrz2 = P.op("vector", lambda e: e.tensor_scalar(
                rz8[:], rz8[:], 8.0, None, ALU.mult), [hrz])
            hsc = P.op("vector", lambda e: e.tensor_scalar(
                sc8[:], p8_sb[:], rz8[:], None, ALU.mult),
                [hrz2, war5.get("sc8")])
            war5["p8"] = hsc
            war5["rz8"] = hsc
            htr = P.op("tensor", lambda e: e.transpose(
                ps_t8, sc8[:], ident_sb[:]), [hsc, hid, war5.get("ps_t8")])
            war5["sc8"] = htr
            hev = P.op("scalar", lambda e: e.copy(sct_sb[:], ps_t8),
                       [htr, war5.get("sct")])
            war5["ps_t8"] = hev
            hsb = P.op("tensor", lambda e: e.matmul(
                ps_b[:, 0:128], e8_sb[:], sct_sb[:],
                start=True, stop=True), [hev, he8, war5.get("ps_b")])
            war5["sct"] = hsb
            war5["ps_a"] = hsb
            hscb = P.op("vector", lambda e: e.tensor_copy(
                scb_sb[:], ps_b[:, 0:128]), [hsb, war5.get("scb")])
            war5["ps_b"] = hscb

            def lowmm(e, tt=tt):
                r = None
                for kc in range(32):
                    r = e.matmul(ps_c[:, 0:128],
                                 a1_sb[:, kc * 128:(kc + 1) * 128],
                                 xnA[:, kc * TPC + tt * 128:
                                     kc * TPC + (tt + 1) * 128],
                                 start=(kc == 0), stop=(kc == 31))
                return r
            hlo = P.op("tensor", lowmm, [ha1, war5.get("ps_c")])
            hls = P.op("vector", lambda e, tt=tt: e.tensor_tensor(
                lows_sb[:, tt * 128:(tt + 1) * 128], ps_c[:, 0:128],
                scb_sb[:], ALU.mult), [hlo, hscb])
            war5["ps_c"] = hls
            war5["scb"] = hls
            lows_done.append(hls)
        hlows = fences_list(lows_done)

        o1agr_a = o1_ag_a.ap().rearrange("(m p) q -> p m q", p=128)
        o1agr_b = o1_ag_b.ap().rearrange("(m p) q -> p m q", p=128)
        war_wblk5 = [None]
        war_ot5 = [None, None]
        war_ps5 = [war5["ps_a"], war5["ps_b"]]
        war_xt5 = [None, None]
        war_tmp5 = [None, None]
        war_o5 = [None, None]
        war_g5 = [None]
        out_stores = []
        for ob in range(8):
            hwo_ = P.dma("gpsimd", wblkC[:].rearrange("p (k c) -> p k c", k=32),
                         wo1.rearrange("(k p) o -> p k o", p=128)
                         [:, :, ob * 512:(ob + 1) * 512],
                         [war_wblk5[0]] + (ph4 if ob == 0 else []))
            g_done = []
            for tt in range(8):
                pi = tt % 2
                pst = ps_a if pi == 0 else ps_b
                hg = P.op("tensor", lambda e, tt=tt, ob=ob, pst=pst: e.matmul(
                    pst, lows_sb[:, tt * 128:(tt + 1) * 128],
                    b1_sb[:, ob * 512:(ob + 1) * 512],
                    start=True, stop=True),
                    hlows + [hb1, war_ps5[pi]])
                hsil = P.op("scalar", lambda e, tt=tt, pst=pst: e.activation(
                    gC[:, tt * 512:(tt + 1) * 512], pst, AF.Silu,
                    scale=2.0), [hg, war_g5[0]])
                war_ps5[pi] = hsil
                g_done.append(hsil)
            g_all = fences_list(g_done)
            for tt in range(8):
                bi = tt % 2
                otc = [ot_ttC0, ot_ttC1][bi]

                def ldo1_a(e, tt=tt, otc=otc):
                    return e.dma_start(
                        out=otc[:, 0:2048].rearrange("p (m c) -> p m c",
                                                     m=16),
                        in_=o1agr_a[:, :,
                                    bass.ds(SS["rk"] * 1024 + 128 * tt, 128)])
                hol_a = P.op("gpsimd", ldo1_a, [war_ot5[bi], hcc_o1a],
                             kind=("d", f"otC{bi}a"))

                def ldo1_b(e, tt=tt, otc=otc):
                    return e.dma_start(
                        out=otc[:, 2048:4096].rearrange("p (m c) -> p m c",
                                                        m=16),
                        in_=o1agr_b[:, :,
                                    bass.ds(SS["rk"] * 1024 + 128 * tt, 128)])
                hol_b = P.op("gpsimd", ldo1_b, [war_ot5[bi], hcc_o1b],
                             kind=("d", f"otC{bi}b"))
                pi = tt % 2
                pst = ps_a if pi == 0 else ps_b

                def womm5_a(e, tt=tt, ob=ob, pst=pst, otc=otc):
                    r = None
                    for i, (m, g) in enumerate(GA):
                        r = e.matmul(pst,
                                     otc[:, m * 128:(m + 1) * 128],
                                     wblkC[:, g * 512:(g + 1) * 512],
                                     start=(i == 0), stop=False)
                    return r
                hmm_a = P.op("tensor", womm5_a, [hol_a, hwo_, war_ps5[pi]])

                def womm5_b(e, tt=tt, ob=ob, pst=pst, otc=otc):
                    r = None
                    for i, (m, g) in enumerate(GB):
                        r = e.matmul(pst,
                                     otc[:, 2048 + m * 128:2048 + (m + 1) * 128],
                                     wblkC[:, g * 512:(g + 1) * 512],
                                     start=False, stop=(i == len(GB) - 1))
                    return r
                hmm = P.op("tensor", womm5_b, [hol_b, hmm_a])
                war_ot5[bi] = hmm
                bi = tt % 2
                hxt = P.dma("sync", [xtC0, xtC1][bi][:],
                            x1_d.ap()[tt * 128:(tt + 1) * 128,
                                      ob * 512:(ob + 1) * 512],
                            [war_xt5[bi]] + hx1f + (ph4 if ob == 0 else []))
                htm = P.op("vector", lambda e, tt=tt, pst=pst, bi=bi:
                           e.tensor_tensor([tmpC0, tmpC1][bi][:], pst,
                                           gC[:, tt * 512:(tt + 1) * 512],
                                           ALU.mult),
                           [hmm, war_tmp5[bi]] + g_all)
                war_ps5[pi] = htm
                hx1 = P.op("vector", lambda e, bi=bi: e.tensor_tensor(
                    [x1tC0, x1tC1][bi][:], [tmpC0, tmpC1][bi][:],
                    [xtC0, xtC1][bi][:], ALU.add),
                    [htm, hxt, war_o5[bi]])
                war_tmp5[bi] = hx1
                war_xt5[bi] = hx1
                hst = P.dma("sync", out[tt * 128:(tt + 1) * 128,
                                        ob * 512:(ob + 1) * 512],
                            [x1tC0, x1tC1][bi][:], [hx1])
                war_o5[bi] = hst
                out_stores.append(hst)
            war_wblk5[0] = hmm
            war_g5[0] = fences_list([war_ps5[0], war_ps5[1]])[0]
        P.barrier("sync", fences_list(out_stores))
        block = ctx.enter_context(nc.Block())
        P.emit(ctx, block)
    return nc

def kernel(L0_attn_x, L0_attn_Wq, L0_attn_bq, L0_attn_Wk, L0_attn_bk,
           L0_attn_Wv, L0_attn_bv, L0_attn_Wo, L0_attn_bo, L0_ffn_weight,
           L0_ffn_bias, L0_ffn_lora_A, L0_ffn_lora_B,
           L0_ffn_lora_start_indices, L1_attn_wq, L1_attn_wk, L1_attn_wv,
           L1_attn_wo, L1_ffn_A, L1_ffn_B, L1_ffn_Wr, L1_ffn_br):
    f32 = np.float32
    x = np.asarray(L0_attn_x, dtype=f32)
    Wq = np.asarray(L0_attn_Wq, dtype=f32); bq = np.asarray(L0_attn_bq, dtype=f32)
    Wk = np.asarray(L0_attn_Wk, dtype=f32); bk = np.asarray(L0_attn_bk, dtype=f32)
    Wv = np.asarray(L0_attn_Wv, dtype=f32); bv = np.asarray(L0_attn_bv, dtype=f32)
    Wo = np.asarray(L0_attn_Wo, dtype=f32); bo = np.asarray(L0_attn_bo, dtype=f32)
    Wf = np.asarray(L0_ffn_weight, dtype=f32)
    bfn = np.asarray(L0_ffn_bias, dtype=f32)
    lA = np.asarray(L0_ffn_lora_A, dtype=f32)
    lB = np.asarray(L0_ffn_lora_B, dtype=f32)
    lsi = np.asarray(L0_ffn_lora_start_indices).astype(np.int32).reshape(1, 4)
    wq1 = np.asarray(L1_attn_wq, dtype=f32)
    wk1 = np.asarray(L1_attn_wk, dtype=f32)
    wv1 = np.asarray(L1_attn_wv, dtype=f32)
    wo1 = np.asarray(L1_attn_wo, dtype=f32)
    A1 = np.asarray(L1_ffn_A, dtype=f32)
    B1 = np.asarray(L1_ffn_B, dtype=f32)
    Wr = np.asarray(L1_ffn_Wr, dtype=f32)
    br = np.asarray(L1_ffn_br, dtype=f32)

    mask_diag, mask_win = make_masks()
    ident = np.eye(128, dtype=f32)
    e8 = np.zeros((8, 128), dtype=f32)
    for h in range(8):
        e8[h, h * 16:(h + 1) * 16] = 1.0
    C = lambda a: np.ascontiguousarray(a)

    shared = {
        "wqt": C(Wq.T), "wkt": C(Wk.T), "wvt": C(Wv.T),
        "bqt": C(bq.reshape(32, 128).T), "bkt": C(bk.reshape(32, 128).T),
        "bvr": C(bv.reshape(1, -1)),
        "wot": C(Wo.T), "bor": C(bo.reshape(1, -1)),
        "wft": C(Wf.T), "bfr": C(bfn.reshape(1, -1)),
        "acat": C(lA.transpose(2, 0, 1).reshape(D, 48)),
        "bcat": C(lB.transpose(0, 2, 1).reshape(48, D)),
        "lsi": lsi,
        "wq1": wq1, "wk1": wk1, "wv1": wv1,
        "wo1": wo1,
        "a1cat": C(A1.transpose(2, 0, 1).reshape(D, 128)),
        "b1cat": C(B1.transpose(0, 2, 1).reshape(128, D)),
        "wrt": C(Wr.T), "brr": C(br.reshape(1, 8)),
        "e8": e8, "mdg": mask_diag, "mwn": mask_win, "identity": ident,
    }
    im = []
    for c in range(8):
        b, r = c // 4, c % 4
        pos = np.arange(r * TPC, (r + 1) * TPC)
        rcos, rsin = make_rope_tables(pos, 1.0)
        d = dict(shared)
        d["x"] = C(x[b, r * TPC:(r + 1) * TPC, :])
        d["bsel"] = np.array([[float(b)]], dtype=f32)
        d["rank"] = np.array([[r]], dtype=np.int32)
        d["rcos"] = rcos
        d["rsin"] = rsin
        im.append(d)
    r5 = _run(build_fused(), im)
    out = np.empty((2, S, D), dtype=f32)
    for c in range(8):
        b, r = c // 4, c % 4
        out[b, r * TPC:(r + 1) * TPC, :] = r5[c]["out"]
    return out



# version 06
# speedup vs baseline: 1.3183x; 1.2713x over previous
import numpy as np
import ml_dtypes
from contextlib import ExitStack

import concourse.bass as bass
import concourse.mybir as mybir
from concourse.bass_utils import run_bass_kernel_spmd

F32 = mybir.dt.float32
BF16 = mybir.dt.bfloat16
I32 = mybir.dt.int32
AF = mybir.ActivationFunctionType
ALU = mybir.AluOpType
AX = mybir.AxisListType

D = 4096
S = 4096
TPC = 1024
NEG = -1.0e30
EPS = 1e-5
ASCALE = 1.0 / np.sqrt(128.0)

ENGINES = ("sync", "scalar", "vector", "tensor", "gpsimd")



class H:
    __slots__ = ("eng", "kind", "val")

    def __init__(self, eng, kind, val):
        self.eng, self.kind, self.val = eng, kind, val


class Plan:

    def __init__(self, nc):
        self.nc = nc
        self.q = {e: [] for e in ENGINES}
        self.cnt = {}
        for e in ENGINES:
            self.cnt[(e, "c")] = 0
        self.settle_fn = None

    def op(self, eng, fn, deps=(), kind="c"):
        amt = 16 if (isinstance(kind, tuple) and kind[0] == "d") else 1
        key = (eng, kind)
        self.cnt[key] = self.cnt.get(key, 0) + amt
        h = H(eng, kind, self.cnt[key])
        self.q[eng].append((fn, tuple(d for d in deps if d is not None), kind))
        return h

    def fence_all(self):
        return [H(e, k, v) for (e, k), v in self.cnt.items() if v > 0]

    def barrier(self, eng, deps):
        self.q[eng].append((lambda e: None,
                            tuple(d for d in deps if d is not None), "c"))

    def raw(self, eng, fn, deps=()):
        self.q[eng].append((lambda e, f=fn: (f(e), None)[1],
                            tuple(d for d in deps if d is not None), "c"))

    def dma(self, eng, out, in_, deps=(), slot=None):
        if slot is None:
            to = out.tensor if hasattr(out, "tensor") else out
            if type(to).__name__ == "DRamTensorHandle":
                ti = in_.tensor if hasattr(in_, "tensor") else in_
                slot = "st_" + getattr(ti, "name", "anon")
            else:
                slot = getattr(to, "name", "anon")
        return self.op(eng, lambda e, o=out, i=in_: e.dma_start(out=o, in_=i),
                       deps, kind=("d", slot))

    def emit(self, ctx, block):
        sems = {}
        for (e, k), v in self.cnt.items():
            if v > 0:
                nm = f"sem_{e}_{k if isinstance(k, str) else 'd_' + k[1]}"
                sems[(e, k)] = ctx.enter_context(self.nc.semaphore(nm))

        def mk(e):
            def section(eng):
                waited = {}
                for fn, deps, kind in self.q[e]:
                    for dd in deps:
                        key = (dd.eng, dd.kind)
                        if waited.get(key, 0) >= dd.val:
                            continue
                        eng.wait_ge(sems[key], dd.val)
                        waited[key] = dd.val
                    inst = fn(eng)
                    if inst is not None:
                        if e == "tensor" and self.settle_fn is not None:
                            inst = self.settle_fn(eng)
                        inst.then_inc(
                            sems[(e, kind)],
                            16 if (isinstance(kind, tuple) and kind[0] == "d")
                            else 1)
            return section

        for e in ENGINES:
            if self.q[e]:
                getattr(block, e)(mk(e))
        return sems



def make_masks():
    i = np.arange(128)[:, None]
    j = np.arange(128)[None, :]
    mask_diag = np.where(j <= i, 0.0, NEG).astype(np.float32)
    mask_win = np.where(j > i, 0.0, NEG).astype(np.float32)
    return mask_diag, mask_win


def make_rope_tables(pos, scale):
    half = 64
    inv_freq = 10000.0 ** (-(np.arange(half, dtype=np.float64) / half))
    ang = pos[:, None].astype(np.float64) * inv_freq[None, :]
    cos = np.cos(ang).T
    sin = np.sin(ang).T
    cosf = np.concatenate([cos, cos], 0) * scale
    sinf = np.concatenate([sin, -sin], 0) * scale
    return cosf.astype(np.float32), sinf.astype(np.float32)



def alloc_stats(nc, ctx, tag):
    names = ["ssq", "ssum", "mean", "tmp", "vpe", "s0", "r0", "t2", "s1",
             "rstd", "nmr"]
    return {n: ctx.enter_context(nc.sbuf_tensor(f"{tag}_{n}", [128, 1], F32))
            for n in names}


def emit_ln(P, bufs, x_dram, xnt_sb, ident_sb, ident_h=None, extra_dep=()):
    ntiles = TPC // 128
    xs = bufs["xs"]
    st = bufs["stats"]
    xn_tm = bufs["xn_tm"]
    pt_ps = bufs["pt_ps"]
    ident = ident_sb
    last_xs = [None, None]
    last_tm = [None, None]
    last_pt = [None, None]
    prev_stats = None
    done = []
    for t in range(ntiles):
        bi = t % 2
        dep = list(extra_dep) + [last_xs[bi]]
        hld = P.dma("sync", xs[bi][:], x_dram[t * 128:(t + 1) * 128, :], dep)
        hsq = P.op("scalar", lambda e, b=bi: e.activation(
            xn_tm[b][:], xs[b][:], AF.Square, accum_out=st["ssq"][:]),
            [hld, prev_stats, last_tm[bi]])
        hsum = P.op("vector", lambda e, b=bi: e.reduce_sum(
            st["ssum"][:], xs[b][:], axis=AX.X), [hld, prev_stats])
        hmean = P.op("vector", lambda e: e.tensor_scalar(
            st["mean"][:], st["ssum"][:], 1.0 / D, None, ALU.mult), [hsum])
        htmp = P.op("vector", lambda e: e.tensor_scalar(
            st["tmp"][:], st["mean"][:], st["mean"][:], EPS,
            ALU.mult, ALU.subtract), [hmean])
        hvpe = P.op("vector", lambda e: e.tensor_scalar(
            st["vpe"][:], st["ssq"][:], 1.0 / D, st["tmp"][:],
            ALU.mult, ALU.subtract), [htmp, hsq])
        hs0 = P.op("scalar", lambda e: e.activation(
            st["s0"][:], st["vpe"][:], AF.Sqrt), [hvpe])
        hr0 = P.op("vector", lambda e: e.reciprocal(st["r0"][:], st["s0"][:]),
                   [hs0])
        ht2 = P.op("vector", lambda e: e.tensor_scalar(
            st["t2"][:], st["vpe"][:], st["r0"][:], None, ALU.mult),
            [hr0])
        hs1 = P.op("vector", lambda e: e.tensor_scalar(
            st["s1"][:], st["s0"][:], st["t2"][:], 0.5, ALU.add, ALU.mult),
            [ht2])
        hrstd = P.op("vector", lambda e: e.reciprocal(
            st["rstd"][:], st["s1"][:]), [hs1])
        hnmr = P.op("vector", lambda e: e.tensor_scalar(
            st["nmr"][:], st["mean"][:], st["rstd"][:], -1.0,
            ALU.mult, ALU.mult), [hrstd])
        hxn = P.op("scalar", lambda e, b=bi: e.activation(
            xn_tm[b][:], xs[b][:], AF.Identity, bias=st["nmr"][:],
            scale=st["rstd"][:]), [hnmr, last_tm[bi]])
        prev_stats = hxn
        last_xs[bi] = hxn
        for g in range(4):
            pi = (t * 4 + g) % 2

            def trg(e, b=bi, g=g, pi=pi):
                r = None
                for c in range(8):
                    kc = g * 8 + c
                    r = e.transpose(pt_ps[pi][:, c * 128:(c + 1) * 128],
                                    xn_tm[b][:, kc * 128:(kc + 1) * 128],
                                    ident[:])
                return r
            htr = P.op("tensor", trg, [hxn, last_pt[pi], ident_h])

            def evg(e, t=t, g=g, pi=pi):
                r = None
                for c in range(8):
                    kc = g * 8 + c
                    dst = xnt_sb[:, kc * TPC + t * 128: kc * TPC + (t + 1) * 128]
                    src = pt_ps[pi][:, c * 128:(c + 1) * 128]
                    r = e.tensor_copy(dst, src) if hasattr(e, "tensor_copy") \
                        else e.copy(dst, src)
                return r
            hev = P.op("vector" if g % 2 == 0 else "scalar", evg, [htr])
            last_pt[pi] = hev
            last_tm[bi] = hev
            done.append(hev)
    return done


def emit_proj_fm(P, bufs, wt_dram, out_dram, xnt_done, xnt_sb, n_out,
                 bias_sb=None, bias_h=None, scale=1.0, rope=None,
                 rope_dep=None, tag=""):
    nc = P.nc
    w_sb = bufs["w_fm"]
    ps = bufs["ps_fm"]
    ob = bufs["o_fm"]
    ra = bufs.get("rope_a")
    rb = bufs.get("rope_b")
    if "rope_war" not in bufs:
        bufs["rope_war"] = [None]
    last_w = bufs.setdefault("war_w_fm", [None, None])
    last_ps = bufs.setdefault("war_ps_fm", [None, None])
    last_ob = bufs.setdefault("war_o_fm", [None, None])
    single_w = len(w_sb) > 1 and (w_sb[0] is w_sb[1])
    nchunk = n_out // 128
    out_handles = []
    for oc in range(nchunk):
        wb = 0 if single_w else oc % 2
        hw = P.dma("gpsimd", w_sb[wb][:].rearrange("p (k c) -> p k c", k=32),
                   wt_dram.rearrange("(k p) o -> p k o", p=128)
                   [:, :, oc * 128:(oc + 1) * 128],
                   [last_w[wb]])
        tile_done = []
        for tt in range(TPC // 512):
            pi = (oc * (TPC // 512) + tt) % 2

            def mmg(e, wb=wb, tt=tt, pi=pi):
                r = None
                for kc in range(32):
                    r = e.matmul(ps[pi][:],
                                 w_sb[wb][:, kc * 128:(kc + 1) * 128],
                                 xnt_sb[:, kc * TPC + tt * 512:
                                        kc * TPC + (tt + 1) * 512],
                                 start=(kc == 0), stop=(kc == 31))
                return r
            hmm = P.op("tensor", mmg,
                       fences_list(xnt_done) + [hw, last_ps[pi]])
            if rope is None:
                bias_ap = (bias_sb[:, oc:oc + 1]
                           if bias_sb is not None else 0.0)
                deps = [hmm, last_ob[wb]]
                if bias_h is not None:
                    deps.append(bias_h)

                def ev(e, wb=wb, tt=tt, pi=pi, bias_ap=bias_ap):
                    return e.activation(
                        ob[wb][:, tt * 512:(tt + 1) * 512], ps[pi][:],
                        AF.Identity, bias=bias_ap, scale=scale)
                hev = P.op("scalar", ev, deps)
                last_ps[pi] = hev
                tile_done.append(hev)
            else:
                cos_sb, sin_sb = rope
                c_ap = cos_sb[:, tt * 512:(tt + 1) * 512]
                sL = sin_sb[0:64, tt * 512:(tt + 1) * 512]
                sH = sin_sb[64:128, tt * 512:(tt + 1) * 512]
                rw = bufs.get("rope_war")

                def r1(e, pi=pi, c_ap=c_ap):
                    return e.tensor_tensor(ra[:], ps[pi][:], c_ap, ALU.mult)
                h1 = P.op("vector", r1, [hmm, rw[0]] + list(rope_dep or []))

                def r2(e, pi=pi, sH=sH):
                    return e.tensor_tensor(rb[0:64, :], ps[pi][64:128, :], sH,
                                           ALU.mult)
                h2 = P.op("vector", r2, [hmm, rw[0]] + list(rope_dep or []))

                def r3(e, pi=pi, sL=sL):
                    return e.tensor_tensor(rb[64:128, :], ps[pi][0:64, :], sL,
                                           ALU.mult)
                h3 = P.op("vector", r3, [hmm, rw[0]] + list(rope_dep or []))

                def r4(e, wb=wb, tt=tt):
                    return e.tensor_tensor(ob[wb][:, tt * 512:(tt + 1) * 512],
                                           ra[:], rb[:], ALU.add)
                h4 = P.op("vector", r4, [h1, h2, h3, last_ob[wb]])
                rw[0] = h4
                last_ps[pi] = h4
                tile_done.append(h4)
        hst = P.dma("sync", out_dram[oc * 128:(oc + 1) * 128, :], ob[wb][:],
                    tile_done)
        last_ob[wb] = hst
        last_w[wb] = hmm
        out_handles.append(hst)
    return out_handles


def fences_list(handles):
    best = {}
    for h in handles:
        if h is None:
            continue
        k = (h.eng, h.kind)
        if k not in best or best[k].val < h.val:
            best[k] = h
    return list(best.values())


def emit_proj_tm(P, bufs, w_dram, out_dram, xnt_done, xnt_sb, n_out,
                 bias_row_sb=None, bias_h=None, ones_sb=None, ones_h=None,
                 bias_dram=None, tag="", W=512):
    w_sb = bufs["w_tm"]
    single_w = w_sb[0] is w_sb[1] if len(w_sb) > 1 else True
    ps = bufs["ps_tm"]
    vb = bufs["v_tm"]
    b_tm = bufs.get("b_tm")
    last_w = bufs.setdefault("war_w_tm", [None, None])
    last_ps = bufs.setdefault("war_ps_tm", [None, None])
    last_vb = bufs.setdefault("war_v_tm", [None, None])
    last_b = bufs.setdefault("war_b_tm", [None, None])
    out_handles = []
    nblk = n_out // W
    for obk in range(nblk):
        wb = 0 if single_w else obk % 2
        hw = P.dma("gpsimd", w_sb[wb][:].rearrange("p (k c) -> p k c", k=32),
                   w_dram.rearrange("(k p) o -> p k o", p=128)
                   [:, :, obk * W:(obk + 1) * W],
                   [last_w[wb]])
        hb = None
        if bias_dram is not None:
            hb = P.dma("sync", b_tm[obk % 2][0:1, 0:W],
                       bias_dram[0:1, obk * W:(obk + 1) * W],
                       [last_b[obk % 2]])
        for tt in range(TPC // 128):
            pi = (obk * (TPC // 128) + tt) % 2

            def mmg(e, wb=wb, tt=tt, pi=pi, obk=obk):
                r = None
                if bias_dram is not None:
                    r = e.matmul(ps[pi][:, 0:W], ones_sb[:],
                                 b_tm[obk % 2][0:1, 0:W],
                                 start=True, stop=False)
                elif bias_row_sb is not None:
                    r = e.matmul(ps[pi][:, 0:W], ones_sb[:],
                                 bias_row_sb[:, obk * W:(obk + 1) * W],
                                 start=True, stop=False)
                has_bias = bias_dram is not None or bias_row_sb is not None
                for kc in range(32):
                    r = e.matmul(ps[pi][:, 0:W],
                                 xnt_sb[:, kc * TPC + tt * 128:
                                        kc * TPC + (tt + 1) * 128],
                                 w_sb[wb][:, kc * W:(kc + 1) * W],
                                 start=(kc == 0 and not has_bias),
                                 stop=(kc == 31))
                return r
            deps = fences_list(xnt_done) + [hw, last_ps[pi], hb]
            if bias_h is not None:
                deps += [bias_h, ones_h]
            if bias_dram is not None:
                deps += [ones_h]
            hmm = P.op("tensor", mmg, deps)
            bj = tt % 2

            def ev(e, pi=pi, bj=bj):
                return e.tensor_copy(vb[bj][:, 0:W], ps[pi][:, 0:W]) \
                    if hasattr(e, "tensor_copy") \
                    else e.copy(vb[bj][:, 0:W], ps[pi][:, 0:W])
            hev = P.op("vector" if tt % 2 == 0 else "scalar", ev,
                       [hmm, last_vb[bj]])
            last_ps[pi] = hev
            hst = P.dma("sync",
                        out_dram[tt * 128:(tt + 1) * 128,
                                 obk * W:(obk + 1) * W],
                        vb[bj][:, 0:W], [hev])
            last_vb[bj] = hst
            out_handles.append(hst)
        last_w[wb] = hmm
        if hb is not None:
            last_b[obk % 2] = hmm
    return out_handles





def alloc_attn_bufs(nc, ctx, tag=""):
    sb = lambda n, s, d: ctx.enter_context(nc.sbuf_tensor(tag + n, s, d))
    psum = lambda n, s, d: ctx.enter_context(nc.psum_tensor(tag + n, s, d))
    names = ["negm", "negm2", "dtmp", "alpha", "zz", "zp", "rz", "negmsc"]
    fl = {n: sb("fl_" + n, [128, 1], F32) for n in names}
    return {
        "fl": fl,
        "oacc": sb("oacc", [128, 128], F32),
        "p_sb": [sb("p0", [128, 2048], BF16), sb("p1", [128, 2048], BF16)],
        "pt_sb": [sb("pt0", [128, 1024], BF16), sb("pt1", [128, 1024], BF16)],
        "ohead": [sb("oh0", [128, 32 * 128], BF16),
                  sb("oh1", [128, 32 * 128], BF16)],
        "s_ps": psum("s_ps", [128, 2048], F32),
        "pt_ps": [psum("ptp0", [128, 1024], BF16),
                  psum("ptp1", [128, 1024], BF16)],
        "o_ps": psum("o_ps", [128, 512], F32),
        "war": {},
    }


def _s_matmuls(e, ab, q_sb, k_sb, j, kstart, p0, chunks):
    r = None
    for (c0, cw) in chunks:
        r = e.matmul(ab["s_ps"][:, c0:c0 + cw],
                     q_sb[:, j * 128:(j + 1) * 128],
                     k_sb[:, kstart + p0 + c0: kstart + p0 + c0 + cw],
                     start=True, stop=True)
    return r


def _tr_group(e, ab, pp, gp, g0, nch, ident_sb):
    r = None
    for c in range(nch):
        r = e.transpose(ab["pt_ps"][gp][:, c * 128:(c + 1) * 128],
                        ab["p_sb"][pp][:, (g0 + c) * 128:(g0 + c + 1) * 128],
                        ident_sb[:])
    return r


def _av_group(e, ab, v_sb, gp, g0, nch, kbase_c, first, lastg):
    r = None
    for c in range(nch):
        kc = kbase_c + g0 + c
        r = e.matmul(ab["o_ps"][:, 0:128],
                     ab["pt_sb"][gp][:, c * 128:(c + 1) * 128],
                     v_sb[:, kc * 128:(kc + 1) * 128],
                     start=(first and c == 0),
                     stop=(lastg and c == nch - 1))
    return r


def emit_attn_head(P, ab, h, n_jtiles, q_sb, k_sb, v_sb, kv_dep, q_dep,
                   o_dram_ap, mask_diag_sb, mask_win_sb, mask_h, ident_sb,
                   ident_h, windowed, sscale=ASCALE, mask_h2=None,
                   transpose_out=False, o_store_eng="sync", o_slot=None):
    fl = ab["fl"]
    oacc = ab["oacc"]
    war = ab["war"]
    hp = h % 2
    ohead = ab["ohead"][hp]
    qk_reads = []
    v_reads = []
    for j in range(n_jtiles):
        Q0 = j * 128
        kend = Q0 + 128
        kstart = max(0, kend - 1152) if windowed else 0
        kn = kend - kstart
        pbounds = list(range(0, kn, 2048)) + [kn]
        npass = len(pbounds) - 1
        hZ = None
        hO = None
        halpha = None
        for ip in range(npass):
            first = ip == 0
            lastp = ip == npass - 1
            p0, p1 = pbounds[ip], pbounds[ip + 1]
            w = p1 - p0
            chunks = []
            c0 = 0
            while c0 < w:
                cw = min(512, w - c0)
                chunks.append((c0, cw))
                c0 += cw
            hs = P.op("tensor",
                      (lambda e, j=j, ks=kstart, p0=p0, ch=tuple(chunks):
                       _s_matmuls(e, ab, q_sb, k_sb, j, ks, p0, ch)),
                      [war.get("s_ps_free"), q_dep, ident_h] + list(kv_dep))
            qk_reads.append(hs)
            deps_red = [hs]
            if lastp:
                hmd = P.op("vector", lambda e, w=w: e.tensor_tensor(
                    ab["s_ps"][:, w - 128:w], ab["s_ps"][:, w - 128:w],
                    mask_diag_sb[:], ALU.add), [hs, mask_h])
                deps_red.append(hmd)
            if first and windowed and kstart > 0:
                hmw = P.op("vector", lambda e: e.tensor_tensor(
                    ab["s_ps"][:, 0:128], ab["s_ps"][:, 0:128],
                    mask_win_sb[:], ALU.add), [hs, mask_h2 or mask_h])
                deps_red.append(hmw)
            hred = P.op("vector", lambda e, w=w: e.reduce_max(
                fl["negm2"][:], ab["s_ps"][:, 0:w], axis=AX.X, negate=True),
                deps_red + [war.get("negm2_free")])
            if first:
                halpha = None
                hngm = P.op("vector", lambda e: e.tensor_copy(
                    fl["negm"][:], fl["negm2"][:]),
                    [hred, war.get("negm_free")])
            else:
                hdm = P.op("vector", lambda e: e.tensor_tensor(
                    fl["dtmp"][:], fl["negm"][:], fl["negm2"][:], ALU.min),
                    [hred, war.get("dtmp_free")])
                hdf = P.op("vector", lambda e: e.tensor_tensor(
                    fl["negm2"][:], fl["dtmp"][:], fl["negm"][:],
                    ALU.subtract), [hdm])
                halpha = P.op("scalar", lambda e: e.activation(
                    fl["alpha"][:], fl["negm2"][:], AF.Exp, scale=sscale),
                    [hdf, war.get("alpha_free")])
                hngm = P.op("vector", lambda e: e.tensor_copy(
                    fl["negm"][:], fl["dtmp"][:]),
                    [hdf, war.get("negm_free")])
                war["dtmp_free"] = hngm
            war["negm2_free"] = halpha or hngm
            pp = war.setdefault("pp_ctr", [0])[0] % 2
            war["pp_ctr"][0] += 1
            hnsc = P.op("vector", lambda e: e.tensor_scalar(
                fl["negmsc"][:], fl["negm"][:], sscale, None, ALU.mult),
                [hngm, war.get("negmsc_free")])
            hexp = P.op("scalar", lambda e, pp=pp, w=w: e.activation(
                ab["p_sb"][pp][:, 0:w], ab["s_ps"][:, 0:w], AF.Exp,
                scale=sscale, bias=fl["negmsc"][:], accum_out=fl["zp"][:]),
                [hnsc, halpha, war.get(("p_free", pp)), war.get("zp_free")])
            war["negmsc_free"] = hexp
            war["s_ps_free"] = hexp
            war["negm_free"] = hexp
            if first:
                hZ = P.op("vector", lambda e: e.tensor_copy(
                    fl["zz"][:], fl["zp"][:]), [hexp, war.get("zz_free")])
            else:
                hZ = P.op("vector", lambda e: e.scalar_tensor_tensor(
                    fl["zz"][:], fl["zz"][:], fl["alpha"][:], fl["zp"][:],
                    ALU.mult, ALU.add), [hexp, hZ, halpha])
            war["zp_free"] = hZ
            nkc = w // 128
            kbase_c = (kstart + p0) // 128
            hav = None
            htr_last = None
            first_av = True
            for g0 in range(0, nkc, 8):
                nch = min(8, nkc - g0)
                lastg = g0 + 8 >= nkc
                gp = war.setdefault("gp_ctr", [0])[0] % 2
                war["gp_ctr"][0] += 1
                htr = P.op("tensor",
                           (lambda e, pp=pp, gp=gp, g0=g0, nch=nch:
                            _tr_group(e, ab, pp, gp, g0, nch, ident_sb)),
                           [hexp, ident_h, war.get(("ptps_free", gp))])
                htr_last = htr
                hev = P.op("vector" if (g0 // 8) % 2 == 0 else "scalar",
                           (lambda e, gp=gp, nch=nch:
                            (e.tensor_copy if hasattr(e, "tensor_copy")
                             else e.copy)(ab["pt_sb"][gp][:, 0:nch * 128],
                                          ab["pt_ps"][gp][:, 0:nch * 128])),
                           [htr, war.get(("ptsb_free", gp))])
                war[("ptps_free", gp)] = hev
                hav = P.op("tensor",
                           (lambda e, gp=gp, g0=g0, nch=nch, kb=kbase_c,
                            fa=first_av, lg=lastg:
                            _av_group(e, ab, v_sb, gp, g0, nch, kb, fa, lg)),
                           [hev, war.get("o_ps_free")] + list(kv_dep))
                war[("ptsb_free", gp)] = hav
                v_reads.append(hav)
                first_av = False
            war[("p_free", pp)] = htr_last
            if first:
                hO = P.op("vector", lambda e: e.tensor_copy(
                    oacc[:], ab["o_ps"][:, 0:128]),
                    [hav, war.get("oacc_free")])
            else:
                hO = P.op("vector", lambda e: e.scalar_tensor_tensor(
                    oacc[:], oacc[:], fl["alpha"][:], ab["o_ps"][:, 0:128],
                    ALU.mult, ALU.add), [hav, hO, halpha])
                war["alpha_free"] = hO
            war["o_ps_free"] = hO
        hrz = P.op("vector", lambda e: e.reciprocal(
            fl["rz"][:], fl["zz"][:]), [hZ, war.get("rz_free")])
        war["zz_free"] = hrz
        if not transpose_out:
            hfin = P.op("vector", lambda e, j=j: e.tensor_scalar(
                ohead[:, j * 128:(j + 1) * 128], oacc[:], fl["rz"][:], None,
                ALU.mult), [hrz, hO, war.get(("ohead_free", hp))])
        else:
            hfn0 = P.op("vector", lambda e: e.tensor_scalar(
                ab["o_tmp"][:], oacc[:], fl["rz"][:], None, ALU.mult),
                [hrz, hO, war.get("o_tmp_free")])
            hotr = P.op("tensor", lambda e: e.transpose(
                ab["otr_ps"][:], ab["o_tmp"][:], ident_sb[:]),
                [hfn0, ident_h, war.get("otr_free")])
            war["o_tmp_free"] = hotr
            hfin = P.op("scalar", lambda e, j=j: e.copy(
                ohead[:, j * 128:(j + 1) * 128], ab["otr_ps"][:]),
                [hotr, war.get(("ohead_free", hp))])
            war["otr_free"] = hfin
        war["rz_free"] = hfin
        war["oacc_free"] = hfin
        war.setdefault("fins", []).append(hfin)
    hdma = P.dma(o_store_eng, o_dram_ap,
                 ohead[:].rearrange("p (j c) -> p j c", j=n_jtiles),
                 war["fins"], slot=o_slot)
    war["fins"] = []
    war[("ohead_free", hp)] = hdma
    return qk_reads, v_reads, hdma



PW = 1408


def emit_attn_pipe(P, ab, h, n_jtiles, q_sb, k_sb, v_sb, kv_dep, q_dep,
                   o_dram_ap, mask_diag_sb, mask_win_sb, mask_h, ident_sb,
                   ident_h, windowed, sscale=ASCALE, mask_h2=None,
                   o_slot=None, o_store_eng="sync"):
    SP = ab["SP"]
    ptp = ab["ptp"]
    pt_sb = ab["pt_sb"]
    fl = ab["fl"]
    oacc = ab["oacc"]
    o_tmp = ab["o_tmp"]
    p_sb = ab["p_sb"]
    war = ab["war"]
    hp = h % 2
    ohead = ab["ohead"][hp]
    otr = ptp[1][:, 896:1024]
    qk_reads, v_reads = [], []

    class _St:
        pass

    def mkstate(j, x):
        s = _St()
        s.j, s.x = j, x
        kend = (j + 1) * 128
        s.kstart = max(0, kend - 1152) if windowed else 0
        kn = kend - s.kstart
        bb = list(range(0, kn, PW))
        if bb[-1] != kn:
            bb.append(kn)
        s.pb = [(bb[i], bb[i + 1]) for i in range(len(bb) - 1)]
        s.np = len(s.pb)
        s.hZ = s.hO = s.halpha = s.hexp = s.hs = None
        return s

    def st1(s, ip):
        x = s.x
        p0, p1 = s.pb[ip]
        w = p1 - p0
        chunks = []
        c0 = 0
        while c0 < w:
            cw = min(512, w - c0)
            chunks.append((c0, cw))
            c0 += cw

        def smm(e, s=s, p0=p0, ch=tuple(chunks), x=x):
            r = None
            for (cc, cw) in ch:
                r = e.matmul(SP[x][:, cc:cc + cw],
                             q_sb[:, s.j * 128:(s.j + 1) * 128],
                             k_sb[:, s.kstart + p0 + cc:
                                  s.kstart + p0 + cc + cw],
                             start=True, stop=True)
            return r
        s.hs = P.op("tensor", smm,
                    [war.get(("sp_free", x)), q_dep] + list(kv_dep))
        qk_reads.append(s.hs)

    def st2(s, ip):
        x = s.x
        flx = fl[x]
        p0, p1 = s.pb[ip]
        w = p1 - p0
        first, lastp = ip == 0, ip == s.np - 1
        deps_red = [s.hs]
        if lastp:
            hmd_ = P.op("vector", lambda e, w=w, x=x: e.tensor_tensor(
                SP[x][:, w - 128:w], SP[x][:, w - 128:w], mask_diag_sb[:],
                ALU.add), [s.hs, mask_h])
            deps_red.append(hmd_)
        if first and windowed and s.kstart > 0:
            hmw_ = P.op("vector", lambda e, x=x: e.tensor_tensor(
                SP[x][:, 0:128], SP[x][:, 0:128], mask_win_sb[:], ALU.add),
                [s.hs, mask_h2 or mask_h])
            deps_red.append(hmw_)
        hred = P.op("vector", lambda e, w=w, x=x: e.reduce_max(
            flx["negm2"][:], SP[x][:, 0:w], axis=AX.X, negate=True),
            deps_red + [war.get(("negm2_free", x))])
        if s.np == 1:
            hnsc = P.op("vector", lambda e, flx=flx: e.tensor_scalar(
                flx["negmsc"][:], flx["negm2"][:], sscale, None, ALU.mult),
                [hred, war.get(("negmsc_free", x))])
            war[("negm2_free", x)] = hnsc
            s.halpha = None
        elif first:
            hngm = P.op("vector", lambda e, flx=flx: e.tensor_copy(
                flx["negm"][:], flx["negm2"][:]),
                [hred, war.get(("negm_free", x))])
            war[("negm2_free", x)] = hngm
            hnsc = P.op("vector", lambda e, flx=flx: e.tensor_scalar(
                flx["negmsc"][:], flx["negm"][:], sscale, None, ALU.mult),
                [hngm, war.get(("negmsc_free", x))])
            war[("negm_free", x)] = hnsc
            s.halpha = None
        else:
            hdm = P.op("vector", lambda e, flx=flx: e.tensor_tensor(
                flx["dtmp"][:], flx["negm"][:], flx["negm2"][:], ALU.min),
                [hred, war.get(("dtmp_free", x))])
            hdf = P.op("vector", lambda e, flx=flx: e.tensor_tensor(
                flx["negm2"][:], flx["dtmp"][:], flx["negm"][:],
                ALU.subtract), [hdm])
            s.halpha = P.op("scalar", lambda e, flx=flx: e.activation(
                flx["alpha"][:], flx["negm2"][:], AF.Exp, scale=sscale),
                [hdf, war.get(("alpha_free", x))])
            war[("negm2_free", x)] = s.halpha
            hngm = P.op("vector", lambda e, flx=flx: e.tensor_copy(
                flx["negm"][:], flx["dtmp"][:]),
                [hdf, war.get(("negm_free", x))])
            war[("dtmp_free", x)] = hngm
            hnsc = P.op("vector", lambda e, flx=flx: e.tensor_scalar(
                flx["negmsc"][:], flx["negm"][:], sscale, None, ALU.mult),
                [hngm, war.get(("negmsc_free", x))])
            war[("negm_free", x)] = hnsc
        s.hexp = P.op("scalar", lambda e, w=w, x=x, flx=flx: e.activation(
            p_sb[x][:, 0:w], SP[x][:, 0:w], AF.Exp, scale=sscale,
            bias=flx["negmsc"][:], accum_out=flx["zp"][:]),
            [hnsc, war.get(("p_free", x)), war.get(("zp_free", x))])
        war[("negmsc_free", x)] = s.hexp
        war[("sp_free", x)] = s.hexp
        if s.np == 1:
            s.hZ = s.hexp
        elif first:
            s.hZ = P.op("vector", lambda e, flx=flx: e.tensor_copy(
                flx["zz"][:], flx["zp"][:]),
                [s.hexp, war.get(("zz_free", x))])
            war[("zp_free", x)] = s.hZ
        else:
            s.hZ = P.op("vector", lambda e, flx=flx: e.scalar_tensor_tensor(
                flx["zz"][:], flx["zz"][:], flx["alpha"][:], flx["zp"][:],
                ALU.mult, ALU.add), [s.hexp, s.hZ, s.halpha])
            war[("zp_free", x)] = s.hZ

    def st3(s, ip):
        x = s.x
        p0, p1 = s.pb[ip]
        w = p1 - p0
        first = ip == 0
        nkc = w // 128
        kbase = (s.kstart + p0) // 128
        groups = [(0, 0, min(8, nkc))]
        if nkc > 8:
            groups.append((1, 8, nkc - 8))
        trs = []
        for gi, g0, nch in groups:
            def trg(e, x=x, gi=gi, g0=g0, nch=nch):
                r = None
                for c in range(nch):
                    r = e.transpose(ptp[gi][:, c * 128:(c + 1) * 128],
                                    p_sb[x][:, (g0 + c) * 128:
                                            (g0 + c + 1) * 128],
                                    ident_sb[:])
                return r
            htr = P.op("tensor", trg,
                       [s.hexp, ident_h, war.get(("ptps_free", gi))])
            trs.append((gi, g0, nch, htr))
        war[("p_free", x)] = trs[-1][3]
        hav = None
        for idx, (gi, g0, nch, htr) in enumerate(trs):
            hev = P.op("vector" if gi == 0 else "scalar",
                       (lambda e, gi=gi, nch=nch:
                        (e.tensor_copy if hasattr(e, "tensor_copy")
                         else e.copy)(pt_sb[gi][:, 0:nch * 128],
                                      ptp[gi][:, 0:nch * 128])),
                       [htr, war.get(("ptsb_free", gi))])
            war[("ptps_free", gi)] = hev

            def avg(e, x=x, gi=gi, g0=g0, nch=nch, kbase=kbase,
                    fst=(idx == 0 and True), lst=(idx == len(trs) - 1)):
                r = None
                for c in range(nch):
                    kc = kbase + g0 + c
                    r = e.matmul(SP[x][:, PW:PW + 128],
                                 pt_sb[gi][:, c * 128:(c + 1) * 128],
                                 v_sb[:, kc * 128:(kc + 1) * 128],
                                 start=(fst and c == 0),
                                 stop=(lst and c == nch - 1))
                return r
            hav = P.op("tensor", avg,
                       [hev, war.get(("tail_free", x)) if idx == 0 else None]
                       + list(kv_dep))
            war[("ptsb_free", gi)] = hav
            v_reads.append(hav)
        if s.np == 1:
            s.hO = hav
        elif first:
            s.hO = P.op("vector", lambda e, x=x: e.tensor_copy(
                oacc[x][:], SP[x][:, PW:PW + 128]),
                [hav, war.get(("oacc_free", x))])
            war[("tail_free", x)] = s.hO
        else:
            s.hO = P.op("vector", lambda e, x=x, flx=fl[x]:
                        e.scalar_tensor_tensor(
                            oacc[x][:], oacc[x][:], flx["alpha"][:],
                            SP[x][:, PW:PW + 128], ALU.mult, ALU.add),
                        [hav, s.hO, s.halpha])
            war[("tail_free", x)] = s.hO
            war[("alpha_free", x)] = s.hO

    def fin(s):
        x = s.x
        flx = fl[x]
        if s.np == 1:
            hrz = P.op("vector", lambda e, flx=flx: e.reciprocal(
                flx["rz"][:], flx["zp"][:]),
                [s.hexp, war.get(("rz_free", x))])
            war[("zp_free", x)] = hrz
            hfn0 = P.op("scalar", lambda e, x=x, flx=flx: e.activation(
                o_tmp[x][:], SP[x][:, PW:PW + 128], AF.Identity,
                scale=flx["rz"][:]),
                [hrz, s.hO, war.get(("o_tmp_free", x))])
            war[("tail_free", x)] = hfn0
        else:
            hrz = P.op("vector", lambda e, flx=flx: e.reciprocal(
                flx["rz"][:], flx["zz"][:]), [s.hZ, war.get(("rz_free", x))])
            war[("zz_free", x)] = hrz
            hfn0 = P.op("scalar", lambda e, x=x, flx=flx: e.activation(
                o_tmp[x][:], oacc[x][:], AF.Identity, scale=flx["rz"][:]),
                [hrz, s.hO, war.get(("o_tmp_free", x))])
            war[("oacc_free", x)] = hfn0
        war[("rz_free", x)] = hfn0
        hotr = P.op("tensor", lambda e, x=x: e.transpose(
            otr, o_tmp[x][:], ident_sb[:]),
            [hfn0, ident_h, war.get("otr_free")])
        war[("o_tmp_free", x)] = hotr
        hfin = P.op("scalar", lambda e, j=s.j: e.copy(
            ohead[:, j * 128:(j + 1) * 128], otr),
            [hotr, war.get(("ohead_free", hp))])
        war["otr_free"] = hfin
        war.setdefault("fins", []).append(hfin)

    for jb in range(0, n_jtiles, 2):
        sA = mkstate(jb, 0)
        sB = mkstate(jb + 1, 1)
        both = [sA, sB]
        for p in range(max(sA.np, sB.np)):
            for s in both:
                if p < s.np:
                    st1(s, p)
            for s in both:
                if p < s.np:
                    st2(s, p)
            for s in both:
                if p < s.np:
                    st3(s, p)
        fin(sA)
        fin(sB)
    hdma = P.dma(o_store_eng, o_dram_ap,
                 ohead[:].rearrange("p (j c) -> p j c", j=n_jtiles),
                 war["fins"], slot=o_slot)
    war["fins"] = []
    war[("ohead_free", hp)] = hdma
    return qk_reads, v_reads, hdma



def build_launch1():
    nc = bass.Bass("TRN2", target_bir_lowering=False)
    x = nc.dram_tensor("x", [TPC, D], F32, kind="ExternalInput").ap()
    wqt = nc.dram_tensor("wqt", [D, D], F32, kind="ExternalInput").ap()
    wkt = nc.dram_tensor("wkt", [D, D], F32, kind="ExternalInput").ap()
    wvt = nc.dram_tensor("wvt", [D, D], F32, kind="ExternalInput").ap()
    bqt = nc.dram_tensor("bqt", [128, 32], F32, kind="ExternalInput").ap()
    bkt = nc.dram_tensor("bkt", [128, 32], F32, kind="ExternalInput").ap()
    bvr = nc.dram_tensor("bvr", [1, D], F32, kind="ExternalInput").ap()
    identity = nc.dram_tensor("identity", [128, 128], F32,
                              kind="ExternalInput").ap()
    qt = nc.dram_tensor("qt", [D, TPC], BF16, kind="ExternalOutput").ap()
    kt = nc.dram_tensor("kt", [D, TPC], BF16, kind="ExternalOutput").ap()
    v = nc.dram_tensor("v", [TPC, D], BF16, kind="ExternalOutput").ap()
    xnt = nc.dram_tensor("xnt", [D, TPC], BF16, kind="ExternalOutput").ap()

    with ExitStack() as ctx:
        sb = lambda n, s, d: ctx.enter_context(nc.sbuf_tensor(n, s, d))
        psum = lambda n, s, d: ctx.enter_context(nc.psum_tensor(n, s, d))
        bufs = {
            "xs": [sb("xs0", [128, D], F32), sb("xs1", [128, D], F32)],
            "stats": alloc_stats(nc, ctx, "st"),
            "xn_tm": [sb("xntm0", [128, D], BF16), sb("xntm1", [128, D], BF16)],
            "pt_ps": [psum("ptps0", [128, 1024], BF16),
                      psum("ptps1", [128, 1024], BF16)],
            "w_fm": [sb("wfm0", [128, 32 * 128], BF16),
                     sb("wfm1", [128, 32 * 128], BF16)],
            "ps_fm": [psum("psfm0", [128, 512], F32),
                      psum("psfm1", [128, 512], F32)],
            "o_fm": [sb("ofm0", [128, TPC], BF16), sb("ofm1", [128, TPC], BF16)],
            "w_tm": [sb("wtm0", [128, 32 * 512], BF16)] * 2,
            "ps_tm": [psum("pstm0", [128, 512], F32),
                      psum("pstm1", [128, 512], F32)],
            "v_tm": [sb("vtm0", [128, 512], BF16), sb("vtm1", [128, 512], BF16)],
        }
        xnt_sb = sb("xnt_sb", [128, 32 * TPC], BF16)
        ident_sb = sb("ident", [128, 128], BF16)
        bq_sb = sb("bq_sb", [128, 32], F32)
        bk_sb = sb("bk_sb", [128, 32], F32)
        bv_sb = sb("bv_sb", [1, D], F32)
        ones_sb = sb("ones_sb", [1, 128], F32)

        dscr = psum("dscr", [128, 1024], BF16)
        P = Plan(nc)
        P.settle_fn = lambda e: e.transpose(dscr[:, 0:128], ident_sb[:],
                                            ident_sb[:])
        hid = P.dma("gpsimd", ident_sb[:], identity)
        hbq = P.dma("sync", bq_sb[:], bqt)
        hbk = P.dma("sync", bk_sb[:], bkt)
        hbv = P.dma("sync", bv_sb[:], bvr)
        hones = P.op("vector", lambda e: e.memset(ones_sb[:], 1.0))

        ln_done = emit_ln(P, bufs, x, xnt_sb, ident_sb, ident_h=hid)
        lnf = fences_list(ln_done)
        hxnt = P.dma("sync", xnt.rearrange("(k p) t -> p k t", p=128),
                     xnt_sb[:].rearrange("p (k t) -> p k t", k=32), lnf)
        hq = emit_proj_fm(P, bufs, wqt, qt, lnf, xnt_sb, D,
                          bias_sb=bq_sb, bias_h=hbq, scale=1.0, tag="q")
        hk = emit_proj_fm(P, bufs, wkt, kt, lnf, xnt_sb, D,
                          bias_sb=bk_sb, bias_h=hbk, scale=1.0, tag="k")
        hv = emit_proj_tm(P, bufs, wvt, v, lnf, xnt_sb, D,
                          bias_row_sb=bv_sb, bias_h=hbv, ones_sb=ones_sb,
                          ones_h=hones, tag="v")
        P.barrier("sync", fences_list([hxnt] + hq + hk + hv))
        block = ctx.enter_context(nc.Block())
        P.emit(ctx, block)
    return nc



def _run(nc, in_maps):
    return run_bass_kernel_spmd(nc, in_maps, core_ids=list(range(8))).results


def kernel_unfused(L0_attn_x, L0_attn_Wq, L0_attn_bq, L0_attn_Wk, L0_attn_bk,
           L0_attn_Wv, L0_attn_bv, L0_attn_Wo, L0_attn_bo, L0_ffn_weight,
           L0_ffn_bias, L0_ffn_lora_A, L0_ffn_lora_B,
           L0_ffn_lora_start_indices, L1_attn_wq, L1_attn_wk, L1_attn_wv,
           L1_attn_wo, L1_ffn_A, L1_ffn_B, L1_ffn_Wr, L1_ffn_br):
    f32 = np.float32
    x = np.asarray(L0_attn_x, dtype=f32)
    Wq = np.asarray(L0_attn_Wq, dtype=f32); bq = np.asarray(L0_attn_bq, dtype=f32)
    Wk = np.asarray(L0_attn_Wk, dtype=f32); bk = np.asarray(L0_attn_bk, dtype=f32)
    Wv = np.asarray(L0_attn_Wv, dtype=f32); bv = np.asarray(L0_attn_bv, dtype=f32)
    Wo = np.asarray(L0_attn_Wo, dtype=f32); bo = np.asarray(L0_attn_bo, dtype=f32)
    Wf = np.asarray(L0_ffn_weight, dtype=f32)
    bfn = np.asarray(L0_ffn_bias, dtype=f32)
    lA = np.asarray(L0_ffn_lora_A, dtype=f32)
    lB = np.asarray(L0_ffn_lora_B, dtype=f32)
    lsi = np.asarray(L0_ffn_lora_start_indices).astype(np.int32).reshape(1, 4)
    wq1 = np.asarray(L1_attn_wq, dtype=f32)
    wk1 = np.asarray(L1_attn_wk, dtype=f32)
    wv1 = np.asarray(L1_attn_wv, dtype=f32)
    wo1 = np.asarray(L1_attn_wo, dtype=f32)
    A1 = np.asarray(L1_ffn_A, dtype=f32)
    B1 = np.asarray(L1_ffn_B, dtype=f32)
    Wr = np.asarray(L1_ffn_Wr, dtype=f32)
    br = np.asarray(L1_ffn_br, dtype=f32)

    mask_diag, mask_win = make_masks()
    ident = np.eye(128, dtype=f32)
    e8 = np.zeros((8, 128), dtype=f32)
    for h in range(8):
        e8[h, h * 16:(h + 1) * 16] = 1.0
    C = lambda a: np.ascontiguousarray(a)

    im = []
    for c in range(8):
        b, r = c // 4, c % 4
        im.append({"x": C(x[b, r * TPC:(r + 1) * TPC, :]),
                   "wqt": C(Wq.T), "wkt": C(Wk.T), "wvt": C(Wv.T),
                   "bqt": C(bq.reshape(32, 128).T),
                   "bkt": C(bk.reshape(32, 128).T),
                   "bvr": C(bv.reshape(1, -1)), "identity": ident})
    r1 = _run(build_launch1(), im)
    QT = [np.concatenate([r1[4 * b + r]["qt"] for r in range(4)], 1)
          for b in range(2)]
    KT = [np.concatenate([r1[4 * b + r]["kt"] for r in range(4)], 1)
          for b in range(2)]
    V = [np.concatenate([r1[4 * b + r]["v"] for r in range(4)], 0)
         for b in range(2)]

    im = []
    for c in range(8):
        b, r = c // 4, c % 4
        im.append({"q": C(QT[b][r * TPC:(r + 1) * TPC, :]),
                   "k": C(KT[b][r * TPC:(r + 1) * TPC, :]),
                   "v": C(V[b][:, r * TPC:(r + 1) * TPC]),
                   "mdg": mask_diag, "identity": ident})
    r2 = _run(build_launch2(), im)
    O = [np.concatenate([r2[4 * b + r]["o"] for r in range(4)], 1)
         for b in range(2)]

    im = []
    for c in range(8):
        b, r = c // 4, c % 4
        pos = np.arange(r * TPC, (r + 1) * TPC)
        rcos, rsin = make_rope_tables(pos, 1.0)
        im.append({
            "x": C(x[b, r * TPC:(r + 1) * TPC, :]),
            "xnt": r1[c]["xnt"],
            "ot": C(O[b][r * TPC:(r + 1) * TPC, :].T),
            "wot": C(Wo.T), "bor": C(bo.reshape(1, -1)),
            "wft": C(Wf.T), "bfr": C(bfn.reshape(1, -1)),
            "acat": C(lA.transpose(2, 0, 1).reshape(D, 48)),
            "bcat": C(lB.transpose(0, 2, 1).reshape(48, D)),
            "lsi": lsi, "bsel": np.array([[float(b)]], dtype=f32),
            "wq1": wq1, "wk1": wk1, "wv1": wv1,
            "rcos": rcos, "rsin": rsin, "identity": ident})
    r3 = _run(build_launch3(), im)
    Q1 = [np.concatenate([r3[4 * b + r]["q1t"] for r in range(4)], 1)
          for b in range(2)]
    K1 = [np.concatenate([r3[4 * b + r]["k1t"] for r in range(4)], 1)
          for b in range(2)]
    V1 = [np.concatenate([r3[4 * b + r]["v1"] for r in range(4)], 0)
          for b in range(2)]

    im = []
    for c in range(8):
        b, r = c // 4, c % 4
        im.append({"q": C(Q1[b][r * TPC:(r + 1) * TPC, :]),
                   "k": C(K1[b][r * 256:(r + 1) * 256, :]),
                   "v": C(V1[b][:, r * 256:(r + 1) * 256]),
                   "mdg": mask_diag, "mwn": mask_win, "identity": ident})
    r4 = _run(build_launch4(), im)
    O1 = [np.concatenate([r4[4 * b + r]["o"] for r in range(4)], 1)
          for b in range(2)]

    im = []
    for c in range(8):
        b, r = c // 4, c % 4
        im.append({
            "x1": r3[c]["x1"],
            "xn1t": r3[c]["xn1t"],
            "o1t": C(O1[b][r * TPC:(r + 1) * TPC, :].T),
            "wo1": wo1,
            "a1cat": C(A1.transpose(2, 0, 1).reshape(D, 128)),
            "b1cat": C(B1.transpose(0, 2, 1).reshape(128, D)),
            "wrt": C(Wr.T), "brr": C(br.reshape(1, 8)),
            "e8": e8, "identity": ident})
    r5 = _run(build_launch5(), im)

    out = np.empty((2, S, D), dtype=f32)
    for c in range(8):
        b, r = c // 4, c % 4
        out[b, r * TPC:(r + 1) * TPC, :] = r5[c]["out"]
    return out


def build_launch2():
    nc = bass.Bass("TRN2", target_bir_lowering=False)
    q = nc.dram_tensor("q", [8 * 128, S], BF16, kind="ExternalInput").ap()
    k = nc.dram_tensor("k", [8 * 128, S], BF16, kind="ExternalInput").ap()
    v = nc.dram_tensor("v", [S, 8 * 128], BF16, kind="ExternalInput").ap()
    mdg = nc.dram_tensor("mdg", [128, 128], F32, kind="ExternalInput").ap()
    identity = nc.dram_tensor("identity", [128, 128], F32,
                              kind="ExternalInput").ap()
    o = nc.dram_tensor("o", [S, 8 * 128], BF16, kind="ExternalOutput").ap()

    with ExitStack() as ctx:
        sb = lambda n, s, d: ctx.enter_context(nc.sbuf_tensor(n, s, d))
        ab = alloc_attn_bufs(nc, ctx)
        q_sb = [sb("q0", [128, S], BF16), sb("q1", [128, S], BF16)]
        k_sb = [sb("k0", [128, S], BF16), sb("k1", [128, S], BF16)]
        v_sb = [sb("v0", [128, 32 * 128], BF16), sb("v1", [128, 32 * 128], BF16)]
        md_sb = sb("md", [128, 128], F32)
        ident_sb = sb("ident", [128, 128], BF16)
        dscr = ctx.enter_context(nc.psum_tensor("dscr", [128, 1024], BF16))

        P = Plan(nc)
        P.settle_fn = lambda e: e.transpose(dscr[:, 0:128], ident_sb[:],
                                            ident_sb[:])
        hid = P.dma("gpsimd", ident_sb[:], identity)
        hmd = P.dma("gpsimd", md_sb[:], mdg)
        qk_free = [None, None]
        v_free = [None, None]
        odma = []
        for h in range(8):
            hb = h % 2
            hq = P.dma("sync", q_sb[hb][:], q[h * 128:(h + 1) * 128, :],
                       [qk_free[hb]])
            hk = P.dma("sync", k_sb[hb][:], k[h * 128:(h + 1) * 128, :],
                       [qk_free[hb]])
            hv = P.dma("sync", v_sb[hb][:].rearrange("p (kc c) -> p kc c", kc=32),
                       v.rearrange("(kc p) c -> p kc c", p=128)
                       [:, :, h * 128:(h + 1) * 128], [v_free[hb]])
            o_ap = o.rearrange("(j q) c -> q j c", q=128)[:, :, h * 128:(h + 1) * 128]
            qk_reads, v_reads, hdma = emit_attn_head(
                P, ab, h, 32, q_sb[hb], k_sb[hb], v_sb[hb],
                [hk, hv], hq, o_ap, md_sb, None, hmd,
                ident_sb, hid, windowed=False)
            qk_free[hb] = fences_list(qk_reads)[0] if qk_reads else None
            vf = fences_list(v_reads)
            v_free[hb] = vf[0] if vf else None
            odma.append(hdma)
        P.barrier("sync", fences_list(odma))
        block = ctx.enter_context(nc.Block())
        P.emit(ctx, block)
    return nc



def build_launch4():
    nc = bass.Bass("TRN2", target_bir_lowering=False)
    q = nc.dram_tensor("q", [8 * 128, S], BF16, kind="ExternalInput").ap()
    k = nc.dram_tensor("k", [2 * 128, S], BF16, kind="ExternalInput").ap()
    v = nc.dram_tensor("v", [S, 2 * 128], BF16, kind="ExternalInput").ap()
    mdg = nc.dram_tensor("mdg", [128, 128], F32, kind="ExternalInput").ap()
    mwn = nc.dram_tensor("mwn", [128, 128], F32, kind="ExternalInput").ap()
    identity = nc.dram_tensor("identity", [128, 128], F32,
                              kind="ExternalInput").ap()
    o = nc.dram_tensor("o", [S, 8 * 128], BF16, kind="ExternalOutput").ap()

    with ExitStack() as ctx:
        sb = lambda n, s, d: ctx.enter_context(nc.sbuf_tensor(n, s, d))
        ab = alloc_attn_bufs(nc, ctx)
        q_sb = [sb("q0", [128, S], BF16), sb("q1", [128, S], BF16)]
        k_sb = [sb("k0", [128, S], BF16), sb("k1", [128, S], BF16)]
        v_sb = [sb("v0", [128, 32 * 128], BF16), sb("v1", [128, 32 * 128], BF16)]
        md_sb = sb("md", [128, 128], F32)
        mw_sb = sb("mw", [128, 128], F32)
        ident_sb = sb("ident", [128, 128], BF16)
        dscr = ctx.enter_context(nc.psum_tensor("dscr", [128, 1024], BF16))

        P = Plan(nc)
        P.settle_fn = lambda e: e.transpose(dscr[:, 0:128], ident_sb[:],
                                            ident_sb[:])
        hid = P.dma("gpsimd", ident_sb[:], identity)
        hmd = P.dma("gpsimd", md_sb[:], mdg)
        hmw = P.dma("gpsimd", mw_sb[:], mwn)
        hks, hvs = [], []
        for kv in range(2):
            hks.append(P.dma("sync", k_sb[kv][:], k[kv * 128:(kv + 1) * 128, :]))
            hvs.append(P.dma(
                "sync", v_sb[kv][:].rearrange("p (kc c) -> p kc c", kc=32),
                v.rearrange("(kc p) c -> p kc c", p=128)
                [:, :, kv * 128:(kv + 1) * 128]))
        qbufs = [sb("qb0", [128, S], BF16), sb("qb1", [128, S], BF16)]
        q_free = [None, None]
        odma = []
        for h in range(8):
            hb = h % 2
            kv = h // 4
            hq = P.dma("sync", qbufs[hb][:], q[h * 128:(h + 1) * 128, :],
                       [q_free[hb]])
            o_ap = o.rearrange("(j q) c -> q j c", q=128)[:, :, h * 128:(h + 1) * 128]
            qk_reads, v_reads, hdma = emit_attn_head(
                P, ab, h, 32, qbufs[hb], k_sb[kv], v_sb[kv],
                [hks[kv], hvs[kv]], hq, o_ap, md_sb, mw_sb,
                hmd, ident_sb, hid, windowed=True, mask_h2=hmw)
            qf = fences_list(qk_reads)
            q_free[hb] = qf[0] if qf else None
            odma.append(hdma)
        P.barrier("sync", fences_list(odma))
        block = ctx.enter_context(nc.Block())
        P.emit(ctx, block)
    return nc


def build_launch3():
    nc = bass.Bass("TRN2", target_bir_lowering=False)
    di = lambda n, s, d: nc.dram_tensor(n, s, d, kind="ExternalInput").ap()
    do = lambda n, s, d: nc.dram_tensor(n, s, d, kind="ExternalOutput").ap()
    x = di("x", [TPC, D], F32)
    xnt = di("xnt", [D, TPC], BF16)
    ot = di("ot", [D, TPC], BF16)
    wot = di("wot", [D, D], F32)
    bor = di("bor", [1, D], F32)
    wft = di("wft", [D, D], F32)
    bfr = di("bfr", [1, D], F32)
    acat = di("acat", [D, 48], F32)
    bcat = di("bcat", [48, D], F32)
    lsi = di("lsi", [1, 4], I32)
    bsel = di("bsel", [1, 1], F32)
    wq1 = di("wq1", [D, D], F32)
    wk1 = di("wk1", [D, 1024], F32)
    wv1 = di("wv1", [D, 1024], F32)
    rcos = di("rcos", [128, TPC], F32)
    rsin = di("rsin", [128, TPC], F32)
    identity = di("identity", [128, 128], F32)
    x1 = do("x1", [TPC, D], F32)
    q1t = do("q1t", [D, TPC], BF16)
    k1t = do("k1t", [1024, TPC], BF16)
    v1 = do("v1", [TPC, 1024], BF16)
    xn1t = do("xn1t", [D, TPC], BF16)
    wdram = nc.dram_tensor("wdram", [1, 48], F32)

    with ExitStack() as ctx:
        sb = lambda n, s, d: ctx.enter_context(nc.sbuf_tensor(n, s, d))
        psum = lambda n, s, d: ctx.enter_context(nc.psum_tensor(n, s, d))
        xn1t_sb = sb("xn1t_sb", [128, 32 * TPC], BF16)
        xnt_tt = sb("xnt_tt", [128, 32 * 128], BF16)
        ot_tt = sb("ot_tt", [128, 32 * 128], BF16)
        wblk = sb("wblk", [128, 32 * 512], BF16)
        g_blk = sb("g_blk", [128, D], F32)
        xs0 = sb("xs0", [128, D], F32)
        xs1 = g_blk
        xntm0 = xnt_tt
        xntm1 = ot_tt
        rcos_sb = sb("rcos_sb", [128, TPC], F32)
        rsin_sb = sb("rsin_sb", [128, TPC], F32)
        wfm = sb("wfm", [128, 32 * 128], BF16)
        ofm0 = sb("ofm0", [128, TPC], BF16)
        ofm1 = sb("ofm1", [128, TPC], BF16)
        rhs0 = sb("rhs0", [128, 512], BF16)
        rhs1 = sb("rhs1", [128, 512], BF16)
        xt0 = sb("xt0", [128, 512], F32)
        xt1 = sb("xt1", [128, 512], F32)
        tmp0 = sb("tmp0", [128, 512], F32)
        tmp1 = sb("tmp1", [128, 512], F32)
        x1t0 = sb("x1t0", [128, 512], F32)
        x1t1 = sb("x1t1", [128, 512], F32)
        acat_sb = sb("acat_sb", [128, 32 * 48], BF16)
        bcat_sb = sb("bcat_sb", [48, D], BF16)
        rt_sb = sb("rt_sb", [48, TPC], BF16)
        w48_sb = sb("w48_sb", [48, 1], F32)
        bo_t = [sb("bo_t0", [1, 512], F32), sb("bo_t1", [1, 512], F32)]
        bf_t = [sb("bf_t0", [1, 512], F32), sb("bf_t1", [1, 512], F32)]
        ones_sb = sb("ones_sb", [1, 128], F32)
        ones16 = sb("ones16", [1, 16], F32)
        lsii = sb("lsii", [1, 4], I32)
        lsif = sb("lsif", [1, 4], F32)
        bsel_sb = sb("bsel_sb", [1, 1], F32)
        cnt_sb = sb("cnt_sb", [1, 1], F32)
        uc_sb = sb("uc_sb", [1, 1], F32)
        gate_sb = sb("gate_sb", [1, 1], F32)
        wa_sb = sb("wa_sb", [1, 3], F32)
        wfree = sb("wfree", [1, 48], F32)
        ident_sb = sb("ident_sb", [128, 128], BF16)
        stats = alloc_stats(nc, ctx, "st")
        ps_a = psum("ps_a", [128, 512], F32)
        ps_b = psum("ps_b", [128, 512], F32)
        ps_rt = psum("ps_rt", [48, 512], F32)
        ptps0 = psum("ptps0", [128, 1024], BF16)
        ptps1 = psum("ptps1", [128, 1024], BF16)
        ps_c = psum("ps_c", [128, 512], F32)
        ps_d = psum("ps_d", [128, 512], F32)
        dscr = psum("dscr", [128, 1024], BF16)

        P = Plan(nc)
        P.settle_fn = lambda e: e.transpose(dscr[:, 0:128], ident_sb[:],
                                            ident_sb[:])
        hid = P.dma("gpsimd", ident_sb[:], identity)

        hcos = P.dma("sync", rcos_sb[:], rcos)
        hsin = P.dma("sync", rsin_sb[:], rsin)
        hones = P.op("vector", lambda e: e.memset(ones_sb[:], 1.0))
        hones16 = P.op("vector", lambda e: e.memset(ones16[:], 1.0))
        hacat = P.dma("gpsimd", acat_sb[:].rearrange("p (k c) -> p k c", k=32),
                      acat.rearrange("(k p) c -> p k c", p=128))
        hbcat = P.dma("gpsimd", bcat_sb[:], bcat)

        hlsi = P.dma("sync", lsii[:], lsi)
        hbs = P.dma("sync", bsel_sb[:], bsel)
        hc0 = P.op("vector", lambda e: e.tensor_copy(lsif[:], lsii[:]), [hlsi])
        hc1 = P.op("vector", lambda e: e.tensor_scalar(
            lsif[:], lsif[:], bsel_sb[0:1, 0:1], None, ALU.is_le), [hc0, hbs])
        hc2 = P.op("vector", lambda e: e.reduce_sum(
            cnt_sb[:], lsif[:], axis=AX.X), [hc1])
        hc3 = P.op("vector", lambda e: e.tensor_scalar(
            uc_sb[:], cnt_sb[:], -2.0, 0.0, ALU.add, ALU.max), [hc2])
        hc4 = P.op("vector", lambda e: e.tensor_scalar(
            uc_sb[:], uc_sb[:], 2.0, None, ALU.min), [hc3])
        hc5 = P.op("vector", lambda e: e.tensor_scalar(
            gate_sb[:], cnt_sb[:], 2.0, None, ALU.is_ge), [hc2])
        hws = []
        for a in range(3):
            hw_ = P.op("vector", lambda e, a=a: e.tensor_scalar(
                wa_sb[0:1, a:a + 1], uc_sb[:], float(a),
                gate_sb[0:1, 0:1], ALU.is_equal, ALU.mult), [hc4, hc5])
            hws.append(hw_)
        hwf = []
        for a in range(3):
            hw2 = P.op("vector", lambda e, a=a: e.tensor_scalar(
                wfree[0:1, 16 * a:16 * (a + 1)], ones16[:],
                wa_sb[0:1, a:a + 1], None, ALU.mult), [hws[a], hones16])
            hwf.append(hw2)
        hwd = P.dma("sync", wdram.ap(), wfree[:], fences_list(hwf))
        hw48 = P.dma("sync", w48_sb[:], wdram.ap().rearrange("o (k c) -> k (o c)", k=48),
                     [hwd])

        rt_done = []
        hrt_prev = None
        for t2 in range(2):
            hmm = None
            for kc in range(32):
                rb_ = rhs0 if kc % 2 == 0 else rhs1
                hrl = P.dma("sync", rb_[:],
                            xnt[kc * 128:(kc + 1) * 128,
                                t2 * 512:(t2 + 1) * 512],
                            [P._war_rhs[kc % 2] if hasattr(P, "_war_rhs")
                             else None])
                if not hasattr(P, "_war_rhs"):
                    P._war_rhs = [None, None]
                hmm = P.op("tensor", lambda e, kc=kc, rb_=rb_: e.matmul(
                    ps_rt[:], acat_sb[:, kc * 48:(kc + 1) * 48], rb_[:],
                    start=(kc == 0), stop=(kc == 31)),
                    [hrl, hacat, hid, hrt_prev if kc == 0 else None])
                P._war_rhs[kc % 2] = hmm
            hev = P.op("vector", lambda e, t2=t2: e.tensor_scalar(
                rt_sb[:, t2 * 512:(t2 + 1) * 512], ps_rt[:], w48_sb[:],
                None, ALU.mult), [hmm, hw48])
            hrt_prev = hev
            rt_done.append(hev)
        hrt = fences_list(rt_done)

        war_xnt_tt = [None]
        war_ot_tt = [None]
        war_wblk = [None]
        war_ps = {"a": [None], "b": [None]}
        war_xt = [None, None]
        war_tmp = [None, None]
        war_x1t = [None, None]
        war_g = [None]
        x1_stores = []
        war_bf = [None, None]
        war_bo = [None, None]
        for ob in range(8):
            pb = ob % 2
            hwf_ = P.dma("gpsimd", wblk[:].rearrange("p (k c) -> p k c", k=32),
                         wft.rearrange("(k p) o -> p k o", p=128)
                         [:, :, ob * 512:(ob + 1) * 512], [war_wblk[0]])
            hbf = P.dma("sync", bf_t[pb][:],
                        bfr[0:1, ob * 512:(ob + 1) * 512], [war_bf[pb]])
            g_done = []
            for tt in range(8):
                hxl = P.dma("sync",
                            xnt_tt[:].rearrange("p (k c) -> p k c", k=32),
                            xnt.rearrange("(k p) t -> p k t", p=128)
                            [:, :, tt * 128:(tt + 1) * 128],
                            [war_xnt_tt[0]])
                pi = tt % 2
                pst = ps_a if pi == 0 else ps_b

                def ffnmm(e, tt=tt, ob=ob, pst=pst, pb=pb):
                    r = e.matmul(pst[:], ones_sb[:], bf_t[pb][:],
                                 start=True, stop=False)
                    for kc in range(32):
                        r = e.matmul(pst[:],
                                     xnt_tt[:, kc * 128:(kc + 1) * 128],
                                     wblk[:, kc * 512:(kc + 1) * 512],
                                     start=False, stop=False)
                    r = e.matmul(pst[:], rt_sb[:, tt * 128:(tt + 1) * 128],
                                 bcat_sb[:, ob * 512:(ob + 1) * 512],
                                 start=False, stop=True)
                    return r
                hmm = P.op("tensor", ffnmm,
                           [hxl, hwf_, war_ps["a" if pi == 0 else "b"][0],
                            hones, hbf, hbcat] + hrt)
                war_xnt_tt[0] = hmm
                hsil = P.op("scalar", lambda e, tt=tt, pst=pst: e.activation(
                    g_blk[:, tt * 512:(tt + 1) * 512], pst[:], AF.Silu),
                    [hmm, war_g[0]])
                war_ps["a" if pi == 0 else "b"][0] = hsil
                g_done.append(hsil)
            war_bf[pb] = hmm
            hwo_ = P.dma("gpsimd", wblk[:].rearrange("p (k c) -> p k c", k=32),
                         wot.rearrange("(k p) o -> p k o", p=128)
                         [:, :, ob * 512:(ob + 1) * 512],
                         [war_xnt_tt[0]])
            hbo = P.dma("sync", bo_t[pb][:],
                        bor[0:1, ob * 512:(ob + 1) * 512], [war_bo[pb]])
            g_all = fences_list(g_done)
            for tt in range(8):
                hol = P.dma("sync",
                            ot_tt[:].rearrange("p (k c) -> p k c", k=32),
                            ot.rearrange("(k p) t -> p k t", p=128)
                            [:, :, tt * 128:(tt + 1) * 128],
                            [war_ot_tt[0]])
                pi = tt % 2
                pst = ps_a if pi == 0 else ps_b

                def womm(e, tt=tt, ob=ob, pst=pst, pb=pb):
                    r = e.matmul(pst[:], ones_sb[:], bo_t[pb][:],
                                 start=True, stop=False)
                    for kc in range(32):
                        r = e.matmul(pst[:],
                                     ot_tt[:, kc * 128:(kc + 1) * 128],
                                     wblk[:, kc * 512:(kc + 1) * 512],
                                     start=False, stop=(kc == 31))
                    return r
                hmm = P.op("tensor", womm,
                           [hol, hwo_, war_ps["a" if pi == 0 else "b"][0],
                            hones, hbo])
                war_ot_tt[0] = hmm
                bi = tt % 2
                hxt = P.dma("sync", (xt0 if bi == 0 else xt1)[:],
                            x[tt * 128:(tt + 1) * 128,
                              ob * 512:(ob + 1) * 512], [war_xt[bi]])
                htm = P.op("vector", lambda e, tt=tt, pst=pst, bi=bi:
                           e.tensor_tensor((tmp0 if bi == 0 else tmp1)[:],
                                           pst[:],
                                           g_blk[:, tt * 512:(tt + 1) * 512],
                                           ALU.mult),
                           [hmm, war_tmp[bi]] + g_all)
                war_ps["a" if pi == 0 else "b"][0] = htm
                hx1 = P.op("vector", lambda e, bi=bi: e.tensor_tensor(
                    (x1t0 if bi == 0 else x1t1)[:],
                    (tmp0 if bi == 0 else tmp1)[:],
                    (xt0 if bi == 0 else xt1)[:], ALU.add),
                    [htm, hxt, war_x1t[bi]])
                war_tmp[bi] = hx1
                war_xt[bi] = hx1
                hst = P.dma("sync", x1[tt * 128:(tt + 1) * 128,
                                       ob * 512:(ob + 1) * 512],
                            (x1t0 if bi == 0 else x1t1)[:], [hx1])
                war_x1t[bi] = hst
                x1_stores.append(hst)
            war_bo[pb] = hmm
            war_wblk[0] = hmm
            war_g[0] = fences_list([war_ps["a"][0], war_ps["b"][0]])[0]
        hx1f = fences_list(x1_stores)

        ln_bufs = {
            "xs": [xs0, xs1], "stats": stats,
            "xn_tm": [xntm0, xntm1],
            "pt_ps": [ptps0, ptps1],
        }
        ln_done = emit_ln(P, ln_bufs, x1, xn1t_sb, ident_sb, ident_h=hid,
                          extra_dep=hx1f)
        lnf = fences_list(ln_done)
        hxn1 = P.dma("sync", xn1t.rearrange("(k p) t -> p k t", p=128),
                     xn1t_sb[:].rearrange("p (k t) -> p k t", k=32), lnf)

        pbufs = {
            "w_fm": [wfm, wfm],
            "ps_fm": [ps_c, ps_d],
            "o_fm": [ofm0, ofm1],
            "rope_a": tmp0, "rope_b": tmp1,
            "w_tm": [wblk, wblk],
            "ps_tm": [ps_a, ps_b],
            "v_tm": [rhs0, rhs1],
            "war_w_fm": [None, None], "war_ps_fm": [None, None],
            "war_o_fm": [None, None],
            "war_w_tm": [fences_list([war_wblk[0]])[0], None],
            "war_ps_tm": [war_ps["a"][0], war_ps["b"][0]],
            "war_v_tm": [P._war_rhs[0], P._war_rhs[1]],
            "rope_war": [fences_list([war_tmp[0], war_tmp[1]])[0]],
        }
        hv1 = emit_proj_tm(P, pbufs, wv1, v1, lnf, xn1t_sb, 1024, tag="v1")
        pbufs["war_w_fm"] = [None, None]
        hrope = [hcos, hsin]
        hq1 = emit_proj_fm(P, pbufs, wq1, q1t, lnf, xn1t_sb, D,
                           rope=(rcos_sb, rsin_sb), rope_dep=hrope, tag="q1")
        hk1 = emit_proj_fm(P, pbufs, wk1, k1t, lnf, xn1t_sb, 1024,
                           rope=(rcos_sb, rsin_sb), rope_dep=hrope, tag="k1")
        P.barrier("sync", fences_list([hxn1] + hv1 + hq1 + hk1 +
                                      x1_stores + [hcos, hsin, hid]))
        block = ctx.enter_context(nc.Block())
        P.emit(ctx, block)
    return nc


def build_launch5():
    nc = bass.Bass("TRN2", target_bir_lowering=False)
    di = lambda n, s, d: nc.dram_tensor(n, s, d, kind="ExternalInput").ap()
    do = lambda n, s, d: nc.dram_tensor(n, s, d, kind="ExternalOutput").ap()
    x1 = di("x1", [TPC, D], F32)
    xn1t = di("xn1t", [D, TPC], BF16)
    o1t = di("o1t", [D, TPC], BF16)
    wo1 = di("wo1", [D, D], F32)
    a1cat = di("a1cat", [D, 128], F32)
    b1cat = di("b1cat", [128, D], F32)
    wrt = di("wrt", [D, 8], F32)
    brr = di("brr", [1, 8], F32)
    e8 = di("e8", [8, 128], F32)
    identity = di("identity", [128, 128], F32)
    out = do("out", [TPC, D], F32)

    with ExitStack() as ctx:
        sb = lambda n, s, d: ctx.enter_context(nc.sbuf_tensor(n, s, d))
        psum = lambda n, s, d: ctx.enter_context(nc.psum_tensor(n, s, d))
        xn1t_sb = sb("xn1t_sb", [128, 32 * TPC], BF16)
        wblk = sb("wblk", [128, 32 * 512], BF16)
        g_blk = sb("g_blk", [128, D], F32)
        o1_tt = sb("o1_tt", [128, 32 * 128], BF16)
        a1_sb = sb("a1_sb", [128, 32 * 128], BF16)
        b1_sb = sb("b1_sb", [128, D], BF16)
        wr_sb = sb("wr_sb", [128, 32 * 8], BF16)
        e8_sb = sb("e8_sb", [8, 128], BF16)
        br_sb = sb("br_sb", [1, 8], F32)
        ones_sb = sb("ones_sb", [1, 128], F32)
        ident_sb = sb("ident_sb", [128, 128], BF16)
        p8_sb = sb("p8_sb", [128, 8], F32)
        negm8 = sb("negm8", [128, 1], F32)
        z8 = sb("z8", [128, 1], F32)
        rz8 = sb("rz8", [128, 1], F32)
        sc8 = sb("sc8", [128, 8], BF16)
        sct_sb = sb("sct_sb", [8, 128], BF16)
        scb_sb = sb("scb_sb", [128, 128], F32)
        lows_sb = sb("lows_sb", [128, TPC], BF16)
        xt0 = sb("xt0", [128, 512], F32)
        xt1 = sb("xt1", [128, 512], F32)
        tmp0 = sb("tmp0", [128, 512], F32)
        tmp1 = sb("tmp1", [128, 512], F32)
        ot0 = sb("ot0", [128, 512], F32)
        ot1 = sb("ot1", [128, 512], F32)
        ps_r1 = psum("ps_r1", [128, 512], F32)
        ps_r2 = psum("ps_r2", [128, 512], F32)
        ps_t8 = psum("ps_t8", [8, 128], BF16)
        ps_a = psum("ps_a", [128, 512], F32)
        ps_b = psum("ps_b", [128, 512], F32)
        dscr = psum("dscr", [128, 1024], BF16)

        P = Plan(nc)
        P.settle_fn = lambda e: e.transpose(dscr[:, 0:128], ident_sb[:],
                                            ident_sb[:])
        hid = P.dma("gpsimd", ident_sb[:], identity)
        hxn = P.dma("sync", xn1t_sb[:].rearrange("p (k t) -> p k t", k=32),
                    xn1t.rearrange("(k p) t -> p k t", p=128))
        ha1 = P.dma("gpsimd", a1_sb[:].rearrange("p (k c) -> p k c", k=32),
                    a1cat.rearrange("(k p) c -> p k c", p=128))
        hb1 = P.dma("gpsimd", b1_sb[:], b1cat)
        hwr = P.dma("gpsimd", wr_sb[:].rearrange("p (k c) -> p k c", k=32),
                    wrt.rearrange("(k p) c -> p k c", p=128))
        he8 = P.dma("gpsimd", e8_sb[:], e8)
        hbr = P.dma("sync", br_sb[:], brr)
        hones = P.op("vector", lambda e: e.memset(ones_sb[:], 1.0))

        war = {}
        lows_done = []
        for tt in range(8):
            def logmm(e, tt=tt):
                r = e.matmul(ps_r1[:, 0:8], ones_sb[:], br_sb[:],
                             start=True, stop=False)
                for kc in range(32):
                    r = e.matmul(ps_r1[:, 0:8],
                                 xn1t_sb[:, kc * TPC + tt * 128:
                                         kc * TPC + (tt + 1) * 128],
                                 wr_sb[:, kc * 8:(kc + 1) * 8],
                                 start=False, stop=(kc == 31))
                return r
            hlg = P.op("tensor", logmm,
                       [hxn, hwr, hbr, hones, hid, war.get("ps_r1")])
            hmx = P.op("vector", lambda e: e.reduce_max(
                negm8[:], ps_r1[:, 0:8], axis=AX.X, negate=True),
                [hlg, war.get("negm8")])
            hex_ = P.op("scalar", lambda e: e.activation(
                p8_sb[:], ps_r1[:, 0:8], AF.Exp, bias=negm8[:],
                accum_out=z8[:]), [hmx, war.get("p8")])
            war["negm8"] = hex_
            hrz = P.op("vector", lambda e: e.reciprocal(rz8[:], z8[:]),
                       [hex_, war.get("rz8")])
            hrz2 = P.op("vector", lambda e: e.tensor_scalar(
                rz8[:], rz8[:], 8.0, None, ALU.mult), [hrz])
            hsc = P.op("vector", lambda e: e.tensor_scalar(
                sc8[:], p8_sb[:], rz8[:], None, ALU.mult),
                [hrz2, war.get("sc8")])
            war["p8"] = hsc
            war["rz8"] = hsc
            htr = P.op("tensor", lambda e: e.transpose(
                ps_t8[:], sc8[:], ident_sb[:]),
                [hsc, hid, war.get("ps_t8")])
            war["sc8"] = htr
            hev = P.op("scalar", lambda e: e.copy(sct_sb[:], ps_t8[:]),
                       [htr, war.get("sct")])
            war["ps_t8"] = hev
            hsb = P.op("tensor", lambda e: e.matmul(
                ps_r1[:, 128:256], e8_sb[:], sct_sb[:],
                start=True, stop=True), [hev, he8, war.get("ps_r1b")])
            war["sct"] = hsb
            war["ps_r1"] = hsb
            hscb = P.op("vector", lambda e: e.tensor_copy(
                scb_sb[:], ps_r1[:, 128:256]), [hsb, war.get("scb")])
            war["ps_r1b"] = hscb

            def lowmm(e, tt=tt):
                r = None
                for kc in range(32):
                    r = e.matmul(ps_r2[:, 0:128],
                                 a1_sb[:, kc * 128:(kc + 1) * 128],
                                 xn1t_sb[:, kc * TPC + tt * 128:
                                         kc * TPC + (tt + 1) * 128],
                                 start=(kc == 0), stop=(kc == 31))
                return r
            hlo = P.op("tensor", lowmm, [hxn, ha1, war.get("ps_r2")])
            hls = P.op("vector", lambda e, tt=tt: e.tensor_tensor(
                lows_sb[:, tt * 128:(tt + 1) * 128], ps_r2[:, 0:128],
                scb_sb[:], ALU.mult), [hlo, hscb, war.get("lows")])
            war["ps_r2"] = hls
            war["scb"] = hls
            lows_done.append(hls)
        hlows = fences_list(lows_done)

        war_wblk = [None]
        war_o1 = [None]
        war_ps = [None, None]
        war_xt = [None, None]
        war_tmp = [None, None]
        war_ot = [None, None]
        war_g = [None]
        out_stores = []
        for ob in range(8):
            hwo_ = P.dma("gpsimd", wblk[:].rearrange("p (k c) -> p k c", k=32),
                         wo1.rearrange("(k p) o -> p k o", p=128)
                         [:, :, ob * 512:(ob + 1) * 512], [war_wblk[0]])
            g_done = []
            for tt in range(8):
                pi = tt % 2
                pst = ps_a if pi == 0 else ps_b
                hg = P.op("tensor", lambda e, tt=tt, ob=ob, pst=pst: e.matmul(
                    pst[:], lows_sb[:, tt * 128:(tt + 1) * 128],
                    b1_sb[:, ob * 512:(ob + 1) * 512],
                    start=True, stop=True),
                    hlows + [hb1, war_ps[pi]])
                hsil = P.op("scalar", lambda e, tt=tt, pst=pst: e.activation(
                    g_blk[:, tt * 512:(tt + 1) * 512], pst[:], AF.Silu,
                    scale=2.0), [hg, war_g[0]])
                war_ps[pi] = hsil
                g_done.append(hsil)
            g_all = fences_list(g_done)
            for tt in range(8):
                hol = P.dma("sync",
                            o1_tt[:].rearrange("p (k c) -> p k c", k=32),
                            o1t.rearrange("(k p) t -> p k t", p=128)
                            [:, :, tt * 128:(tt + 1) * 128], [war_o1[0]])
                pi = tt % 2
                pst = ps_a if pi == 0 else ps_b

                def womm(e, tt=tt, ob=ob, pst=pst):
                    r = None
                    for kc in range(32):
                        r = e.matmul(pst[:],
                                     o1_tt[:, kc * 128:(kc + 1) * 128],
                                     wblk[:, kc * 512:(kc + 1) * 512],
                                     start=(kc == 0), stop=(kc == 31))
                    return r
                hmm = P.op("tensor", womm, [hol, hwo_, war_ps[pi]])
                war_o1[0] = hmm
                bi = tt % 2
                hxt = P.dma("sync", (xt0 if bi == 0 else xt1)[:],
                            x1[tt * 128:(tt + 1) * 128,
                               ob * 512:(ob + 1) * 512], [war_xt[bi]])
                htm = P.op("vector", lambda e, tt=tt, pst=pst, bi=bi:
                           e.tensor_tensor((tmp0 if bi == 0 else tmp1)[:],
                                           pst[:],
                                           g_blk[:, tt * 512:(tt + 1) * 512],
                                           ALU.mult),
                           [hmm, war_tmp[bi]] + g_all)
                war_ps[pi] = htm
                hx1 = P.op("vector", lambda e, bi=bi: e.tensor_tensor(
                    (ot0 if bi == 0 else ot1)[:],
                    (tmp0 if bi == 0 else tmp1)[:],
                    (xt0 if bi == 0 else xt1)[:], ALU.add),
                    [htm, hxt, war_ot[bi]])
                war_tmp[bi] = hx1
                war_xt[bi] = hx1
                hst = P.dma("sync", out[tt * 128:(tt + 1) * 128,
                                        ob * 512:(ob + 1) * 512],
                            (ot0 if bi == 0 else ot1)[:], [hx1])
                war_ot[bi] = hst
                out_stores.append(hst)
            war_wblk[0] = hmm
            war_g[0] = fences_list([war_ps[0], war_ps[1]])[0]
        P.barrier("sync", fences_list(out_stores))
        block = ctx.enter_context(nc.Block())
        P.emit(ctx, block)
    return nc


RG = [[0, 1, 2, 3], [4, 5, 6, 7]]


def build_fused():
    nc = bass.Bass("TRN2", target_bir_lowering=False)
    di = lambda n, s, d: nc.dram_tensor(n, s, d, kind="ExternalInput").ap()
    do = lambda n, s, d: nc.dram_tensor(n, s, d, kind="ExternalOutput").ap()
    dint = lambda n, s, d: nc.dram_tensor(n, s, d)
    x = di("x", [TPC, D], F32)
    wqt = di("wqt", [D, D], F32)
    wkt = di("wkt", [D, D], F32)
    wvt = di("wvt", [D, D], F32)
    bqt = di("bqt", [128, 32], F32)
    bkt = di("bkt", [128, 32], F32)
    bvr = di("bvr", [1, D], F32)
    wot = di("wot", [D, D], F32)
    bor = di("bor", [1, D], F32)
    wft = di("wft", [D, D], F32)
    bfr = di("bfr", [1, D], F32)
    acat = di("acat", [D, 48], F32)
    bcat = di("bcat", [48, D], F32)
    lsi = di("lsi", [1, 4], I32)
    bsel = di("bsel", [1, 1], F32)
    wq1 = di("wq1", [D, D], F32)
    wk1 = di("wk1", [D, 1024], F32)
    wv1 = di("wv1", [D, 1024], F32)
    rcos = di("rcos", [128, TPC], F32)
    rsin = di("rsin", [128, TPC], F32)
    wo1 = di("wo1", [D, D], F32)
    a1cat = di("a1cat", [D, 128], F32)
    b1cat = di("b1cat", [128, D], F32)
    wrt = di("wrt", [D, 8], F32)
    brr = di("brr", [1, 8], F32)
    e8 = di("e8", [8, 128], F32)
    mdg = di("mdg", [128, 128], F32)
    mwn = di("mwn", [128, 128], F32)
    identity = di("identity", [128, 128], F32)
    rank = di("rank", [1, 1], I32)
    out = do("out", [TPC, D], F32)
    x1_d = dint("x1_d", [TPC, D], F32)
    wdram = dint("wdram", [1, 48], F32)
    qt_in = dint("qt_in", [D, TPC], BF16)
    kt_in = dint("kt_in", [D, TPC], BF16)
    v_in = dint("v_in", [TPC, D], BF16)
    qt_ag = dint("qt_ag", [4 * D, TPC], BF16)
    kt_ag = dint("kt_ag", [4 * D, TPC], BF16)
    v_ag = dint("v_ag", [S, D], BF16)
    ot_in_a = dint("ot_in_a", [512, S], BF16)
    ot_in_b = dint("ot_in_b", [512, S], BF16)
    ot_ag_a = dint("ot_ag_a", [2048, S], BF16)
    ot_ag_b = dint("ot_ag_b", [2048, S], BF16)
    q1_in = dint("q1_in", [D, TPC], BF16)
    k1_in = dint("k1_in", [1024, TPC], BF16)
    v1_in = dint("v1_in", [TPC, 1024], BF16)
    q1_ag = dint("q1_ag", [4 * D, TPC], BF16)
    k1_ag = dint("k1_ag", [4096, TPC], BF16)
    v1_ag = dint("v1_ag", [S, 1024], BF16)
    o1_in_a = dint("o1_in_a", [512, S], BF16)
    o1_in_b = dint("o1_in_b", [512, S], BF16)
    o1_ag_a = dint("o1_ag_a", [2048, S], BF16)
    o1_ag_b = dint("o1_ag_b", [2048, S], BF16)

    with ExitStack() as ctx:
        ar = bass.AutoArena(nc)
        T = ar.sbuf_tensor
        xnA = T("xnA", [128, 32 * TPC], BF16)
        ident_sb = T("identW", [128, 128], BF16)
        md_sb = T("md_sb", [128, 128], F32)
        mw_sb = T("mw_sb", [128, 128], F32)
        ones_sb = T("ones_sb", [1, 128], F32)
        ones16 = T("ones16", [1, 16], F32)
        rank_sb = T("rank_sb", [1, 1], I32)
        lsii = T("lsii", [1, 4], I32)
        lsif = T("lsif", [1, 4], F32)
        bsel_sb = T("bsel_sb", [1, 1], F32)
        cnt_sb = T("cnt_sb", [1, 1], F32)
        uc_sb = T("uc_sb", [1, 1], F32)
        gate_sb = T("gate_sb", [1, 1], F32)
        wa_sb = T("wa_sb", [1, 3], F32)
        wfree = T("wfree", [1, 48], F32)
        w48_sb = T("w48_sb", [48, 1], F32)
        rt_sb = T("rt_sb", [48, TPC], BF16)
        o_tmpA = T("o_tmpA", [128, 128], BF16)
        o_tmpB = T("o_tmpB", [128, 128], BF16)
        oaccA = T("oaccA", [128, 128], F32)
        oaccB = T("oaccB", [128, 128], F32)
        stats = {n: T(f"st_{n}", [128, 1], F32)
                 for n in ["ssq", "ssum", "mean", "tmp", "vpe", "s0", "r0",
                           "t2", "s1", "rstd", "nmr"]}
        FLN = ["negm", "negm2", "dtmp", "alpha", "zz", "zp", "rz", "negmsc"]
        flA = {n: T(f"flA_{n}", [128, 1], F32) for n in FLN}
        flB = {n: T(f"flB_{n}", [128, 1], F32) for n in FLN}
        mark = ar.addr
        xs0 = T("xs0", [128, D], F32)
        xs1 = T("xs1", [128, D], F32)
        xntm0 = T("xntm0", [128, D], BF16)
        xntm1 = T("xntm1", [128, D], BF16)
        wfm0 = T("wfm0", [128, 32 * 128], BF16)
        wfm1 = T("wfm1", [128, 32 * 128], BF16)
        ofm0 = T("ofm0", [128, TPC], BF16)
        ofm1 = T("ofm1", [128, TPC], BF16)
        wblkA0 = T("wblkA0", [128, 32 * 256], BF16)
        wblkA1 = T("wblkA1", [128, 32 * 256], BF16)
        vtm0 = T("vtm0", [128, 512], BF16)
        vtm1 = T("vtm1", [128, 512], BF16)
        bq_sb = T("bq_sb", [128, 32], F32)
        bk_sb = T("bk_sb", [128, 32], F32)
        bvt0 = T("bvt0", [1, 256], F32)
        bvt1 = T("bvt1", [1, 256], F32)
        wm1 = ar.addr
        ar.addr = mark
        q_b0 = T("q_b0", [128, S], BF16)
        q_b1 = T("q_b1", [128, S], BF16)
        k_b0 = T("k_b0", [128, S], BF16)
        k_b1 = T("k_b1", [128, S], BF16)
        v_b0 = T("v_b0", [128, 32 * 128], BF16)
        v_b1 = T("v_b1", [128, 32 * 128], BF16)
        pA = T("pA", [128, PW], BF16)
        pB = T("pB", [128, PW], BF16)
        pt_b0 = T("pt_b0", [128, 1024], BF16)
        pt_b1 = T("pt_b1", [128, 1024], BF16)
        oh_b0 = T("oh_b0", [128, 32 * 128], BF16)
        oh_b1 = T("oh_b1", [128, 32 * 128], BF16)
        wm2 = ar.addr
        ar.addr = mark
        wblkB = T("wblkB", [128, 32 * 512], BF16)
        tmp0 = T("tmp0", [128, 512], F32)
        tmp1 = T("tmp1", [128, 512], F32)
        rcos_sb = T("rcos_sb", [128, TPC], F32)
        rsin_sb = T("rsin_sb", [128, TPC], F32)
        wfmB0 = T("wfmB0", [128, 32 * 128], BF16)
        wfmB1 = T("wfmB1", [128, 32 * 128], BF16)
        ofmB0 = T("ofmB0", [128, TPC], BF16)
        ofmB1 = T("ofmB1", [128, TPC], BF16)
        vtmB0 = T("vtmB0", [128, 512], BF16)
        vtmB1 = T("vtmB1", [128, 512], BF16)
        mark3 = ar.addr
        gB = T("gB", [128, D], F32)
        ot_ttB0 = T("ot_ttB0", [128, 32 * 128], BF16)
        ot_ttB1 = T("ot_ttB1", [128, 32 * 128], BF16)
        xtB0 = T("xtB0", [128, 512], F32)
        xtB1 = T("xtB1", [128, 512], F32)
        x1tB0 = T("x1tB0", [128, 512], F32)
        x1tB1 = T("x1tB1", [128, 512], F32)
        acat_sb = T("acat_sb", [128, 32 * 48], BF16)
        bcat_sb = T("bcat_sb", [48, D], BF16)
        bo_t = [T("bo_t0", [1, 512], F32), T("bo_t1", [1, 512], F32)]
        bf_t = [T("bf_t0", [1, 512], F32), T("bf_t1", [1, 512], F32)]
        wm3a = ar.addr
        ar.addr = mark3
        xsB0 = T("xsB0", [128, D], F32)
        xsB1 = T("xsB1", [128, D], F32)
        xntmB0 = T("xntmB0", [128, D], BF16)
        xntmB1 = T("xntmB1", [128, D], BF16)
        wm3b = ar.addr
        ar.addr = max(wm3a, wm3b)
        wm3 = ar.addr
        ar.addr = mark
        wblkC = T("wblkC", [128, 32 * 512], BF16)
        gC = T("gC", [128, D], F32)
        ot_ttC0 = T("ot_ttC0", [128, 32 * 128], BF16)
        ot_ttC1 = T("ot_ttC1", [128, 32 * 128], BF16)
        xtC0 = T("xtC0", [128, 512], F32)
        xtC1 = T("xtC1", [128, 512], F32)
        tmpC0 = T("tmpC0", [128, 512], F32)
        tmpC1 = T("tmpC1", [128, 512], F32)
        x1tC0 = T("x1tC0", [128, 512], F32)
        x1tC1 = T("x1tC1", [128, 512], F32)
        a1_sb = T("a1_sb", [128, 32 * 128], BF16)
        b1_sb = T("b1_sb", [128, D], BF16)
        wr_sb = T("wr_sb", [128, 32 * 8], BF16)
        e8_sb = T("e8_sb", [8, 128], BF16)
        br_sb = T("br_sb", [1, 8], F32)
        p8_sb = T("p8_sb", [128, 8], F32)
        negm8 = T("negm8", [128, 1], F32)
        z8 = T("z8", [128, 1], F32)
        rz8 = T("rz8", [128, 1], F32)
        sc8 = T("sc8", [128, 8], BF16)
        sct_sb = T("sct_sb", [8, 128], BF16)
        scb_sb = T("scb_sb", [128, 128], F32)
        lows_sb = T("lows_sb", [128, TPC], BF16)
        ar.addr = max(ar.addr, wm1, wm2, wm3)
        ar.allocate()

        psum = lambda n, s, d: ctx.enter_context(nc.psum_tensor(n, s, d))
        SA = psum("SA", [128, 1536], F32)
        SB = psum("SB", [128, 1536], F32)
        ptp0 = psum("ptp0", [128, 1024], BF16)
        ptp1 = psum("ptp1", [128, 1024], BF16)
        ps_a = SA[:, 0:512]
        ps_b = SB[:, 0:512]
        ps_c = SA[:, 512:1024]
        ps_d = SB[:, 512:1024]
        rk_reg = ctx.enter_context(nc.gpsimd.register("rk_reg"))

        P = Plan(nc)
        SS = {}

        hid = P.dma("gpsimd", ident_sb[:], identity)
        hmd = P.dma("gpsimd", md_sb[:], mdg)
        hmw = P.dma("gpsimd", mw_sb[:], mwn)
        hrank = P.dma("gpsimd", rank_sb[:], rank)

        def pre(e):
            e.reg_load(rk_reg, rank_sb[0:1, 0:1])
            SS["rk"] = e.snap(rk_reg, min_val=0, max_val=3)
        P.raw("gpsimd", pre, [hrank])
        hbq = P.dma("sync", bq_sb[:], bqt)
        hbk = P.dma("sync", bk_sb[:], bkt)
        hones = P.op("vector", lambda e: e.memset(ones_sb[:], 1.0))
        hones16 = P.op("vector", lambda e: e.memset(ones16[:], 1.0))
        hlsi = P.dma("sync", lsii[:], lsi)
        hbs = P.dma("sync", bsel_sb[:], bsel)
        hc0 = P.op("vector", lambda e: e.tensor_copy(lsif[:], lsii[:]), [hlsi])
        hc1 = P.op("vector", lambda e: e.tensor_scalar(
            lsif[:], lsif[:], bsel_sb[0:1, 0:1], None, ALU.is_le), [hc0, hbs])
        hc2 = P.op("vector", lambda e: e.reduce_sum(
            cnt_sb[:], lsif[:], axis=AX.X), [hc1])
        hc3 = P.op("vector", lambda e: e.tensor_scalar(
            uc_sb[:], cnt_sb[:], -2.0, 0.0, ALU.add, ALU.max), [hc2])
        hc4 = P.op("vector", lambda e: e.tensor_scalar(
            uc_sb[:], uc_sb[:], 2.0, None, ALU.min), [hc3])
        hc5 = P.op("vector", lambda e: e.tensor_scalar(
            gate_sb[:], cnt_sb[:], 2.0, None, ALU.is_ge), [hc2])
        hws = []
        for a in range(3):
            hws.append(P.op("vector", lambda e, a=a: e.tensor_scalar(
                wa_sb[0:1, a:a + 1], uc_sb[:], float(a),
                gate_sb[0:1, 0:1], ALU.is_equal, ALU.mult), [hc4, hc5]))
        hwf = []
        for a in range(3):
            hwf.append(P.op("vector", lambda e, a=a: e.tensor_scalar(
                wfree[0:1, 16 * a:16 * (a + 1)], ones16[:],
                wa_sb[0:1, a:a + 1], None, ALU.mult), [hws[a], hones16]))
        hwd = P.dma("sync", wdram.ap(), wfree[:], fences_list(hwf))
        hw48 = P.dma("sync", w48_sb[:],
                     wdram.ap().rearrange("o (k c) -> k (o c)", k=48), [hwd])

        ln_bufs = {"xs": [xs0, xs1], "stats": stats,
                   "xn_tm": [xntm0, xntm1], "pt_ps": [ptp0, ptp1]}
        ln_done = emit_ln(P, ln_bufs, x, xnA, ident_sb, ident_h=hid)
        lnf = fences_list(ln_done)
        def mkcc(name, in_t, out_t, deps):
            return P.op("gpsimd", lambda e: e.collective_compute(
                "AllGather", ALU.bypass, replica_groups=RG,
                ins=[in_t.ap().opt()], outs=[out_t.ap().opt()]),
                deps, kind=("cc", name))

        pbufs = {"w_fm": [wfm0, wfm1], "ps_fm": [ps_a, ps_b],
                 "o_fm": [ofm0, ofm1],
                 "w_tm": [wblkA0, wblkA1], "ps_tm": [ps_c, ps_d],
                 "v_tm": [vtm0, vtm1], "b_tm": [bvt0, bvt1]}
        hq = emit_proj_fm(P, pbufs, wqt, qt_in.ap(), lnf, xnA, D,
                          bias_sb=bq_sb, bias_h=hbq, scale=1.0, tag="q")
        hcc_q = mkcc("ccq", qt_in, qt_ag, fences_list(hq))
        hk = emit_proj_fm(P, pbufs, wkt, kt_in.ap(), lnf, xnA, D,
                          bias_sb=bk_sb, bias_h=hbk, scale=1.0, tag="k")
        hcc_k = mkcc("cck", kt_in, kt_ag, fences_list(hk))
        hv = emit_proj_tm(P, pbufs, wvt, v_in.ap(), lnf, xnA, D,
                          bias_dram=bvr, ones_sb=ones_sb,
                          ones_h=hones, tag="v", W=256)
        hcc_v = mkcc("ccv", v_in, v_ag, fences_list(hv))

        def nocc(f):
            return [h for h in f
                    if not (isinstance(h.kind, tuple) and h.kind[0] == "cc")]
        ph1 = nocc(P.fence_all())

        ab = {"fl": [flA, flB], "oacc": [oaccA, oaccB],
              "o_tmp": [o_tmpA, o_tmpB], "p_sb": [pA, pB],
              "pt_sb": [pt_b0, pt_b1], "ohead": [oh_b0, oh_b1],
              "SP": [SA, SB], "ptp": [ptp0, ptp1], "war": {}}
        qagr = qt_ag.ap().rearrange("(r q) t -> q r t", r=4)
        kagr = kt_ag.ap().rearrange("(r q) t -> q r t", r=4)
        vagr = v_ag.ap().rearrange("(kc p) c -> p kc c", p=128)
        qk_free = [None, None]
        v_free = [None, None]
        att0_dma = []
        for h in range(8):
            hb = h % 2
            qsb = [q_b0, q_b1][hb]
            ksb = [k_b0, k_b1][hb]
            vsb = [v_b0, v_b1][hb]

            def ldq(e, h=h, qsb=qsb):
                return e.dma_start(
                    out=qsb[:].rearrange("p (r t) -> p r t", r=4),
                    in_=qagr[bass.ds(SS["rk"] * 1024 + 128 * h, 128), :, :])
            hql = P.op("gpsimd", ldq,
                       [hcc_q, qk_free[hb]] + (ph1 if h < 2 else []),
                       kind=("d", f"qb{hb}"))

            def ldk(e, h=h, ksb=ksb):
                return e.dma_start(
                    out=ksb[:].rearrange("p (r t) -> p r t", r=4),
                    in_=kagr[bass.ds(SS["rk"] * 1024 + 128 * h, 128), :, :])
            hkl = P.op("gpsimd", ldk,
                       [hcc_k, qk_free[hb]] + (ph1 if h < 2 else []),
                       kind=("d", f"kb{hb}"))

            def ldv(e, h=h, vsb=vsb):
                return e.dma_start(
                    out=vsb[:].rearrange("p (kc c) -> p kc c", kc=32),
                    in_=vagr[:, :, bass.ds(SS["rk"] * 1024 + 128 * h, 128)])
            hvl = P.op("gpsimd", ldv,
                       [hcc_v, v_free[hb]] + (ph1 if h < 2 else []),
                       kind=("d", f"vb{hb}"))
            o_t = ot_in_a if h < 4 else ot_in_b
            o_ap = o_t.ap()[(h % 4) * 128:(h % 4 + 1) * 128, :]
            qk_reads, v_reads, hdma = emit_attn_pipe(
                P, ab, h, 32, qsb, ksb, vsb, [hkl, hvl], hql, o_ap,
                md_sb, None, hmd, ident_sb, hid, windowed=False,
                o_slot=f"st_oh{hb}")
            qf = fences_list(qk_reads)
            qk_free[hb] = qf[0] if qf else None
            vf = fences_list(v_reads)
            v_free[hb] = vf[0] if vf else None
            att0_dma.append(hdma)
            if h == 4:
                hcc_oa = mkcc("ccoa", ot_in_a, ot_ag_a,
                              fences_list(att0_dma[0:4]))
        hcc_ob = mkcc("ccob", ot_in_b, ot_ag_b, fences_list(att0_dma[4:8]))
        ph2 = nocc(P.fence_all())

        hacat = P.dma("gpsimd", acat_sb[:].rearrange("p (k c) -> p k c", k=32),
                      acat.rearrange("(k p) c -> p k c", p=128), ph2)
        hbcat = P.dma("gpsimd", bcat_sb[:], bcat, ph2)
        rt_done = []
        hrt_prev = None
        for t2 in range(2):
            def rtmm(e, t2=t2):
                r = None
                for kc in range(32):
                    r = e.matmul(ps_c[0:48, :],
                                 acat_sb[:, kc * 48:(kc + 1) * 48],
                                 xnA[:, kc * TPC + t2 * 512:
                                     kc * TPC + (t2 + 1) * 512],
                                 start=(kc == 0), stop=(kc == 31))
                return r
            hmm_rt = P.op("tensor", rtmm, [hacat, hrt_prev] + lnf + ph2)
            hev_rt = P.op("vector", lambda e, t2=t2: e.tensor_scalar(
                rt_sb[:, t2 * 512:(t2 + 1) * 512], ps_c[0:48, :], w48_sb[:],
                None, ALU.mult), [hmm_rt, hw48])
            hrt_prev = hev_rt
            rt_done.append(hev_rt)
        hrt = fences_list(rt_done)

        otagr_a = ot_ag_a.ap().rearrange("(m p) q -> p m q", p=128)
        otagr_b = ot_ag_b.ap().rearrange("(m p) q -> p m q", p=128)
        GA = [(m, (m // 4) * 8 + m % 4) for m in range(16)]
        GB = [(m, (m // 4) * 8 + 4 + m % 4) for m in range(16)]
        war_wblk = [None]
        war_ot = [None, None]
        war_psA = [hrt_prev]
        war_psB = [None]
        war_xt = [None, None]
        war_tmp = [None, None]
        war_x1t = [None, None]
        war_g = [None]
        war_bf = [None, None]
        war_bo = [None, None]
        x1_stores = []
        for ob in range(8):
            pb = ob % 2
            hwf_ = P.dma("gpsimd", wblkB[:].rearrange("p (k c) -> p k c", k=32),
                         wft.rearrange("(k p) o -> p k o", p=128)
                         [:, :, ob * 512:(ob + 1) * 512],
                         [war_wblk[0]] + (ph2 if ob == 0 else []))
            hbf = P.dma("sync", bf_t[pb][:],
                        bfr[0:1, ob * 512:(ob + 1) * 512],
                        [war_bf[pb]] + (ph2 if ob < 2 else []))
            g_done = []
            for tt in range(8):
                pi = tt % 2
                pst = ps_a if pi == 0 else ps_b

                def ffnmm(e, tt=tt, ob=ob, pst=pst, pb=pb):
                    r = e.matmul(pst, ones_sb[:], bf_t[pb][:],
                                 start=True, stop=False)
                    for kc in range(32):
                        r = e.matmul(pst,
                                     xnA[:, kc * TPC + tt * 128:
                                         kc * TPC + (tt + 1) * 128],
                                     wblkB[:, kc * 512:(kc + 1) * 512],
                                     start=False, stop=False)
                    r = e.matmul(pst, rt_sb[:, tt * 128:(tt + 1) * 128],
                                 bcat_sb[:, ob * 512:(ob + 1) * 512],
                                 start=False, stop=True)
                    return r
                hmm = P.op("tensor", ffnmm,
                           [hwf_, hbf, hbcat, hones,
                            war_psA[0] if pi == 0 else war_psB[0]]
                           + hrt + lnf)
                hsil = P.op("scalar", lambda e, tt=tt, pst=pst: e.activation(
                    gB[:, tt * 512:(tt + 1) * 512], pst, AF.Silu),
                    [hmm, war_g[0]])
                if pi == 0:
                    war_psA[0] = hsil
                else:
                    war_psB[0] = hsil
                g_done.append(hsil)
            war_bf[pb] = hmm
            hwo_ = P.dma("gpsimd", wblkB[:].rearrange("p (k c) -> p k c", k=32),
                         wot.rearrange("(k p) o -> p k o", p=128)
                         [:, :, ob * 512:(ob + 1) * 512], [hmm])
            hbo = P.dma("sync", bo_t[pb][:],
                        bor[0:1, ob * 512:(ob + 1) * 512],
                        [war_bo[pb]] + (ph2 if ob < 2 else []))
            g_all = fences_list(g_done)
            for tt in range(8):
                bi = tt % 2
                otb = [ot_ttB0, ot_ttB1][bi]

                def ldot_a(e, tt=tt, otb=otb):
                    return e.dma_start(
                        out=otb[:, 0:2048].rearrange("p (m c) -> p m c",
                                                     m=16),
                        in_=otagr_a[:, :,
                                    bass.ds(SS["rk"] * 1024 + 128 * tt, 128)])
                hol_a = P.op("gpsimd", ldot_a, [war_ot[bi], hcc_oa],
                             kind=("d", f"otB{bi}a"))

                def ldot_b(e, tt=tt, otb=otb):
                    return e.dma_start(
                        out=otb[:, 2048:4096].rearrange("p (m c) -> p m c",
                                                        m=16),
                        in_=otagr_b[:, :,
                                    bass.ds(SS["rk"] * 1024 + 128 * tt, 128)])
                hol_b = P.op("gpsimd", ldot_b, [war_ot[bi], hcc_ob],
                             kind=("d", f"otB{bi}b"))
                pi = tt % 2
                pst = ps_a if pi == 0 else ps_b

                def womm_a(e, tt=tt, ob=ob, pst=pst, pb=pb, otb=otb):
                    r = e.matmul(pst, ones_sb[:], bo_t[pb][:],
                                 start=True, stop=False)
                    for m, g in GA:
                        r = e.matmul(pst,
                                     otb[:, m * 128:(m + 1) * 128],
                                     wblkB[:, g * 512:(g + 1) * 512],
                                     start=False, stop=False)
                    return r
                hmm_a = P.op("tensor", womm_a,
                             [hol_a, hwo_, hbo, hones,
                              war_psA[0] if pi == 0 else war_psB[0]])

                def womm_b(e, tt=tt, ob=ob, pst=pst, otb=otb):
                    r = None
                    for i, (m, g) in enumerate(GB):
                        r = e.matmul(pst,
                                     otb[:, 2048 + m * 128:2048 + (m + 1) * 128],
                                     wblkB[:, g * 512:(g + 1) * 512],
                                     start=False, stop=(i == len(GB) - 1))
                    return r
                hmm = P.op("tensor", womm_b, [hol_b, hmm_a])
                war_ot[bi] = hmm
                bi = tt % 2
                hxt = P.dma("sync", [xtB0, xtB1][bi][:],
                            x[tt * 128:(tt + 1) * 128,
                              ob * 512:(ob + 1) * 512],
                            [war_xt[bi]] + (ph2 if ob == 0 else []))
                htm = P.op("vector", lambda e, tt=tt, pst=pst, bi=bi:
                           e.tensor_tensor([tmp0, tmp1][bi][:], pst,
                                           gB[:, tt * 512:(tt + 1) * 512],
                                           ALU.mult),
                           [hmm, war_tmp[bi]] + g_all)
                if pi == 0:
                    war_psA[0] = htm
                else:
                    war_psB[0] = htm
                hx1 = P.op("vector", lambda e, bi=bi: e.tensor_tensor(
                    [x1tB0, x1tB1][bi][:], [tmp0, tmp1][bi][:],
                    [xtB0, xtB1][bi][:], ALU.add),
                    [htm, hxt, war_x1t[bi]])
                war_tmp[bi] = hx1
                war_xt[bi] = hx1
                hst = P.dma("sync", x1_d.ap()[tt * 128:(tt + 1) * 128,
                                              ob * 512:(ob + 1) * 512],
                            [x1tB0, x1tB1][bi][:], [hx1])
                war_x1t[bi] = hst
                x1_stores.append(hst)
            war_bo[pb] = hmm
            war_wblk[0] = hmm
            war_g[0] = fences_list([war_psA[0], war_psB[0]])[0]
        hx1f = fences_list(x1_stores)

        ln2_extra = hx1f + [war_g[0], war_psA[0], war_psB[0], war_bf[0],
                            war_bf[1], war_bo[0], war_bo[1], war_ot[0],
                            war_ot[1], war_x1t[0], war_x1t[1]]
        ln_bufs2 = {"xs": [xsB0, xsB1], "stats": stats,
                    "xn_tm": [xntmB0, xntmB1], "pt_ps": [ptp0, ptp1]}
        ln_done2 = emit_ln(P, ln_bufs2, x1_d.ap(), xnA, ident_sb,
                           ident_h=hid, extra_dep=fences_list(ln2_extra))
        lnf2 = fences_list(ln_done2)

        hcos = P.dma("sync", rcos_sb[:], rcos, ph2)
        hsin = P.dma("sync", rsin_sb[:], rsin, ph2)
        pbufs2 = {"w_fm": [wfmB0, wfmB1], "ps_fm": [ps_a, ps_b],
                  "o_fm": [ofmB0, ofmB1], "rope_a": tmp0, "rope_b": tmp1,
                  "w_tm": [wblkB, wblkB], "ps_tm": [ps_c, ps_d],
                  "v_tm": [vtmB0, vtmB1],
                  "war_w_fm": [None, None],
                  "war_ps_fm": [war_psA[0], war_psB[0]],
                  "war_o_fm": [None, None],
                  "war_w_tm": [war_wblk[0], None],
                  "war_ps_tm": [hrt_prev, None],
                  "war_v_tm": [None, None],
                  "rope_war": [fences_list([war_tmp[0], war_tmp[1]])[0]]}
        hv1 = emit_proj_tm(P, pbufs2, wv1, v1_in.ap(), lnf2, xnA, 1024,
                           tag="v1")
        hcc_v1 = mkcc("ccv1", v1_in, v1_ag, fences_list(hv1))
        hrope = [hcos, hsin]
        hq1 = emit_proj_fm(P, pbufs2, wq1, q1_in.ap(), lnf2, xnA, D,
                           rope=(rcos_sb, rsin_sb), rope_dep=hrope, tag="q1")
        hcc_q1 = mkcc("ccq1", q1_in, q1_ag, fences_list(hq1))
        hk1 = emit_proj_fm(P, pbufs2, wk1, k1_in.ap(), lnf2, xnA, 1024,
                           rope=(rcos_sb, rsin_sb), rope_dep=hrope, tag="k1")
        hcc_k1 = mkcc("cck1", k1_in, k1_ag, fences_list(hk1))
        ph3 = nocc(P.fence_all())

        ab2 = dict(ab)
        ab2["war"] = {}
        q1agr = q1_ag.ap().rearrange("(r q) t -> q r t", r=4)
        k1agr = k1_ag.ap().rearrange("(r q) t -> q r t", r=4)
        v1agr = v1_ag.ap().rearrange("(kc p) c -> p kc c", p=128)
        hkls, hvls = [], []
        for kv in range(2):
            def ldk1(e, kv=kv):
                return e.dma_start(
                    out=[k_b0, k_b1][kv][:].rearrange("p (r t) -> p r t", r=4),
                    in_=k1agr[bass.ds(SS["rk"] * 256 + 128 * kv, 128), :, :])
            hkls.append(P.op("gpsimd", ldk1, [hcc_k1] + ph3,
                             kind=("d", f"kb{kv}")))

            def ldv1(e, kv=kv):
                return e.dma_start(
                    out=[v_b0, v_b1][kv][:].rearrange("p (kc c) -> p kc c",
                                                      kc=32),
                    in_=v1agr[:, :, bass.ds(SS["rk"] * 256 + 128 * kv, 128)])
            hvls.append(P.op("gpsimd", ldv1, [hcc_v1] + ph3,
                             kind=("d", f"vb{kv}")))
        q_free = [None, None]
        att1_dma = []
        for h in range(8):
            hb = h % 2
            kv = h // 4
            qsb = [q_b0, q_b1][hb]

            def ldq1(e, h=h, qsb=qsb):
                return e.dma_start(
                    out=qsb[:].rearrange("p (r t) -> p r t", r=4),
                    in_=q1agr[bass.ds(SS["rk"] * 1024 + 128 * h, 128), :, :])
            hql = P.op("gpsimd", ldq1,
                       [hcc_q1, q_free[hb]] + (ph3 if h < 2 else []),
                       kind=("d", f"qb{hb}"))
            o_t = o1_in_a if h < 4 else o1_in_b
            o_ap = o_t.ap()[(h % 4) * 128:(h % 4 + 1) * 128, :]
            qk_reads, v_reads, hdma = emit_attn_pipe(
                P, ab2, h, 32, qsb, [k_b0, k_b1][kv], [v_b0, v_b1][kv],
                [hkls[kv], hvls[kv]], hql, o_ap, md_sb, mw_sb, hmd,
                ident_sb, hid, windowed=True, mask_h2=hmw,
                o_slot=f"st_oh{hb}")
            qf = fences_list(qk_reads)
            q_free[hb] = qf[0] if qf else None
            att1_dma.append(hdma)
            if h == 4:
                hcc_o1a = mkcc("cco1a", o1_in_a, o1_ag_a,
                               fences_list(att1_dma[0:4]))
        hcc_o1b = mkcc("cco1b", o1_in_b, o1_ag_b, fences_list(att1_dma[4:8]))
        ph4 = nocc(P.fence_all())

        ha1 = P.dma("gpsimd", a1_sb[:].rearrange("p (k c) -> p k c", k=32),
                    a1cat.rearrange("(k p) c -> p k c", p=128), ph4)
        hb1 = P.dma("gpsimd", b1_sb[:], b1cat, ph4)
        hwr = P.dma("gpsimd", wr_sb[:].rearrange("p (k c) -> p k c", k=32),
                    wrt.rearrange("(k p) c -> p k c", p=128), ph4)
        he8 = P.dma("gpsimd", e8_sb[:], e8, ph4)
        hbr = P.dma("sync", br_sb[:], brr, ph4)
        war5 = {}
        lows_done = []
        ps_t8 = ptp0[0:8, 0:128]
        for tt in range(8):
            def logmm(e, tt=tt):
                r = e.matmul(ps_a[:, 0:8], ones_sb[:], br_sb[:],
                             start=True, stop=False)
                for kc in range(32):
                    r = e.matmul(ps_a[:, 0:8],
                                 xnA[:, kc * TPC + tt * 128:
                                     kc * TPC + (tt + 1) * 128],
                                 wr_sb[:, kc * 8:(kc + 1) * 8],
                                 start=False, stop=(kc == 31))
                return r
            hlg = P.op("tensor", logmm,
                       [hwr, hbr, hones, hid, war5.get("ps_a")] + ph4)
            hmx = P.op("vector", lambda e: e.reduce_max(
                negm8[:], ps_a[:, 0:8], axis=AX.X, negate=True),
                [hlg, war5.get("negm8")])
            hex_ = P.op("scalar", lambda e: e.activation(
                p8_sb[:], ps_a[:, 0:8], AF.Exp, bias=negm8[:],
                accum_out=z8[:]), [hmx, war5.get("p8")])
            war5["negm8"] = hex_
            hrz = P.op("vector", lambda e: e.reciprocal(rz8[:], z8[:]),
                       [hex_, war5.get("rz8")])
            hrz2 = P.op("vector", lambda e: e.tensor_scalar(
                rz8[:], rz8[:], 8.0, None, ALU.mult), [hrz])
            hsc = P.op("vector", lambda e: e.tensor_scalar(
                sc8[:], p8_sb[:], rz8[:], None, ALU.mult),
                [hrz2, war5.get("sc8")])
            war5["p8"] = hsc
            war5["rz8"] = hsc
            htr = P.op("tensor", lambda e: e.transpose(
                ps_t8, sc8[:], ident_sb[:]), [hsc, hid, war5.get("ps_t8")])
            war5["sc8"] = htr
            hev = P.op("scalar", lambda e: e.copy(sct_sb[:], ps_t8),
                       [htr, war5.get("sct")])
            war5["ps_t8"] = hev
            hsb = P.op("tensor", lambda e: e.matmul(
                ps_b[:, 0:128], e8_sb[:], sct_sb[:],
                start=True, stop=True), [hev, he8, war5.get("ps_b")])
            war5["sct"] = hsb
            war5["ps_a"] = hsb
            hscb = P.op("vector", lambda e: e.tensor_copy(
                scb_sb[:], ps_b[:, 0:128]), [hsb, war5.get("scb")])
            war5["ps_b"] = hscb

            def lowmm(e, tt=tt):
                r = None
                for kc in range(32):
                    r = e.matmul(ps_c[:, 0:128],
                                 a1_sb[:, kc * 128:(kc + 1) * 128],
                                 xnA[:, kc * TPC + tt * 128:
                                     kc * TPC + (tt + 1) * 128],
                                 start=(kc == 0), stop=(kc == 31))
                return r
            hlo = P.op("tensor", lowmm, [ha1, war5.get("ps_c")])
            hls = P.op("vector", lambda e, tt=tt: e.tensor_tensor(
                lows_sb[:, tt * 128:(tt + 1) * 128], ps_c[:, 0:128],
                scb_sb[:], ALU.mult), [hlo, hscb])
            war5["ps_c"] = hls
            war5["scb"] = hls
            lows_done.append(hls)
        hlows = fences_list(lows_done)

        o1agr_a = o1_ag_a.ap().rearrange("(m p) q -> p m q", p=128)
        o1agr_b = o1_ag_b.ap().rearrange("(m p) q -> p m q", p=128)
        war_wblk5 = [None]
        war_ot5 = [None, None]
        war_ps5 = [war5["ps_a"], war5["ps_b"]]
        war_xt5 = [None, None]
        war_tmp5 = [None, None]
        war_o5 = [None, None]
        war_g5 = [None]
        out_stores = []
        for ob in range(8):
            hwo_ = P.dma("gpsimd", wblkC[:].rearrange("p (k c) -> p k c", k=32),
                         wo1.rearrange("(k p) o -> p k o", p=128)
                         [:, :, ob * 512:(ob + 1) * 512],
                         [war_wblk5[0]] + (ph4 if ob == 0 else []))
            g_done = []
            for tt in range(8):
                pi = tt % 2
                pst = ps_a if pi == 0 else ps_b
                hg = P.op("tensor", lambda e, tt=tt, ob=ob, pst=pst: e.matmul(
                    pst, lows_sb[:, tt * 128:(tt + 1) * 128],
                    b1_sb[:, ob * 512:(ob + 1) * 512],
                    start=True, stop=True),
                    hlows + [hb1, war_ps5[pi]])
                hsil = P.op("scalar", lambda e, tt=tt, pst=pst: e.activation(
                    gC[:, tt * 512:(tt + 1) * 512], pst, AF.Silu,
                    scale=2.0), [hg, war_g5[0]])
                war_ps5[pi] = hsil
                g_done.append(hsil)
            g_all = fences_list(g_done)
            for tt in range(8):
                bi = tt % 2
                otc = [ot_ttC0, ot_ttC1][bi]

                def ldo1_a(e, tt=tt, otc=otc):
                    return e.dma_start(
                        out=otc[:, 0:2048].rearrange("p (m c) -> p m c",
                                                     m=16),
                        in_=o1agr_a[:, :,
                                    bass.ds(SS["rk"] * 1024 + 128 * tt, 128)])
                hol_a = P.op("gpsimd", ldo1_a, [war_ot5[bi], hcc_o1a],
                             kind=("d", f"otC{bi}a"))

                def ldo1_b(e, tt=tt, otc=otc):
                    return e.dma_start(
                        out=otc[:, 2048:4096].rearrange("p (m c) -> p m c",
                                                        m=16),
                        in_=o1agr_b[:, :,
                                    bass.ds(SS["rk"] * 1024 + 128 * tt, 128)])
                hol_b = P.op("gpsimd", ldo1_b, [war_ot5[bi], hcc_o1b],
                             kind=("d", f"otC{bi}b"))
                pi = tt % 2
                pst = ps_a if pi == 0 else ps_b

                def womm5_a(e, tt=tt, ob=ob, pst=pst, otc=otc):
                    r = None
                    for i, (m, g) in enumerate(GA):
                        r = e.matmul(pst,
                                     otc[:, m * 128:(m + 1) * 128],
                                     wblkC[:, g * 512:(g + 1) * 512],
                                     start=(i == 0), stop=False)
                    return r
                hmm_a = P.op("tensor", womm5_a, [hol_a, hwo_, war_ps5[pi]])

                def womm5_b(e, tt=tt, ob=ob, pst=pst, otc=otc):
                    r = None
                    for i, (m, g) in enumerate(GB):
                        r = e.matmul(pst,
                                     otc[:, 2048 + m * 128:2048 + (m + 1) * 128],
                                     wblkC[:, g * 512:(g + 1) * 512],
                                     start=False, stop=(i == len(GB) - 1))
                    return r
                hmm = P.op("tensor", womm5_b, [hol_b, hmm_a])
                war_ot5[bi] = hmm
                bi = tt % 2
                hxt = P.dma("sync", [xtC0, xtC1][bi][:],
                            x1_d.ap()[tt * 128:(tt + 1) * 128,
                                      ob * 512:(ob + 1) * 512],
                            [war_xt5[bi]] + hx1f + (ph4 if ob == 0 else []))
                htm = P.op("vector", lambda e, tt=tt, pst=pst, bi=bi:
                           e.tensor_tensor([tmpC0, tmpC1][bi][:], pst,
                                           gC[:, tt * 512:(tt + 1) * 512],
                                           ALU.mult),
                           [hmm, war_tmp5[bi]] + g_all)
                war_ps5[pi] = htm
                hx1 = P.op("vector", lambda e, bi=bi: e.tensor_tensor(
                    [x1tC0, x1tC1][bi][:], [tmpC0, tmpC1][bi][:],
                    [xtC0, xtC1][bi][:], ALU.add),
                    [htm, hxt, war_o5[bi]])
                war_tmp5[bi] = hx1
                war_xt5[bi] = hx1
                hst = P.dma("sync", out[tt * 128:(tt + 1) * 128,
                                        ob * 512:(ob + 1) * 512],
                            [x1tC0, x1tC1][bi][:], [hx1])
                war_o5[bi] = hst
                out_stores.append(hst)
            war_wblk5[0] = hmm
            war_g5[0] = fences_list([war_ps5[0], war_ps5[1]])[0]
        P.barrier("sync", fences_list(out_stores))
        block = ctx.enter_context(nc.Block())
        P.emit(ctx, block)
    return nc

def kernel(L0_attn_x, L0_attn_Wq, L0_attn_bq, L0_attn_Wk, L0_attn_bk,
           L0_attn_Wv, L0_attn_bv, L0_attn_Wo, L0_attn_bo, L0_ffn_weight,
           L0_ffn_bias, L0_ffn_lora_A, L0_ffn_lora_B,
           L0_ffn_lora_start_indices, L1_attn_wq, L1_attn_wk, L1_attn_wv,
           L1_attn_wo, L1_ffn_A, L1_ffn_B, L1_ffn_Wr, L1_ffn_br):
    f32 = np.float32
    x = np.asarray(L0_attn_x, dtype=f32)
    Wq = np.asarray(L0_attn_Wq, dtype=f32); bq = np.asarray(L0_attn_bq, dtype=f32)
    Wk = np.asarray(L0_attn_Wk, dtype=f32); bk = np.asarray(L0_attn_bk, dtype=f32)
    Wv = np.asarray(L0_attn_Wv, dtype=f32); bv = np.asarray(L0_attn_bv, dtype=f32)
    Wo = np.asarray(L0_attn_Wo, dtype=f32); bo = np.asarray(L0_attn_bo, dtype=f32)
    Wf = np.asarray(L0_ffn_weight, dtype=f32)
    bfn = np.asarray(L0_ffn_bias, dtype=f32)
    lA = np.asarray(L0_ffn_lora_A, dtype=f32)
    lB = np.asarray(L0_ffn_lora_B, dtype=f32)
    lsi = np.asarray(L0_ffn_lora_start_indices).astype(np.int32).reshape(1, 4)
    wq1 = np.asarray(L1_attn_wq, dtype=f32)
    wk1 = np.asarray(L1_attn_wk, dtype=f32)
    wv1 = np.asarray(L1_attn_wv, dtype=f32)
    wo1 = np.asarray(L1_attn_wo, dtype=f32)
    A1 = np.asarray(L1_ffn_A, dtype=f32)
    B1 = np.asarray(L1_ffn_B, dtype=f32)
    Wr = np.asarray(L1_ffn_Wr, dtype=f32)
    br = np.asarray(L1_ffn_br, dtype=f32)

    mask_diag, mask_win = make_masks()
    ident = np.eye(128, dtype=f32)
    e8 = np.zeros((8, 128), dtype=f32)
    for h in range(8):
        e8[h, h * 16:(h + 1) * 16] = 1.0
    C = lambda a: np.ascontiguousarray(a)

    shared = {
        "wqt": C(Wq.T), "wkt": C(Wk.T), "wvt": C(Wv.T),
        "bqt": C(bq.reshape(32, 128).T), "bkt": C(bk.reshape(32, 128).T),
        "bvr": C(bv.reshape(1, -1)),
        "wot": C(Wo.T), "bor": C(bo.reshape(1, -1)),
        "wft": C(Wf.T), "bfr": C(bfn.reshape(1, -1)),
        "acat": C(lA.transpose(2, 0, 1).reshape(D, 48)),
        "bcat": C(lB.transpose(0, 2, 1).reshape(48, D)),
        "lsi": lsi,
        "wq1": wq1, "wk1": wk1, "wv1": wv1,
        "wo1": wo1,
        "a1cat": C(A1.transpose(2, 0, 1).reshape(D, 128)),
        "b1cat": C(B1.transpose(0, 2, 1).reshape(128, D)),
        "wrt": C(Wr.T), "brr": C(br.reshape(1, 8)),
        "e8": e8, "mdg": mask_diag, "mwn": mask_win, "identity": ident,
    }
    im = []
    for c in range(8):
        b, r = c // 4, c % 4
        pos = np.arange(r * TPC, (r + 1) * TPC)
        rcos, rsin = make_rope_tables(pos, 1.0)
        d = dict(shared)
        d["x"] = C(x[b, r * TPC:(r + 1) * TPC, :])
        d["bsel"] = np.array([[float(b)]], dtype=f32)
        d["rank"] = np.array([[r]], dtype=np.int32)
        d["rcos"] = rcos
        d["rsin"] = rsin
        im.append(d)
    r5 = _run(build_fused(), im)
    out = np.empty((2, S, D), dtype=f32)
    for c in range(8):
        b, r = c // 4, c % 4
        out[b, r * TPC:(r + 1) * TPC, :] = r5[c]["out"]
    return out

